# version 01
# baseline (speedup 1.0000x reference)
import numpy as np
import ml_dtypes
import concourse.bass as bass
import concourse.mybir as mybir
from concourse.bass_utils import run_bass_kernel_spmd
from contextlib import ExitStack

BF16N = ml_dtypes.bfloat16
F32 = mybir.dt.float32
BF = mybir.dt.bfloat16
AF = mybir.ActivationFunctionType
ALU = mybir.AluOpType
AX = mybir.AxisListType.X

T_SEQ = 2048
C = 1024
NCT = 8
LCH = 32
SEG = 512
E_EXP = 8
DFF = 2048
RT32 = float(1.0 / np.sqrt(np.float32(32.0)))

ENGS = ("tensor", "vector", "scalar", "gpsimd", "sync")
INCOF = {"tensor": "T", "vector": "V", "scalar": "S", "gpsimd": "G", "sync": "D1"}


class Plan:
    def __init__(self, nc):
        self.nc = nc
        self.items = {e: [] for e in ENGS}
        self.n = {}
        self.sem = {}
        self.guards = {}
        self.wlog = {}
        self.acc = {}

    def mksem(self, stack, *names):
        self.stack = stack
        for nm in names:
            self.sem[nm] = stack.enter_context(self.nc.semaphore(nm))
            self.n[nm] = 0

    def getsem(self, nm):
        if nm not in self.sem:
            self.sem[nm] = self.stack.enter_context(self.nc.semaphore(nm))
            self.n[nm] = 0
        return self.sem[nm]

    def op(self, eng, fn, wait=(), inc=None, inc_n=None, reads=(), writes=()):
        if inc is not None and inc_n is None:
            inc_n = 16 if inc.startswith("D") else 1
        w = [x for x in wait if x is not None]
        if eng in ("vector", "scalar", "gpsimd") and reads:
            log = self.wlog.setdefault(eng, [])
            idx = len(self.items[eng])
            need = 0
            for rs in reads:
                if rs is None:
                    continue
                for (widx, wcnt, ws) in reversed(log):
                    if idx - widx > 8:
                        break
                    if ws[0] == rs[0] and ws[1] < rs[2] and rs[1] < ws[2]:
                        need = max(need, wcnt)
                        break
            if need:
                w.append((eng[0].upper(), need))
        is_dma = inc is not None and inc_n == 16
        dom = ("dma", inc, self.n.get(inc, 0)) if is_dma else eng
        reads = [r for r in reads if r is not None]
        writes = [ws for ws in writes if ws is not None]
        if reads or writes:
            for rs in reads:
                rec = self.acc.get(rs[0])
                if rec:
                    for (kind, s, e, d2, sn, sv) in rec["recent"]:
                        if kind == "w" and d2 != dom and s < rs[2] and rs[1] < e:
                            w.append((sn, sv))
                    for (d2, sn), (sv, s, e) in rec["oldw"].items():
                        if d2 != dom and s < rs[2] and rs[1] < e:
                            w.append((sn, sv))
            for ws in writes:
                rec = self.acc.get(ws[0])
                if rec:
                    for (kind, s, e, d2, sn, sv) in rec["recent"]:
                        if d2 != dom and s < ws[2] and ws[1] < e:
                            w.append((sn, sv))
                    for (d2, sn), (sv, s, e) in rec["oldw"].items():
                        if d2 != dom and s < ws[2] and ws[1] < e:
                            w.append((sn, sv))
                    for (d2, sn), (sv, s, e) in rec["oldr"].items():
                        if d2 != dom and s < ws[2] and ws[1] < e:
                            w.append((sn, sv))
            best = {}
            for (sn, sv) in w:
                if sv > best.get(sn, -1):
                    best[sn] = sv
            w = list(best.items())
        self.items[eng].append((w, fn, inc, inc_n))
        if inc is not None:
            self.n[inc] += inc_n
            myc = self.n[inc]
            if eng in ("vector", "scalar", "gpsimd") and writes:
                log = self.wlog.setdefault(eng, [])
                idx = len(self.items[eng]) - 1
                for ws in writes:
                    log.append((idx, myc, ws))
                if len(log) > 64:
                    del log[:-48]
            for kind, sigs in (("r", reads), ("w", writes)):
                for sg in sigs:
                    rec = self.acc.setdefault(
                        sg[0], {"recent": [], "oldw": {}, "oldr": {}})
                    rec["recent"].append((kind, sg[1], sg[2], dom, inc, myc))
                    if len(rec["recent"]) > 40:
                        for (k2, s, e, d2, sn, sv) in rec["recent"][:-32]:
                            old = rec["oldw" if k2 == "w" else "oldr"]
                            prev = old.get((d2, sn))
                            if prev is None:
                                old[(d2, sn)] = (sv, s, e)
                            else:
                                old[(d2, sn)] = (max(prev[0], sv),
                                                 min(prev[1], s),
                                                 max(prev[2], e))
                        del rec["recent"][:-32]
            return (inc, myc)
        return None

    def realize(self, block):
        sems = self.sem

        def runner(eng_name):
            def run(eng):
                for waits, fn, inc, inc_n in self.items[eng_name]:
                    for (snm, val) in waits:
                        if val > 0:
                            eng.wait_ge(sems[snm], val)
                    inst = fn(eng)
                    if inc is not None:
                        assert inst is not None
                        inst.then_inc(sems[inc], inc_n)
            return run

        block.tensor(runner("tensor"))
        block.vector(runner("vector"))
        block.scalar(runner("scalar"))
        block.gpsimd(runner("gpsimd"))
        block.sync(runner("sync"))


class Builder:
    def __init__(self, stages=("ffn1", "attn", "ffn2"), taps=(), stop_after=None,
                 racecheck=False):
        self.stages = stages
        self.taps = set(taps)
        self.stop_after = stop_after
        self.racecheck = racecheck

    @staticmethod
    def sig(ap):
        try:
            t = ap.tensor
        except Exception:
            return None
        try:
            off = int(ap.offset)
            ext = 1
            for pair in ap.ap:
                step, count = int(pair[0]), int(pair[1])
                ext += abs(step) * (count - 1)
            return (t.name, off, off + ext)
        except Exception:
            return (t.name, 0, 1 << 60)

    @staticmethod
    def wsig(ap):
        try:
            return (ap.tensor.name, 0, 1 << 60)
        except Exception:
            return None

    def guard(self, key):
        return self.p.guards.get(key)

    def setg(self, key, cnt):
        self.p.guards[key] = cnt

    def mm(self, gkey, out, lhsT, rhs, start, stop, wait=(), tp=None):
        w = list(wait)
        if start and gkey is not None:
            g = self.guard(gkey)
            if g:
                w.append(g)

        def fn(t, o=out, a=lhsT, b=rhs, s1=start, s2=stop, q=tp):
            return t.matmul(o, a, b, start=s1, stop=s2, tile_position=q)
        return self.p.op("tensor", fn, wait=w, inc="T",
                         reads=[self.sig(lhsT), self.sig(rhs)],
                         writes=[self.wsig(out) if start else self.sig(out)])

    def tr(self, gkey, out, in_, ident, wait=()):
        w = list(wait)
        if gkey is not None:
            g = self.guard(gkey)
            if g:
                w.append(g)

        def fn(t, o=out, a=in_, i=ident):
            return t.transpose(o, a, i)
        return self.p.op("tensor", fn, wait=w, inc="T",
                         reads=[self.sig(in_), self.sig(ident)],
                         writes=[self.sig(out)])

    def cp(self, eng, out, in_, wait=(), relkey=None):
        def fn(e, o=out, a=in_):
            return e.tensor_copy(o, a)
        c = self.p.op(eng, fn, wait=list(wait), inc=INCOF[eng],
                      reads=[self.sig(in_)], writes=[self.sig(out)])
        if relkey is not None:
            self.setg(relkey, c)
        return c

    def tt(self, eng, out, a, b, alu, wait=(), relkey=None):
        def fn(e, o=out, x=a, y=b, op=alu):
            return e.tensor_tensor(o, x, y, op)
        c = self.p.op(eng, fn, wait=list(wait), inc=INCOF[eng],
                      reads=[self.sig(a), self.sig(b)], writes=[self.sig(out)])
        if relkey is not None:
            self.setg(relkey, c)
        return c

    def ts(self, eng, out, in_, s1, s2, op0, op1, wait=(), relkey=None):
        def fn(e, o=out, a=in_, x=s1, y=s2, p0=op0, p1=op1):
            return e.tensor_scalar(o, a, x, y, p0, p1)
        rds = [self.sig(in_)]
        for scl in (s1, s2):
            if hasattr(scl, "tensor"):
                rds.append(self.sig(scl))
        c = self.p.op(eng, fn, wait=list(wait), inc=INCOF[eng],
                      reads=rds, writes=[self.sig(out)])
        if relkey is not None:
            self.setg(relkey, c)
        return c

    def stt(self, eng, out, in0, scalar, in1, op0, op1, wait=(), relkey=None):
        def fn(e, o=out, a=in0, s=scalar, b=in1, p0=op0, p1=op1):
            return e.scalar_tensor_tensor(o, a, s, b, p0, p1)
        rds = [self.sig(in0), self.sig(in1)]
        if hasattr(scalar, "tensor"):
            rds.append(self.sig(scalar))
        c = self.p.op(eng, fn, wait=list(wait), inc=INCOF[eng],
                      reads=rds, writes=[self.sig(out)])
        if relkey is not None:
            self.setg(relkey, c)
        return c

    def act(self, out, in_, func, bias=0.0, scale=1.0, accum=None, wait=(), relkey=None):
        def fn(s, o=out, a=in_, f=func, b=bias, sc=scale, ac=accum):
            return s.activation(o, a, f, bias=b, scale=sc, accum_out=ac)
        rds = [self.sig(in_)]
        for scl in (bias, scale):
            if hasattr(scl, "tensor"):
                rds.append(self.sig(scl))
        wrs = [self.sig(out)] + ([self.sig(accum)] if accum is not None else [])
        c = self.p.op("scalar", fn, wait=list(wait), inc="S", reads=rds, writes=wrs)
        if relkey is not None:
            self.setg(relkey, c)
        return c

    def red(self, out, in_, alu, negate=False, wait=(), relkey=None):
        def fn(v, o=out, a=in_, op=alu, ng=negate):
            return v.tensor_reduce(o, a, AX, op, negate=ng)
        c = self.p.op("vector", fn, wait=list(wait), inc="V",
                      reads=[self.sig(in_)], writes=[self.sig(out)])
        if relkey is not None:
            self.setg(relkey, c)
        return c

    def scan(self, out, d0, d1, init, op0, op1, wait=(), relkey=None):
        def fn(v, o=out, a=d0, b=d1, i=init, p0=op0, p1=op1):
            return v.tensor_tensor_scan(o, a, b, i, p0, p1)
        c = self.p.op("vector", fn, wait=list(wait), inc="V",
                      reads=[self.sig(d0), self.sig(d1)], writes=[self.sig(out)])
        if relkey is not None:
            self.setg(relkey, c)
        return c

    def recip(self, out, in_, wait=(), relkey=None):
        def fn(v, o=out, a=in_):
            return v.reciprocal(o, a)
        c = self.p.op("vector", fn, wait=list(wait), inc="V",
                      reads=[self.sig(in_)], writes=[self.sig(out)])
        if relkey is not None:
            self.setg(relkey, c)
        return c

    def dma(self, q, out, in_, wait=(), sem="dc"):
        eng = {"D1": "sync", "D2": "gpsimd"}[q]
        self.p.getsem(sem)

        def fn(e, o=out, a=in_):
            return e.dma_start(out=o, in_=a)
        return self.p.op(eng, fn, wait=list(wait), inc=sem, inc_n=16,
                         reads=[self.sig(in_)], writes=[self.sig(out)])

    def bcrow(self, gkey, out, row, nrows=128, wait=()):
        ones = self.sb["onesrowf"] if row.dtype == F32 else self.sb["onesrow"]
        return self.mm(gkey, out, ones[0:1, 0:nrows], row, True, True, wait=wait)

    def zfill(self, gkey, out, rhs, wait=()):
        return self.mm(gkey, out, self.sb["zerosq"][:], rhs, True, False, wait=wait)

    def barrier(self):
        p = self.p
        return [(nm, c) for nm, c in p.n.items() if c > 0]

    def fence(self):
        bar = self.barrier()
        for eng in ENGS:
            self.p.op(eng, lambda e: None, wait=bar)
        return bar

    def tap(self, name, ap, shape, dt, wait):
        if name not in self.taps:
            return
        d = self.nc.dram_tensor(f"dbg_{name}", list(shape), dt,
                                kind="ExternalOutput").ap()
        self.dma("D1", d, ap, wait=wait, sem="dt")

    def col(self, name, k=0):
        i = self.colidx[name] + k
        return self.sb["cols"][:, i:i + 1]

    def row(self, name):
        off, n = self.rowidx[name]
        return self.sb["rows"][0:1, off:off + n]

    def build(self):
        nc = bass.Bass("TRN2", target_bir_lowering=False,
                       detect_race_conditions=self.racecheck)
        self.nc = nc
        self.p = Plan(nc)
        D = {}

        def din(name, shape, dt=F32):
            D[name] = nc.dram_tensor(name, list(shape), dt, kind="ExternalInput").ap()
        din("x", [T_SEQ, C])
        din("q_w_sel", [C, 512]); din("k_w", [C, 512]); din("v_w", [C, 512])
        din("o_w", [512, C]); din("router_w_t", [128, 64])
        din("w1", [E_EXP, C, DFF]); din("w2", [E_EXP, DFF, C])
        for nm in ("rW", "kW", "vW", "oW"):
            din(nm, [C, C])
        din("g_A", [C, 128]); din("g_B", [128, C])
        din("k_A", [C, 64]); din("k_B", [64, C])
        din("w_A", [C, 64]); din("w_B", [64, C])
        din("a_A", [C, 64]); din("a_B", [64, C])
        din("x_A", [C, 192]); din("x_Bp", [192, C])
        din("cols", [128, self.ncols]); din("rows", [1, self.nrows])
        for nm, shp in (("identf", [128, 128]), ("identb_src", [128, 128]),
                        ("onesrow_src", [1, 128]), ("onesrowf_src", [1, 128]), ("onescol_src", [128, 1]),
                        ("zerosq_src", [128, 128]), ("dpat_src", [128, 128]),
                        ("cm_a_src", [128, SEG]), ("cm_b_src", [128, SEG]),
                        ("mask_tu_src", [128, 256]), ("mask_sl_src", [128, 128]),
                        ("gnpat_src", [128, 128]), ("gnpatT_src", [16, 1024]),
                        ("qpat_src", [128, 64]), ("qpatT_src", [16, 512]), ("esel_src", [8, 8 * 128])):
            din(nm, shp)
        out = nc.dram_tensor("out", [T_SEQ, C], F32, kind="ExternalOutput").ap()
        self.scr = {}
        for nm in ("rWT", "kWT", "vWT", "oWT"):
            self.scr[nm] = nc.dram_tensor(f"scr_{nm}", [NCT, NCT, 128, 128], BF,
                                          kind="ExternalOutput").ap()
        self.D = D
        with ExitStack() as stack:
            self.stack = stack
            self.p.mksem(stack, "T", "V", "S", "G")
            self.alloc_persist(stack)
            self.init_phase()
            if "ffn1" in self.stages:
                self.ffn_phase("f1")
            if "attn" in self.stages:
                self.attn_phase()
            if "ffn2" in self.stages:
                self.ffn_phase("f2")
            self.finish_phase(out)
            block = stack.enter_context(nc.Block())
            self.p.realize(block)
        return nc

    def alloc_persist(self, stack):
        nc = self.nc
        E = stack.enter_context
        sb = self.sb = {}
        for ct in range(NCT):
            sb[f"x{ct}"] = E(nc.sbuf_tensor(f"x{ct}", [128, T_SEQ], F32))
        for nm, shp, dt in (("cols", [128, self.ncols], F32),
                            ("rows", [1, self.nrows], BF),
                            ("identf", [128, 128], F32),
                            ("identb", [128, 128], BF),
                            ("onesrow", [1, 128], BF),
                            ("onesrowf", [1, 128], F32),
                            ("onescol", [128, 1], BF),
                            ("zerosq", [128, 128], BF),
                            ("dpat", [128, 128], BF),
                            ("qpat", [128, 64], BF),
                            ("qpatT", [16, 512], BF),
                            ("esel", [8, 8 * 128], BF),
                            ("routerw", [128, 64], BF)):
            sb[nm] = E(nc.sbuf_tensor("sb_" + nm, list(shp), dt))
        self.pf = [E(nc.psum_tensor(f"pf{i}", [128, 512], F32)) for i in range(6)]
        self.pb = [E(nc.psum_tensor(f"pbf{i}", [128, 1024], BF)) for i in range(2)]

    def init_phase(self):
        p, D, sb = self.p, self.D, self.sb
        self.dma("D1", sb["cols"][:], D["cols"])
        self.dma("D1", sb["identf"][:], D["identf"])
        self.dma("D1", self.sb["onesrowf"][:], D["onesrowf_src"])
        for src, dst in (("identb_src", "identb"), ("onesrow_src", "onesrow"),
                         ("onescol_src", "onescol"), ("zerosq_src", "zerosq"),
                         ("dpat_src", "dpat"),
                         ("qpat_src", "qpat"), ("qpatT_src", "qpatT"),
                         ("esel_src", "esel"),
                         ("rows", "rows"), ("router_w_t", "routerw")):
            self.dma("D2", sb[dst][:], D[src], sem="dg")
        self.c_consts1 = ("dc", p.n["dc"])
        self.c_consts = ("dg", p.n["dg"])
        self.init_stack = ExitStack()
        stg = self.init_stack.enter_context(self.nc.sbuf_tensor("xstage", [128, 2 * C], F32))
        for tt in range(16):
            sl = stg[:, (tt % 2) * C:(tt % 2) * C + C]
            g = self.guard(("xstg", tt % 2))
            c_ld = self.dma("D1", sl, D["x"][tt * 128:(tt + 1) * 128, :],
                            wait=[g] if g else [], sem=f"dx{tt % 2}")
            for ct in range(NCT):
                bk = ("pf", 4 + ct % 2)
                cm = self.tr(bk, self.pf[4 + ct % 2][:, 0:128],
                             sl[:, ct * 128:(ct + 1) * 128], sb["identf"][:],
                             wait=[c_ld, self.c_consts1])
                cv = self.cp("vector", sb[f"x{ct}"][:, tt * 128:(tt + 1) * 128],
                             self.pf[4 + ct % 2][:, 0:128], wait=[cm], relkey=bk)
                if ct == NCT - 1:
                    self.setg(("xstg", tt % 2), cv)
        self.c_x = ("V", p.n["V"])
        wrow = self.init_stack.enter_context(self.nc.sbuf_tensor("wrow", [128, 2 * C], BF))
        wtr = self.init_stack.enter_context(self.nc.sbuf_tensor("wtr", [128, 256], BF))
        import os as _os
        _nwt = int(_os.environ.get("KWT", "4"))
        for nm in (("rW", "kW", "vW", "oW")[:_nwt] if "attn" in self.stages else ()):
            for r in range(NCT):
                slr = wrow[:, (r % 2) * C:(r % 2) * C + C]
                g = self.guard(("wrow", r % 2))
                c_ld = self.dma("D2", slr, D[nm][r * 128:(r + 1) * 128, :],
                                wait=([g] if g else []) + [self.c_consts],
                                sem=f"dm{r % 2}")
                last = None
                for cb in range(NCT):
                    sw = cb % 2
                    cm = self.tr(("pbh", sw), self.pb[sw][:, 0:128],
                                 slr[:, cb * 128:(cb + 1) * 128], sb["identb"][:],
                                 wait=[c_ld])
                    g2 = self.guard(("wtr", sw))
                    cv = self.cp("vector", wtr[:, sw * 128:sw * 128 + 128],
                                 self.pb[sw][:, 0:128],
                                 wait=[cm] + ([g2] if g2 else []),
                                 relkey=("pbh", sw))
                    c_st = self.dma("D1", self.scr[nm + "T"][cb, r, :, :],
                                    wtr[:, sw * 128:sw * 128 + 128], wait=[cv],
                                    sem=f"ds{sw}")
                    self.setg(("wtr", sw), c_st)
                    last = cm
                self.setg(("wrow", r % 2), last)
        self.c_wt = [(s, p.n[s]) for s in ("ds0", "ds1") if s in p.n]
        self.init_stack.close()
        self.entry_wait = self.c_wt + [("T", p.n["T"])]

    def emit_ln(self, xln_tile_fn, t0, t1, upd, sc):
        p, sb = self.p, self.sb
        xbf, xsq, t1f, rows = sc["xbf"], sc["xsq"], sc["t1f"], sc["rows"]
        mu, esq, var, rstd, mr = (rows[0:1, i * 512:(i + 1) * 512] for i in range(5))
        ew_s = [("V", p.n["V"])]
        for b0 in range(t0, t1, 512):
            sl = slice(b0, b0 + 512)
            cm2 = cm3 = None
            for ct in range(NCT):
                s0 = ct % 2
                gA = self.guard(("lnxbf", s0))
                gB = self.guard(("lnxsq", s0))
                cv = self.cp("vector", xbf[:, s0 * 512:(s0 + 1) * 512],
                             sb[f"x{ct}"][:, sl], wait=[gA] if gA else [])
                cs = self.act(xsq[:, s0 * 512:(s0 + 1) * 512],
                              sb[f"x{ct}"][:, sl], AF.Square,
                              wait=ew_s + ([gB] if gB else []))
                ew_s = []
                cm2 = self.mm(("pf", 2) if ct == 0 else None, self.pf[2][0:1, :],
                              sb["onescol"][:], xbf[:, s0 * 512:(s0 + 1) * 512],
                              ct == 0, ct == NCT - 1,
                              wait=[cv] + ([self.c_consts] if ct == 0 else []))
                self.setg(("lnxbf", s0), cm2)
                cm3 = self.mm(("pf", 3) if ct == 0 else None, self.pf[3][0:1, :],
                              sb["onescol"][:], xsq[:, s0 * 512:(s0 + 1) * 512],
                              ct == 0, ct == NCT - 1, wait=[cs])
                self.setg(("lnxsq", s0), cm3)
            self.ts("vector", mu, self.pf[2][0:1, :], 1.0 / 1024, 0.0, ALU.mult,
                    ALU.add, wait=[cm2], relkey=("pf", 2))
            self.ts("vector", esq, self.pf[3][0:1, :], 1.0 / 1024, 0.0, ALU.mult,
                    ALU.add, wait=[cm3], relkey=("pf", 3))
            self.tt("vector", var, mu, mu, ALU.mult)
            c4 = self.tt("vector", var, esq, var, ALU.subtract)
            self.act(rstd, var, AF.Ln, bias=self.col("c_eps1")[0:1, :], wait=[c4])
            c6 = self.act(rstd, rstd, AF.Exp, scale=-0.5)
            c7 = self.tt("vector", mr, mu, rstd, ALU.mult, wait=[c6])
            cb1 = self.bcrow(("pf", 4), self.pf[4][:, :], rstd, wait=[c7])
            cb2 = self.bcrow(("pf", 5), self.pf[5][:, :], mr)
            lastv = None
            for ct in range(NCT):
                self.tt("vector", t1f[:], sb[f"x{ct}"][:, sl], self.pf[4][:, :],
                        ALU.mult, wait=[cb2] if ct == 0 else [])
                self.tt("vector", t1f[:], t1f[:], self.pf[5][:, :], ALU.subtract)
                ew = self.entry_wait
                self.entry_wait = []
                lastv = self.cp("vector", xln_tile_fn(ct)[:, sl], t1f[:], wait=ew)
                if upd:
                    lastv = self.stt("vector", sb[f"x{ct}"][:, sl], t1f[:], 0.5,
                                     sb[f"x{ct}"][:, sl], ALU.mult, ALU.add)
            self.setg(("pf", 4), lastv)
            self.setg(("pf", 5), lastv)
        return ("V", p.n["V"])

    def emit_ln_seg(self, xlnS, t0, sc, W):
        p, sb = self.p, self.sb
        xbf, xsq, t1f, rows = sc["xbf"], sc["xsq"], sc["t1f"], sc["rows"]
        mu, esq, var, rstd, mr = (rows[0:1, i * W:(i + 1) * W] for i in range(5))
        sl = slice(t0, t0 + W)
        ew_s = [("V", p.n["V"])]
        cm2 = cm3 = None
        for ct in range(NCT):
            q0 = ct % 2
            gA = self.guard(("lnxbf", q0))
            gB = self.guard(("lnxsq", q0))
            cv = self.cp("vector", xbf[:, q0 * W:(q0 + 1) * W],
                         sb[f"x{ct}"][:, sl], wait=[gA] if gA else [])
            cs = self.act(xsq[:, q0 * W:(q0 + 1) * W],
                          sb[f"x{ct}"][:, sl], AF.Square,
                          wait=ew_s + ([gB] if gB else []))
            ew_s = []
            cm2 = self.mm(("pf", 2) if ct == 0 else None, self.pf[2][0:1, 0:W],
                          sb["onescol"][:], xbf[:, q0 * W:(q0 + 1) * W],
                          ct == 0, ct == NCT - 1, wait=[cv])
            self.setg(("lnxbf", q0), cm2)
            cm3 = self.mm(("pf", 3) if ct == 0 else None, self.pf[3][0:1, 0:W],
                          sb["onescol"][:], xsq[:, q0 * W:(q0 + 1) * W],
                          ct == 0, ct == NCT - 1, wait=[cs])
            self.setg(("lnxsq", q0), cm3)
        self.ts("vector", mu, self.pf[2][0:1, 0:W], 1.0 / 1024, 0.0, ALU.mult,
                ALU.add, wait=[cm2], relkey=("pf", 2))
        self.ts("vector", esq, self.pf[3][0:1, 0:W], 1.0 / 1024, 0.0, ALU.mult,
                ALU.add, wait=[cm3], relkey=("pf", 3))
        self.tt("vector", var, mu, mu, ALU.mult)
        c4 = self.tt("vector", var, esq, var, ALU.subtract)
        self.act(rstd, var, AF.Ln, bias=self.col("c_eps1")[0:1, :], wait=[c4])
        c6 = self.act(rstd, rstd, AF.Exp, scale=-0.5)
        c7 = self.tt("vector", mr, mu, rstd, ALU.mult, wait=[c6])
        cb1 = self.bcrow(("pf", 4), self.pf[4][:, 0:W], rstd, wait=[c7])
        cb2 = self.bcrow(("pf", 5), self.pf[5][:, 0:W], mr)
        lastv = None
        for ct in range(NCT):
            self.tt("vector", t1f[:], sb[f"x{ct}"][:, sl], self.pf[4][:, 0:W],
                    ALU.mult, wait=[cb2] if ct == 0 else [])
            self.tt("vector", t1f[:], t1f[:], self.pf[5][:, 0:W], ALU.subtract)
            lastv = self.cp("vector", xlnS[:, ct * W:(ct + 1) * W], t1f[:])
        self.setg(("pf", 4), lastv)
        self.setg(("pf", 5), lastv)
        return ("V", p.n["V"])

    def ffn_phase(self, tag):
        nc, p, sb, D = self.nc, self.p, self.sb, self.D
        self.fence()
        with ExitStack() as st:
            E = st.enter_context
            xln = E(nc.sbuf_tensor(f"{tag}_xln", [128, NCT * T_SEQ], BF))
            comb = E(nc.sbuf_tensor(f"{tag}_comb", [128, 16 * 8], BF))
            combT = E(nc.sbuf_tensor(f"{tag}_combT", [8, T_SEQ], BF))
            stB = ExitStack()
            EB = stB.enter_context
            Q = EB(nc.sbuf_tensor(f"{tag}_Q", [128, 4 * T_SEQ], BF))
            K = EB(nc.sbuf_tensor(f"{tag}_K", [128, 4 * T_SEQ], BF))
            VT = EB(nc.sbuf_tensor(f"{tag}_VT", [128, 16 * 528], BF))
            owf = EB(nc.sbuf_tensor(f"{tag}_owf", [128, 4 * 1024], BF))
            shbc = EB(nc.sbuf_tensor(f"{tag}_shbc", [128, 16], F32))
            stA = ExitStack()
            EA = stA.enter_context
            sc = dict(
                xbf=EA(nc.sbuf_tensor(f"{tag}_xbf", [128, 2 * 512], BF)),
                xsq=EA(nc.sbuf_tensor(f"{tag}_xsq", [128, 2 * 512], BF)),
                t1f=EA(nc.sbuf_tensor(f"{tag}_t1f", [128, 512], F32)),
                rows=EA(nc.sbuf_tensor(f"{tag}_lnr", [1, 5 * 512], F32)),
            )
            qkw = EA(nc.sbuf_tensor(f"{tag}_qkw", [128, NCT * 512], BF))
            kvw = EA(nc.sbuf_tensor(f"{tag}_kvw", [128, NCT * 1024], BF))
            xlt = lambda ct: xln[:, ct * T_SEQ:(ct + 1) * T_SEQ]

            dqs = f"dq_{tag}"
            for ct in range(NCT):
                self.dma("D2", qkw[:, ct * 512:(ct + 1) * 512],
                         D["q_w_sel"][ct * 128:(ct + 1) * 128, :], sem=dqs)
                self.dma("D2", kvw[:, ct * 1024:ct * 1024 + 512],
                         D["k_w"][ct * 128:(ct + 1) * 128, :], sem=dqs)
                self.dma("D2", kvw[:, ct * 1024 + 512:ct * 1024 + 1024],
                         D["v_w"][ct * 128:(ct + 1) * 128, :], sem=dqs)
            for vt in range(4):
                self.dma("D2", owf[:, vt * 1024:(vt + 1) * 1024],
                         D["o_w"][vt * 128:(vt + 1) * 128, :], sem=dqs)
            c_wq = (dqs, p.n[dqs])
            qk_lds = [c_wq] * NCT
            kv_lds = [(c_wq, c_wq)] * NCT
            ow_lds = [c_wq] * 4

            for stile in range(16):
                for h in range(16):
                    self.cp("vector",
                            VT[:, stile * 528 + h * 33 + 32: stile * 528 + h * 33 + 33],
                            sb["onescol"][:],
                            wait=[self.c_consts] if (stile == 0 and h == 0) else [])

            for half in range(2):
                c_ln = self.emit_ln(xlt, half * 1024, half * 1024 + 1024, True, sc)
                for qt in range(4):
                    for tb in range(half * 2, half * 2 + 2):
                        bk = ("pf", (qt * 2 + tb) % 2)
                        cm = None
                        for ct in range(NCT):
                            cm = self.mm(bk if ct == 0 else None,
                                         self.pf[(qt * 2 + tb) % 2][:, :],
                                         qkw[:, ct * 512 + qt * 128: ct * 512 + (qt + 1) * 128],
                                         xlt(ct)[:, tb * 512:(tb + 1) * 512],
                                         ct == 0, ct == NCT - 1,
                                         wait=[qk_lds[ct], c_ln] if ct == 0 else [])
                        self.act(Q[:, qt * T_SEQ + tb * 512: qt * T_SEQ + (tb + 1) * 512],
                                 self.pf[(qt * 2 + tb) % 2][:, :], AF.Identity,
                                 bias=self.col("q_b", qt), wait=[cm], relkey=bk)
                for kt in range(4):
                    for tb in range(half * 2, half * 2 + 2):
                        bk = ("pf", (kt * 2 + tb) % 2)
                        cm = None
                        for ct in range(NCT):
                            cm = self.mm(bk if ct == 0 else None,
                                         self.pf[(kt * 2 + tb) % 2][:, :],
                                         kvw[:, ct * 1024 + kt * 128: ct * 1024 + (kt + 1) * 128],
                                         xlt(ct)[:, tb * 512:(tb + 1) * 512],
                                         ct == 0, ct == NCT - 1,
                                         wait=[kv_lds[ct][0], c_ln] if ct == 0 else [])
                        self.act(K[:, kt * T_SEQ + tb * 512: kt * T_SEQ + (tb + 1) * 512],
                                 self.pf[(kt * 2 + tb) % 2][:, :], AF.Identity,
                                 bias=self.col("k_b", kt), wait=[cm], relkey=bk)
                for stile in range(half * 8, half * 8 + 8):
                    bk = ("pf", stile % 2)
                    cm = None
                    for ct in range(NCT):
                        cm = self.mm(bk if ct == 0 else None, self.pf[stile % 2][:, :],
                                     xlt(ct)[:, stile * 128:(stile + 1) * 128],
                                     kvw[:, ct * 1024 + 512:ct * 1024 + 1024],
                                     ct == 0, False,
                                     wait=[kv_lds[ct][1], c_ln] if ct == 0 else [])
                    cm = self.mm(None, self.pf[stile % 2][:, :],
                                 sb["onesrow"][0:1, 0:128], self.row("v_b"),
                                 False, True, wait=[self.c_consts])
                    self.cp("vector",
                            VT[:, stile * 528:(stile + 1) * 528]
                            .rearrange("p (h q) -> p h q", q=33)[:, :, 0:32],
                            self.pf[stile % 2][:, :].rearrange("p (h q) -> p h q", q=32),
                            wait=[cm], relkey=bk)
            if self.stop_after == "ln":
                stA.close()
                stB.close()
                self.fence()
                return
            c_q = ("S", p.n["S"])
            c_v = ("V", p.n["V"])
            self.tap(f"{tag}_xln", xln[:, 0:T_SEQ], [128, T_SEQ], BF, self.barrier())
            self.tap(f"{tag}_Q", Q[:, 0:T_SEQ], [128, T_SEQ], BF, [c_q])
            self.tap(f"{tag}_K", K[:, 0:T_SEQ], [128, T_SEQ], BF, [c_q])
            self.tap(f"{tag}_VT", VT[:, 0:528], [128, 528], BF, [c_v])

            if self.stop_after == "qkv":
                stA.close()
                stB.close()
                self.fence()
                return
            stA.close()
            self.fence()
            stC = ExitStack()
            EC = stC.enter_context
            aofm = EC(nc.sbuf_tensor(f"{tag}_ao", [128, 4 * T_SEQ], BF))
            dinv = EC(nc.sbuf_tensor(f"{tag}_dinv", [128, 512], BF))
            dinvbc = EC(nc.sbuf_tensor(f"{tag}_dinvbc", [128, 512], BF))
            esb = EC(nc.sbuf_tensor(f"{tag}_esb", [128, 2 * 512], BF))
            rt = EC(nc.sbuf_tensor(f"{tag}_rt", [128, 64], F32))
            qsqt = EC(nc.sbuf_tensor(f"{tag}_qsq", [128, 2 * 512], BF))
            mmr = EC(nc.sbuf_tensor(f"{tag}_mmr", [16, 12], F32))
            shrow = EC(nc.sbuf_tensor(f"{tag}_shrow", [1, 16], F32))
            for which, src in ((0, Q), (1, K)):
                for tb in range(4):
                    cm = None
                    for qt in range(4):
                        s0 = qt % 2
                        gA = self.guard(("qsq", s0))
                        c_sq = self.act(qsqt[:, s0 * 512:(s0 + 1) * 512],
                                        src[:, qt * T_SEQ + tb * 512:
                                            qt * T_SEQ + (tb + 1) * 512],
                                        AF.Square,
                                        wait=([c_q] if (which + tb + qt == 0) else [])
                                              + ([gA] if gA else []))
                        cm = self.mm(("pf", 5) if qt == 0 else None,
                                     self.pf[5][0:16, 0:512],
                                     sb["qpat"][:, qt * 16:(qt + 1) * 16],
                                     qsqt[:, s0 * 512:(s0 + 1) * 512],
                                     qt == 0, qt == 3, wait=[c_sq])
                        self.setg(("qsq", s0), cm)
                    self.red(mmr[0:16, which * 4 + tb:which * 4 + tb + 1],
                             self.pf[5][0:16, 0:512], ALU.max, wait=[cm],
                             relkey=("pf", 5))
            c_r1 = self.red(mmr[0:16, 8:9], mmr[0:16, 0:4], ALU.max)
            c_r2 = self.red(mmr[0:16, 9:10], mmr[0:16, 4:8], ALU.max)
            c_m = self.tt("vector", mmr[0:16, 2:3], mmr[0:16, 8:9], mmr[0:16, 9:10],
                          ALU.mult)
            c_l = self.act(mmr[0:16, 3:4], mmr[0:16, 2:3], AF.Ln, wait=[c_m])
            c_e = self.act(mmr[0:16, 3:4], mmr[0:16, 3:4], AF.Exp, scale=0.5)
            c_n = self.ts("vector", mmr[0:16, 3:4], mmr[0:16, 3:4], -RT32, 0.0,
                          ALU.mult, ALU.add, wait=[c_e])
            c_t = self.tr(("pf", 5), self.pf[5][0:1, 0:16], mmr[0:16, 3:4],
                          sb["identf"][0:16, 0:16], wait=[c_n])
            c_sr = self.cp("vector", shrow[0:1, 0:16], self.pf[5][0:1, 0:16],
                           wait=[c_t], relkey=("pf", 5))
            c_bc = self.bcrow(("pf", 5), self.pf[5][:, 0:16], shrow[0:1, 0:16],
                              wait=[c_sr])
            c_sh = self.cp("vector", shbc[:], self.pf[5][:, 0:16], wait=[c_bc],
                           relkey=("pf", 5))
            self.tap(f"{tag}_shbc", shbc[:], [128, 16], F32, [c_sh])
            self.tap(f"{tag}_mmr", mmr[:], [16, 12], F32, [c_sh])
            self.tap(f"{tag}_shrow", shrow[:], [1, 16], F32, [c_sh])
            self.tap(f"{tag}_dinv", dinv[:], [128, 512], BF, [("S", p.n["S"] + 400)])

            if self.stop_after == "shift":
                stC.close()
                stB.close()
                self.fence()
                return
            c_z = self.p.op("gpsimd", lambda g, a=dinv[:]: g.memzero(a), inc="G")
            for tb in range(4):
                for qt in range(4):
                    cz1 = self.zfill(("pf", 2), self.pf[2][:, :], esb[:, 0:512] if False
                                     else Q[:, 0:512], wait=[c_q, c_sh])
                    cz2 = self.zfill(("pf", 3), self.pf[3][:, :], Q[:, 0:512])
                    for hpp in range(2):
                        h_a, h_b = qt * 4 + 2 * hpp, qt * 4 + 2 * hpp + 1
                        ro_a, ro_b = (2 * hpp) * 32, (2 * hpp + 1) * 32
                        for stile in range(16):
                            cm_a = self.mm(("pf", 0), self.pf[0][:, :],
                                           K[ro_a:ro_a + 32,
                                             qt * T_SEQ + stile * 128:
                                             qt * T_SEQ + (stile + 1) * 128],
                                           Q[ro_a:ro_a + 32,
                                             qt * T_SEQ + tb * 512:
                                             qt * T_SEQ + (tb + 1) * 512],
                                           True, True, tp=(ro_a, 0))
                            cm_b = self.mm(("pf", 1), self.pf[1][:, :],
                                           K[ro_b:ro_b + 32,
                                             qt * T_SEQ + stile * 128:
                                             qt * T_SEQ + (stile + 1) * 128],
                                           Q[ro_b:ro_b + 32,
                                             qt * T_SEQ + tb * 512:
                                             qt * T_SEQ + (tb + 1) * 512],
                                           True, True, tp=(ro_b, 0))
                            g0, g1 = self.guard(("esb", 0)), self.guard(("esb", 1))
                            ce_a = self.act(esb[:, 0:512], self.pf[0][:, :], AF.Exp,
                                            scale=RT32, bias=shbc[:, h_a:h_a + 1],
                                            wait=[cm_a] + ([g0] if g0 else []),
                                            relkey=("pf", 0))
                            ce_b = self.act(esb[:, 512:1024], self.pf[1][:, :], AF.Exp,
                                            scale=RT32, bias=shbc[:, h_b:h_b + 1],
                                            wait=[cm_b] + ([g1] if g1 else []),
                                            relkey=("pf", 1))
                            first = (stile == 0) and (hpp == 0)
                            c_ao_a = self.mm(None, self.pf[2][ro_a:ro_a + 32, :],
                                             VT[:, stile * 528 + h_a * 33:
                                                stile * 528 + h_a * 33 + 32],
                                             esb[:, 0:512], False, False, tp=(0, ro_a),
                                             wait=[ce_a, c_v, cz1] if first else [ce_a])
                            c_dn_a = self.mm(None, self.pf[3][ro_a:ro_a + 1, :],
                                             VT[:, stile * 528 + h_a * 33 + 32:
                                                stile * 528 + h_a * 33 + 33],
                                             esb[:, 0:512], False, False, tp=(0, ro_a),
                                             wait=[cz2] if first else [])
                            self.setg(("esb", 0), c_dn_a)
                            c_ao_b = self.mm(None, self.pf[2][ro_b:ro_b + 32, :],
                                             VT[:, stile * 528 + h_b * 33:
                                                stile * 528 + h_b * 33 + 32],
                                             esb[:, 512:1024], False, False, tp=(0, ro_b),
                                             wait=[ce_b])
                            c_dn_b = self.mm(None, self.pf[3][ro_b:ro_b + 1, :],
                                             VT[:, stile * 528 + h_b * 33 + 32:
                                                stile * 528 + h_b * 33 + 33],
                                             esb[:, 512:1024], False, False, tp=(0, ro_b))
                            self.setg(("esb", 1), c_dn_b)
                    c_cl2 = self.mm(None, self.pf[2][:, :], sb["zerosq"][:],
                                    Q[:, 0:512], False, True)
                    c_cl3 = self.mm(None, self.pf[3][:, :], sb["zerosq"][:],
                                    Q[:, 0:512], False, True)
                    c_last_mm = c_cl3
                    cacts = []
                    for r in range(4):
                        c1 = self.act(dinv[r * 32:r * 32 + 1, :],
                                      self.pf[3][r * 32:r * 32 + 1, :], AF.Ln,
                                      wait=[c_last_mm, ("G", c_z[1])] if r == 0 else [])
                        c2 = self.act(dinv[r * 32:r * 32 + 1, :],
                                      dinv[r * 32:r * 32 + 1, :], AF.Exp, scale=-1.0)
                        cacts.append(c2)
                    self.setg(("pf", 3), cacts[-1])
                    cbc = self.mm(("pf", 4), self.pf[4][:, :], sb["dpat"][:],
                                  dinv[:, :], True, True, wait=[cacts[-1]])
                    cbv = self.cp("vector", dinvbc[:], self.pf[4][:, :],
                                  wait=[cbc], relkey=("pf", 4))
                    cev = self.tt("vector",
                                  aofm[:, qt * T_SEQ + tb * 512:
                                       qt * T_SEQ + (tb + 1) * 512],
                                  self.pf[2][:, :], dinvbc[:], ALU.mult,
                                  wait=[c_last_mm])
                    self.setg(("pf", 2), cev)
            c_ao = ("V", p.n["V"])
            self.tap(f"{tag}_ao", aofm[:, 0:T_SEQ], [128, T_SEQ], BF, [c_ao])

            if self.stop_after == "attnloop":
                stC.close()
                stB.close()
                self.fence()
                return
            mla = xln
            for cpt in range(NCT):
                for tb in range(4):
                    bk = ("pf", tb % 2)
                    cm = None
                    for vt in range(4):
                        cm = self.mm(bk if vt == 0 else None, self.pf[tb % 2][:, :],
                                     owf[:, vt * 1024 + cpt * 128:
                                         vt * 1024 + (cpt + 1) * 128],
                                     aofm[:, vt * T_SEQ + tb * 512:
                                          vt * T_SEQ + (tb + 1) * 512],
                                     vt == 0, vt == 3,
                                     wait=[ow_lds[vt], c_ao] if vt == 0 else [])
                    self.act(mla[:, cpt * T_SEQ + tb * 512: cpt * T_SEQ + (tb + 1) * 512],
                             self.pf[tb % 2][:, :], AF.Identity,
                             bias=self.col("o_b", cpt), wait=[cm], relkey=bk)
            c_mla = ("S", p.n["S"])
            self.tap(f"{tag}_mla", mla[:, 0:T_SEQ], [128, T_SEQ], BF, [c_mla])

            if self.stop_after == "mla":
                stC.close()
                stB.close()
                self.fence()
                return
            for tt in range(16):
                cm = None
                for ct in range(NCT):
                    cm = self.mm(("pf", 2) if ct == 0 else None, self.pf[2][0:128, 0:8],
                                 mla[:, ct * T_SEQ + tt * 128: ct * T_SEQ + (tt + 1) * 128],
                                 sb["routerw"][:, ct * 8:(ct + 1) * 8],
                                 ct == 0, False, wait=[c_mla] if ct == 0 else [])
                cm = self.mm(None, self.pf[2][0:128, 0:8], sb["onesrow"][0:1, 0:128],
                             self.row("router_b"), False, True)
                self.cp("vector", rt[:, 0:8], self.pf[2][0:128, 0:8], wait=[cm],
                        relkey=("pf", 2))
                self.red(rt[:, 8:9], rt[:, 0:8], ALU.max)
                c3 = self.ts("vector", rt[:, 9:10], rt[:, 8:9], -1.0, 0.0, ALU.mult,
                             ALU.add)
                c3b = self.act(rt[:, 16:24], rt[:, 0:8], AF.Exp, bias=rt[:, 9:10],
                               accum=rt[:, 10:11], wait=[c3])
                c4 = self.ts("vector", rt[:, 24:32], rt[:, 0:8], rt[:, 8:9], 0.0,
                             ALU.is_equal, ALU.add, wait=[c3b])
                self.stt("vector", rt[:, 32:40], rt[:, 24:32], -1e30, rt[:, 0:8],
                         ALU.mult, ALU.add)
                self.red(rt[:, 40:41], rt[:, 32:40], ALU.max)
                c7 = self.ts("vector", rt[:, 48:56], rt[:, 32:40], rt[:, 40:41], 0.0,
                             ALU.is_equal, ALU.add)
                c9 = self.act(rt[:, 42:43], rt[:, 40:41], AF.Exp, bias=rt[:, 9:10],
                              wait=[c7])
                c10 = self.ts("vector", rt[:, 43:44], rt[:, 10:11], 1e-8, 1.0,
                              ALU.mult, ALU.add, wait=[c9])
                self.tt("vector", rt[:, 43:44], rt[:, 43:44], rt[:, 42:43], ALU.add)
                self.recip(rt[:, 44:45], rt[:, 43:44])
                self.tt("vector", rt[:, 45:46], rt[:, 42:43], rt[:, 44:45], ALU.mult)
                self.ts("vector", rt[:, 24:32], rt[:, 24:32], rt[:, 44:45], 0.0,
                        ALU.mult, ALU.add)
                self.ts("vector", rt[:, 48:56], rt[:, 48:56], rt[:, 45:46], 0.0,
                        ALU.mult, ALU.add)
                self.tt("vector", comb[:, tt * 8:(tt + 1) * 8], rt[:, 24:32],
                        rt[:, 48:56], ALU.add)
            c_comb = ("V", p.n["V"])
            self.tap(f"{tag}_comb", comb[:], [128, 128], BF, [c_comb])
            for tt in range(16):
                cm = self.tr(("pbh", 1), self.pb[1][0:8, 0:128],
                             comb[:, tt * 8:(tt + 1) * 8], sb["identb"][:],
                             wait=[c_comb] if tt == 0 else [])
                self.cp("vector", combT[0:8, tt * 128:(tt + 1) * 128],
                        self.pb[1][0:8, 0:128], wait=[cm], relkey=("pbh", 1))
            c_combT = ("V", p.n["V"])

            if self.stop_after == "router":
                stC.close()
                stB.close()
                self.fence()
                return
            stC.close()
            stB.close()
            self.fence()
            h1b = E(nc.sbuf_tensor(f"{tag}_h1", [128, 16 * 1024], BF))
            wstr = E(nc.sbuf_tensor(f"{tag}_wstr", [128, 3 * 4096], BF))
            cbbc = E(nc.sbuf_tensor(f"{tag}_cbbc", [128, 2 * 512], BF))
            etmp = E(nc.sbuf_tensor(f"{tag}_etmp", [128, 512], BF))
            gslot = [0]

            def gr_dma(dram_ap, gwait=()):
                s0 = gslot[0] % 3
                gslot[0] += 1
                dst = wstr[:, s0 * 4096:(s0 + 1) * 4096]
                g = self.guard(("wstr", s0))
                c = self.dma("D2", dst, dram_ap, wait=([g] if g else []) + list(gwait),
                             sem=f"dw{s0}_{tag}")
                return s0, c

            for tblk in range(2):
                for e in range(E_EXP):
                    for th in range(2):
                        tb = tblk * 2 + th
                        cm = self.mm(("pf", 4 + th), self.pf[4 + th][:, :],
                                     sb["esel"][0:8, e * 128:(e + 1) * 128],
                                     combT[0:8, tb * 512:(tb + 1) * 512], True, True,
                                     wait=[c_combT] if (tblk == 0 and e == 0) else [])
                        self.ts("vector", cbbc[:, th * 512:(th + 1) * 512],
                                self.pf[4 + th][:, :], 0.15, 0.0, ALU.mult, ALU.add,
                                wait=[cm], relkey=("pf", 4 + th))
                    c_cb = ("V", p.n["V"])
                    for fq in range(4):
                        s0, c_ld = gr_dma(
                            D["w1"][e].rearrange("(a p) f -> p a f", p=128)
                            [:, :, fq * 512:(fq + 1) * 512],
                            gwait=[c_combT] if (tblk == 0 and e == 0 and fq == 0) else ())
                        lastmm = None
                        for fi in range(4):
                            ft = fq * 4 + fi
                            for th in range(2):
                                tb = tblk * 2 + th
                                bki = (ft * 2 + th) % 2
                                cm = None
                                for ct in range(NCT):
                                    cm = self.mm(
                                        ("pf", bki) if ct == 0 else None,
                                        self.pf[bki][:, :],
                                        wstr[:, s0 * 4096 + ct * 512 + fi * 128:
                                             s0 * 4096 + ct * 512 + (fi + 1) * 128],
                                        mla[:, ct * T_SEQ + tb * 512:
                                            ct * T_SEQ + (tb + 1) * 512],
                                        ct == 0, ct == NCT - 1,
                                        wait=[c_ld, c_mla] if ct == 0 else [])
                                lastmm = cm
                                self.act(
                                    h1b[:, ft * 1024 + th * 512:
                                        ft * 1024 + (th + 1) * 512],
                                    self.pf[bki][:, :], AF.Gelu,
                                    bias=self.col(f"b1_{e}", ft), wait=[cm],
                                    relkey=("pf", bki))
                        self.setg(("wstr", s0), lastmm)
                    c_h1 = ("S", p.n["S"])
                    for fh in range(2):
                        for ch in range(2):
                            s0, c_ld = gr_dma(
                                D["w2"][e].rearrange("(a p) c -> p a c", p=128)
                                [:, fh * 8:(fh + 1) * 8, ch * 512:(ch + 1) * 512])
                            lastmm = None
                            for th in range(2):
                                tb = tblk * 2 + th
                                for cq in range(4):
                                    cp_ = ch * 4 + cq
                                    bki = 2 + (th * 4 + cq) % 2
                                    cm = None
                                    for fi in range(8):
                                        ft = fh * 8 + fi
                                        cm = self.mm(
                                            ("pf", bki) if fi == 0 else None,
                                            self.pf[bki][:, :],
                                            wstr[:, s0 * 4096 + fi * 512 + cq * 128:
                                                 s0 * 4096 + fi * 512 + (cq + 1) * 128],
                                            h1b[:, ft * 1024 + th * 512:
                                                ft * 1024 + (th + 1) * 512],
                                            fi == 0, fi == 7,
                                            wait=[c_ld, c_h1] if fi == 0 else [])
                                    lastmm = cm
                                    if fh == 0:
                                        cs = self.ts("vector", etmp[:],
                                                     self.pf[bki][:, :], 1.0, 0.0,
                                                     ALU.mult, ALU.add,
                                                     wait=[cm], relkey=("pf", bki))
                                    else:
                                        cs = self.ts("vector", etmp[:],
                                                     self.pf[bki][:, :],
                                                     self.col(f"b2_{e}", cp_), 0.0,
                                                     ALU.add, ALU.add,
                                                     wait=[cm], relkey=("pf", bki))
                                    self.tt("vector", etmp[:], etmp[:],
                                            cbbc[:, th * 512:(th + 1) * 512],
                                            ALU.mult,
                                            wait=[c_cb])
                                    self.tt("vector",
                                            sb[f"x{cp_}"][:, tb * 512:(tb + 1) * 512],
                                            sb[f"x{cp_}"][:, tb * 512:(tb + 1) * 512],
                                            etmp[:], ALU.add)
                            self.setg(("wstr", s0), lastmm)
            self.tap(f"{tag}_x", sb["x0"][:], [128, T_SEQ], F32, [("V", p.n["V"])])
        self.fence()

    def attn_phase(self):
        nc, p, sb, D = self.nc, self.p, self.sb, self.D
        SEGA = 256
        NSEGA = T_SEQ // SEGA
        NSTL = SEGA // 128
        NCH = SEGA // 32
        self.fence()
        with ExitStack() as st:
            E = st.enter_context
            for nm, shp in (("cm_a", [128, SEGA]), ("gnpat", [128, 128])):
                sb[nm] = E(nc.sbuf_tensor("sb_" + nm, shp, BF))
            for nm, shp in (("mask_tu", [128, 256]), ("mask_sl", [128, 128])):
                sb[nm] = E(nc.sbuf_tensor("sb_" + nm, shp, F32))
            sb["gnpatTf"] = E(nc.sbuf_tensor("sb_gnpatTf", [16, 1024], F32))
            self.dma("D2", sb["cm_a"][:], D["cm_a_src"][:, 0:SEGA], sem="dlr")
            self.dma("D2", sb["gnpat"][:], D["gnpat_src"], sem="dlr")
            for nm in ("mask_tu", "mask_sl", "gnpatTf"):
                srcn = "gnpatT_src" if nm == "gnpatTf" else nm + "_src"
                self.dma("D1", sb[nm][:], D[srcn], sem="dlr2")
            ncols = E(nc.sbuf_tensor("att_ncols", [128, 8], F32))
            _i0 = self.colidx["w_miu"]
            c_neg = self.ts("vector", ncols[:, 0:8], self.sb["cols"][:, _i0:_i0 + 8],
                            -1.0, 0.0,
                            ALU.mult, ALU.add)
            SW = NCT * SEGA
            rS = E(nc.sbuf_tensor("a_rS", [128, SW], BF))
            vS = E(nc.sbuf_tensor("a_vS", [128, SW], BF))
            kapS = E(nc.sbuf_tensor("a_kapS", [128, SW], BF))
            kkS = E(nc.sbuf_tensor("a_kkS", [128, SW], BF))
            bS = E(nc.sbuf_tensor("a_bS", [128, SW], BF))
            gS = E(nc.sbuf_tensor("a_gS", [128, SW], BF))
            wSf = E(nc.sbuf_tensor("a_wSf", [128, SW], F32))
            xdP = E(nc.sbuf_tensor("a_xdP", [128, NCT * SEGA], BF))
            mixA = E(nc.sbuf_tensor("a_mixA", [128, SEGA], BF))
            mixB = E(nc.sbuf_tensor("a_mixB", [64, SEGA], BF))
            bsum = E(nc.sbuf_tensor("a_bsum", [16, SEGA], F32))
            gnr = E(nc.sbuf_tensor("a_gnr", [16, 3 * SEGA], F32))
            stS = E(nc.sbuf_tensor("a_stS", [128, NCT * 64], BF))
            PLs = E(nc.sbuf_tensor("a_PLs", [128, NCT * NCH], F32))
            prevcol = E(nc.sbuf_tensor("a_prev", [128, NCT], BF))
            wstrip = E(nc.sbuf_tensor("a_wstrip", [128, 2 * 1024], BF))
            t64 = E(nc.sbuf_tensor("a_t64", [128, SEGA], BF))
            ag = E(nc.sbuf_tensor("a_ag", [128, SEGA], BF))
            miu = E(nc.sbuf_tensor("a_miu", [128, SEGA], BF))
            t1g = E(nc.sbuf_tensor("a_t1g", [128, SEGA], BF))
            l1t = E(nc.sbuf_tensor("a_l1t", [128, SEGA], F32))
            sb1f = E(nc.sbuf_tensor("a_sb1f", [128, SEGA], F32))
            yS = kkS

            czs = self.p.op("gpsimd", lambda g, a=stS[:]: g.memzero(a), inc="G")
            czp = self.p.op("gpsimd", lambda g, a=prevcol[:]: g.memzero(a), inc="G")
            self._st_init = False

            def strip_dma(mat, rb, slot, wait=()):
                ap = self.scr[mat][:, rb].transpose([1, 0, 2])
                g = self.guard(("astrip", slot))
                dst = wstrip[:, slot * 1024:(slot + 1) * 1024].rearrange(
                    "p (a f) -> p a f", f=128)
                return self.dma("D1", dst, ap,
                                wait=([g] if g else []) + list(wait),
                                sem=f"da{slot}")

            for seg in range(NSEGA):
                s0, s1g = seg * SEGA, (seg + 1) * SEGA
                segsl = lambda ct: slice(ct * SEGA, (ct + 1) * SEGA)
                stD = ExitStack()
                ED = stD.enter_context
                xlnS = ED(nc.sbuf_tensor(f"a_xlnS{seg}", [128, SW], BF))
                dxS = ED(nc.sbuf_tensor(f"a_dxS{seg}", [128, SW], BF))
                sc = dict(
                    xbf=ED(nc.sbuf_tensor(f"a_xbf{seg}", [128, 2 * SEGA], BF)),
                    xsq=ED(nc.sbuf_tensor(f"a_xsq{seg}", [128, 2 * SEGA], BF)),
                    t1f=ED(nc.sbuf_tensor(f"a_t1f{seg}", [128, SEGA], F32)),
                    rows=ED(nc.sbuf_tensor(f"a_lnr{seg}", [1, 5 * SEGA], F32)),
                )
                lr = {}
                for nm, shp, dsrc in (("xaw", [128, 8 * 192], "x_A"),
                                      ("kaw", [128, 8 * 64], "k_A"),
                                      ("waw", [128, 8 * 64], "w_A"),
                                      ("aaw", [128, 8 * 64], "a_A"),
                                      ("gaw", [128, 8 * 128], "g_A")):
                    lr[nm] = ED(nc.sbuf_tensor(f"{nm}{seg}", shp, BF))
                    w = shp[1] // 8
                    for ct in range(NCT):
                        self.dma("D2", lr[nm][:, ct * w:(ct + 1) * w],
                                 D[dsrc][ct * 128:(ct + 1) * 128, :], sem="dlr")
                for nm, shp, dsrc in (("xbA", [128, 1024], None),
                                      ("xbB", [64, 1024], None),
                                      ("kbw", [64, 1024], "k_B"),
                                      ("wbw", [64, 1024], "w_B"),
                                      ("abw", [64, 1024], "a_B"),
                                      ("gbw", [128, 1024], "g_B")):
                    lr[nm] = ED(nc.sbuf_tensor(f"{nm}{seg}", shp, BF))
                    if dsrc is not None:
                        self.dma("D2", lr[nm][:], D[dsrc], sem="dlr")
                self.dma("D2", lr["xbA"][:], D["x_Bp"][0:128, :], sem="dlr")
                self.dma("D2", lr["xbB"][:], D["x_Bp"][128:192, :], sem="dlr")
                c_lr = ("dlr", p.n["dlr"])

                c_ln = self.emit_ln_seg(xlnS, s0, sc, SEGA)
                lastdx = None
                for ct in range(NCT):
                    self.tt("vector", dxS[:, ct * SEGA + 1: ct * SEGA + SEGA],
                            xlnS[:, ct * SEGA: ct * SEGA + SEGA - 1],
                            xlnS[:, ct * SEGA + 1: ct * SEGA + SEGA], ALU.subtract)
                    lastdx = self.tt("vector", dxS[:, ct * SEGA: ct * SEGA + 1],
                                     prevcol[:, ct:ct + 1],
                                     xlnS[:, ct * SEGA: ct * SEGA + 1], ALU.subtract,
                                     wait=[("G", czp[1])] if seg == 0 else [])
                    self.cp("vector", prevcol[:, ct:ct + 1],
                            xlnS[:, ct * SEGA + SEGA - 1: ct * SEGA + SEGA])
                segtw = [("T", p.n["T"])]
                for ct in range(NCT):
                    self.ts("vector", t1g[:], dxS[:, segsl(ct)], self.col("x_miu", ct),
                            0.0, ALU.mult, ALU.add, wait=segtw)
                    segtw = []
                    self.tt("vector", xdP[:, segsl(ct)], t1g[:],
                            xlnS[:, segsl(ct)], ALU.add)
                c_xm = ("V", p.n["V"])
                cmA = cmB = None
                for ct in range(NCT):
                    cmA = self.mm(("pf", 0) if ct == 0 else None, self.pf[0][:, 0:SEGA],
                                  lr["xaw"][:, ct * 192: ct * 192 + 128],
                                  xdP[:, segsl(ct)], ct == 0, ct == 7,
                                  wait=[c_lr, c_xm] if ct == 0 else [])
                for ct in range(NCT):
                    cmB = self.mm(("pf", 1) if ct == 0 else None, self.pf[1][0:64, 0:SEGA],
                                  lr["xaw"][:, ct * 192 + 128: ct * 192 + 192],
                                  xdP[:, segsl(ct)], ct == 0, ct == 7)
                self.act(mixA[:], self.pf[0][:, 0:SEGA], AF.Tanh, wait=[cmA],
                         relkey=("pf", 0))
                c_mix = self.act(mixB[0:64, :], self.pf[1][0:64, 0:SEGA], AF.Tanh,
                                 wait=[cmB], relkey=("pf", 1))

                def xd_build(g6, cdep):
                    for ct in range(NCT):
                        bk = ct % 2
                        if g6 < 4:
                            lhs = lr["xbA"][g6 * 32:(g6 + 1) * 32,
                                            ct * 128:(ct + 1) * 128]
                            rhs = mixA[g6 * 32:(g6 + 1) * 32, :]
                            tp = (g6 * 32, 0)
                        else:
                            lhs = lr["xbB"][(g6 - 4) * 32:(g6 - 3) * 32,
                                            ct * 128:(ct + 1) * 128]
                            rhs = mixB[(g6 - 4) * 32:(g6 - 3) * 32, :]
                            tp = ((g6 - 4) * 32, 0)
                        cm = self.mm(("pf", 2 + bk), self.pf[2 + bk][:, 0:SEGA],
                                     lhs, rhs, True, True, tp=tp,
                                     wait=[cdep] if ct == 0 else [])
                        cs = self.act(miu[:], self.pf[2 + bk][:, 0:SEGA], AF.Identity,
                                      bias=self.col(f"lam{g6}", ct), wait=[cm],
                                      relkey=("pf", 2 + bk))
                        cg = self.tt("gpsimd", t1g[:], dxS[:, segsl(ct)], miu[:],
                                     ALU.mult, wait=[cs])
                        self.tt("vector", xdP[:, segsl(ct)],
                                t1g[:], xlnS[:, segsl(ct)], ALU.add, wait=[cg])
                    return ("V", p.n["V"])

                def cxc_proj(mat, cdep=None):
                    for rb in range(NCT):
                        slot = rb % 2
                        c_st = strip_dma(mat, rb, slot)
                        bk = rb % 2
                        cm = None
                        for ct in range(NCT):
                            w0 = [c_st] if ct == 0 else []
                            if ct == 0 and rb == 0 and cdep is not None:
                                w0.append(cdep)
                            cm = self.mm(("pf", bk) if ct == 0 else None,
                                         self.pf[bk][:, 0:SEGA],
                                         wstrip[:, slot * 1024 + ct * 128:
                                                slot * 1024 + (ct + 1) * 128],
                                         xdP[:, segsl(ct)], ct == 0, False,
                                         wait=w0)
                        yield rb, bk, cm
                        self.setg(("astrip", slot), self.p.guards.get(("pf", bk)))

                c_xd = xd_build(2, c_mix)
                cm = None
                for ct in range(NCT):
                    cm = self.mm(("pf", 4) if ct == 0 else None, self.pf[4][0:64, 0:SEGA],
                                 lr["kaw"][:, ct * 64:(ct + 1) * 64],
                                 xdP[:, segsl(ct)], ct == 0, ct == 7,
                                 wait=[c_xd] if ct == 0 else [])
                c_t64 = self.act(t64[0:64, :], self.pf[4][0:64, 0:SEGA], AF.Tanh,
                                 wait=[cm], relkey=("pf", 4))
                for rb, bk, cm in cxc_proj("kWT"):
                    cm2 = self.mm(None, self.pf[bk][:, 0:SEGA],
                                  lr["kbw"][0:64, rb * 128:(rb + 1) * 128],
                                  t64[0:64, :], False, True, wait=[c_t64])
                    self.cp("vector", kapS[:, segsl(rb)], self.pf[bk][:, 0:SEGA],
                            wait=[cm2], relkey=("pf", bk))
                c_kk0 = ("V", p.n["V"])
                for ct in range(NCT):
                    q0 = ct % 2
                    gq = self.guard(("ksq", q0))
                    dst = t1g[:] if q0 == 0 else miu[:]
                    self.act(dst, kapS[:, segsl(ct)], AF.Square,
                             wait=([c_kk0] if ct == 0 else []) + ([gq] if gq else []))
                    cmk = self.mm(("pf", 4) if ct == 0 else None, self.pf[4][0:1, 0:SEGA],
                                  sb["onescol"][:], dst, ct == 0, ct == 7,
                                  wait=[("S", p.n["S"])])
                    self.setg(("ksq", q0), cmk)
                cr = sc["rows"]
                self.act(cr[0:1, 0:SEGA], self.pf[4][0:1, 0:SEGA], AF.Ln,
                         wait=[("T", p.n["T"])], relkey=("pf", 4))
                self.act(cr[0:1, 0:SEGA], cr[0:1, 0:SEGA], AF.Exp, scale=0.5)
                c_nr = self.ts("vector", cr[0:1, SEGA:2 * SEGA], cr[0:1, 0:SEGA],
                               1e-12, 0.0, ALU.max, ALU.add,
                               wait=[("S", p.n["S"])])
                c_rec = self.recip(cr[0:1, 2 * SEGA:3 * SEGA], cr[0:1, SEGA:2 * SEGA])
                c_bc = self.bcrow(("pf", 5), self.pf[5][:, 0:SEGA],
                                  cr[0:1, 2 * SEGA:3 * SEGA], wait=[c_rec])
                for ct in range(NCT):
                    self.tt("vector", kapS[:, segsl(ct)], kapS[:, segsl(ct)],
                            self.pf[5][:, 0:SEGA], ALU.mult,
                            wait=[c_bc] if ct == 0 else [])
                c_kap = ("V", p.n["V"])
                self.setg(("pf", 5), c_kap)

                c_xd = xd_build(1, c_kap)
                cm = None
                for ct in range(NCT):
                    cm = self.mm(("pf", 4) if ct == 0 else None, self.pf[4][0:64, 0:SEGA],
                                 lr["waw"][:, ct * 64:(ct + 1) * 64],
                                 xdP[:, segsl(ct)], ct == 0, ct == 7,
                                 wait=[c_xd] if ct == 0 else [])
                c_t64 = self.act(t64[0:64, :], self.pf[4][0:64, 0:SEGA], AF.Tanh,
                                 wait=[cm], relkey=("pf", 4))
                if seg == 7:
                    self.tap("w_t64", t64[0:64, :], [64, SEGA], BF, [c_t64])
                for rb in range(NCT):
                    bk = 2 + rb % 2
                    cm2 = self.mm(("pf", bk), self.pf[bk][:, 0:SEGA],
                                  lr["wbw"][0:64, rb * 128:(rb + 1) * 128],
                                  t64[0:64, :], True, True, wait=[c_t64])
                    wtap = []
                    if seg == 7 and rb == 1 and "w_pf1" in self.taps:
                        ctp = self.cp("vector", sc["t1f"][:], self.pf[bk][:, 0:SEGA],
                                      wait=[cm2])
                        self.tap("w_pf1", sc["t1f"][:], [128, SEGA], F32, [ctp])
                        wtap = [("dt", p.n["dt"])]
                    self.act(l1t[:], self.pf[bk][:, 0:SEGA], AF.Exp, scale=-1.0,
                             bias=ncols[:, rb:rb + 1], wait=[cm2, c_neg] + wtap,
                             relkey=("pf", bk))
                    if seg == 7 and rb == 1:
                        self.tap("w_l1e", l1t[:], [128, SEGA], F32,
                                 [("S", p.n["S"])])
                    cl1 = self.act(l1t[:], l1t[:], AF.Ln, bias=self.col("c_one"))
                    self.ts("vector", wSf[:, segsl(rb)], l1t[:], -1.0, -0.5,
                            ALU.mult, ALU.add, wait=[cl1])
                    ce2 = self.act(t1g[:], l1t[:], AF.Exp, scale=-0.5,
                                   bias=self.col("c_mquart"))
                    self.tt("vector", kkS[:, segsl(rb)], kapS[:, segsl(rb)], t1g[:],
                            ALU.mult, wait=[ce2])
                c_w = ("V", p.n["V"])

                c_xd = xd_build(4, c_w)
                cm = None
                for ct in range(NCT):
                    cm = self.mm(("pf", 4) if ct == 0 else None, self.pf[4][0:64, 0:SEGA],
                                 lr["aaw"][:, ct * 64:(ct + 1) * 64],
                                 xdP[:, segsl(ct)], ct == 0, ct == 7,
                                 wait=[c_xd] if ct == 0 else [])
                c_t64 = self.act(t64[0:64, :], self.pf[4][0:64, 0:SEGA], AF.Identity,
                                 wait=[cm], relkey=("pf", 4))
                for rb in range(NCT):
                    bk = 2 + rb % 2
                    cm2 = self.mm(("pf", bk), self.pf[bk][:, 0:SEGA],
                                  lr["abw"][0:64, rb * 128:(rb + 1) * 128],
                                  t64[0:64, :], True, True, wait=[c_t64])
                    wtap = []
                    if seg == 7 and rb == 0 and "a_pf0" in self.taps:
                        ctp = self.cp("vector", sc["t1f"][:], self.pf[bk][:, 0:SEGA],
                                      wait=[cm2])
                        self.tap("a_pf0", sc["t1f"][:], [128, SEGA], F32, [ctp])
                        self.tap("a_t64", t64[0:64, :], [64, SEGA], BF, [ctp])
                        wtap = [("dt", p.n["dt"])]
                    ca = self.act(ag[:], self.pf[bk][:, 0:SEGA], AF.Sigmoid,
                                  bias=self.col("a_time", rb), wait=[cm2] + wtap,
                                  relkey=("pf", bk))
                    if seg == 7 and rb == 0:
                        self.tap("a_ag0", ag[:], [128, SEGA], BF, [ca])
                    self.stt("vector", bS[:, segsl(rb)], kapS[:, segsl(rb)], 2.0,
                             ag[:], ALU.mult, ALU.mult, wait=[ca])
                c_b = ("V", p.n["V"])

                c_xd = xd_build(5, c_b)
                cm = None
                for ct in range(NCT):
                    cm = self.mm(("pf", 4) if ct == 0 else None, self.pf[4][:, 0:SEGA],
                                 lr["gaw"][:, ct * 128:(ct + 1) * 128],
                                 xdP[:, segsl(ct)], ct == 0, ct == 7,
                                 wait=[c_xd] if ct == 0 else [])
                c_tg = self.act(t64[:], self.pf[4][:, 0:SEGA], AF.Tanh, wait=[cm],
                                relkey=("pf", 4))
                for rb in range(NCT):
                    bk = 2 + rb % 2
                    cm2 = self.mm(("pf", bk), self.pf[bk][:, 0:SEGA],
                                  lr["gbw"][:, rb * 128:(rb + 1) * 128],
                                  t64[:], True, True, wait=[c_tg])
                    self.cp("vector", gS[:, segsl(rb)], self.pf[bk][:, 0:SEGA],
                            wait=[cm2], relkey=("pf", bk))
                c_xd = xd_build(3, ("V", p.n["V"]))
                for rb, bk, cm in cxc_proj("vWT", c_xd):
                    cmv = self.mm(None, self.pf[bk][:, 0:SEGA], sb["zerosq"][:],
                                  xdP[:, 0:SEGA], False, True)
                    self.cp("vector", vS[:, segsl(rb)], self.pf[bk][:, 0:SEGA],
                            wait=[cmv], relkey=("pf", bk))
                c_xd = xd_build(0, ("V", p.n["V"]))
                for rb, bk, cm in cxc_proj("rWT", c_xd):
                    cmr = self.mm(None, self.pf[bk][:, 0:SEGA], sb["zerosq"][:],
                                  xdP[:, 0:SEGA], False, True)
                    self.cp("vector", rS[:, segsl(rb)], self.pf[bk][:, 0:SEGA],
                            wait=[cmr], relkey=("pf", bk))
                c_proj = ("V", p.n["V"])
                if seg == 0:
                    for nm_, tn_ in (("a_rS", rS), ("a_kapS", kapS), ("a_vS", vS),
                                     ("a_gS", gS), ("a_bS", bS), ("a_kkS", kkS)):
                        self.tap(nm_, tn_[:], [128, SW], BF, [c_proj])
                    self.tap("a_wSf", wSf[:], [128, SW], F32, [c_proj])

                cm = None
                for ct in range(NCT):
                    q0 = ct % 2
                    dst = t1g[:] if q0 == 0 else miu[:]
                    gq = self.guard(("bns", q0))
                    self.tt("vector", dst, rS[:, segsl(ct)], kapS[:, segsl(ct)],
                            ALU.mult,
                            wait=([c_proj] if ct == 0 else []) + ([gq] if gq else []))
                    self.ts("vector", dst, dst, self.col("xtime", ct), 0.0,
                            ALU.mult, ALU.add)
                    cm = self.mm(("pf", 4) if ct == 0 else None, self.pf[4][0:16, 0:SEGA],
                                 sb["gnpat"][:, ct * 16:(ct + 1) * 16], dst,
                                 ct == 0, ct == 7, wait=[("V", p.n["V"]), c_lr])
                    self.setg(("bns", q0), cm)
                self.cp("vector", bsum[0:16, :], self.pf[4][0:16, 0:SEGA], wait=[cm],
                        relkey=("pf", 4))
                stD.close()
                self.fence()

                stE = ExitStack()
                EE = stE.enter_context
                WWr = EE(nc.sbuf_tensor(f"a_WWr{seg}", [128, 512], BF))
                UUr = EE(nc.sbuf_tensor(f"a_UUr{seg}", [128, 512], BF))
                Wb = EE(nc.sbuf_tensor(f"a_Wb{seg}", [128, 256], BF))
                SQ = EE(nc.sbuf_tensor(f"a_SQ{seg}", [128, 14 * 128], BF))
                Xsb = EE(nc.sbuf_tensor(f"a_Xsb{seg}", [128, 256], BF))
                Vtok = EE(nc.sbuf_tensor(f"a_Vtok{seg}", [128, 128], BF))
                BKt = EE(nc.sbuf_tensor(f"a_BKt{seg}", [128, 256], BF))
                Gfm = EE(nc.sbuf_tensor(f"a_Gfm{seg}", [128, 128], BF))
                Ztok = EE(nc.sbuf_tensor(f"a_Ztok{seg}", [128, 128], BF))
                cwF = EE(nc.sbuf_tensor(f"a_cwF{seg}", [128, SEGA], F32))
                cwP = EE(nc.sbuf_tensor(f"a_cwP{seg}", [128, SEGA], F32))
                subt = EE(nc.sbuf_tensor(f"a_subt{seg}", [128, SEGA], F32))
                crefb = EE(nc.sbuf_tensor(f"a_crefb{seg}", [128, SEGA], F32))
                refc = EE(nc.sbuf_tensor(f"a_refc{seg}", [128, 2 * NCH], F32))
                fPrev = EE(nc.sbuf_tensor(f"a_fPrev{seg}", [128, SEGA], BF))
                fIncl = EE(nc.sbuf_tensor(f"a_fIncl{seg}", [128, SEGA], BF))
                fHatP = EE(nc.sbuf_tensor(f"a_fHatP{seg}", [128, SEGA], BF))
                fHatI = EE(nc.sbuf_tensor(f"a_fHatI{seg}", [128, SEGA], BF))
                fTil = EE(nc.sbuf_tensor(f"a_fTil{seg}", [128, SEGA], BF))
                fBar = EE(nc.sbuf_tensor(f"a_fBar{seg}", [128, SEGA], BF))
                oA = EE(nc.sbuf_tensor(f"a_oA{seg}", [128, SEGA], BF))
                oAh = EE(nc.sbuf_tensor(f"a_oAh{seg}", [128, SEGA], BF))
                oR = EE(nc.sbuf_tensor(f"a_oR{seg}", [128, SEGA], BF))
                oRh = EE(nc.sbuf_tensor(f"a_oRh{seg}", [128, SEGA], BF))
                oBt = EE(nc.sbuf_tensor(f"a_oBt{seg}", [128, SEGA], BF))
                oKt = EE(nc.sbuf_tensor(f"a_oKt{seg}", [128, SEGA], BF))
                oBb = EE(nc.sbuf_tensor(f"a_oBb{seg}", [128, SEGA], BF))
                oKb = EE(nc.sbuf_tensor(f"a_oKb{seg}", [128, SEGA], BF))

                for ct in range(NCT):
                    csl = segsl(ct)
                    prep_tw = ("T", p.n["T"])
                    c_sc = self.scan(cwF[:], sb["cm_a"][:], wSf[:, csl], 0.0,
                                     ALU.mult, ALU.add,
                                     wait=[c_proj, c_lr, prep_tw])
                    self.tt("vector", cwP[:], cwF[:], wSf[:, csl], ALU.subtract)
                    self.cp("vector", refc[:, 0:NCH],
                            cwF[:].rearrange("q (c w) -> q c w", w=32)[:, :, 15:16]
                            .squeeze(2))
                    self.cp("vector", refc[:, NCH:2 * NCH],
                            cwF[:].rearrange("q (c w) -> q c w", w=32)[:, :, 31:32]
                            .squeeze(2))
                    self.act(PLs[:, ct * NCH:(ct + 1) * NCH], refc[:, NCH:2 * NCH],
                             AF.Exp, wait=[("V", p.n["V"])])
                    cbd = self.cp("vector",
                                  crefb[:].rearrange("q (c w) -> q c w", w=32),
                                  refc[:, 0:NCH].rearrange("q (c w) -> q c w", w=1)
                                  .broadcast_to([128, NCH, 32]))
                    c1 = self.tt("vector", subt[:], cwP[:], crefb[:], ALU.subtract)
                    self.act(fHatP[:], subt[:], AF.Exp, bias=self.col("c_m11"),
                             wait=[c1])
                    c2 = self.tt("vector", subt[:], cwF[:], crefb[:], ALU.subtract,
                                 wait=[("S", p.n["S"])])
                    self.act(fHatI[:], subt[:], AF.Exp, bias=self.col("c_m11"),
                             wait=[c2])
                    c2b = self.act(fTil[:], subt[:], AF.Exp, scale=-1.0,
                                   bias=self.col("c_m11"))
                    cbd2 = self.cp("vector",
                                   crefb[:].rearrange("q (c w) -> q c w", w=32),
                                   refc[:, NCH:2 * NCH]
                                   .rearrange("q (c w) -> q c w", w=1)
                                   .broadcast_to([128, NCH, 32]), wait=[c2b])
                    c3 = self.tt("vector", subt[:], crefb[:], cwF[:], ALU.subtract)
                    self.act(fBar[:], subt[:], AF.Exp, wait=[c3])
                    self.act(fPrev[:], cwP[:], AF.Exp)
                    c_f = self.act(fIncl[:], cwF[:], AF.Exp)
                    self.stt("vector", oA[:], kapS[:, csl], -1.0, fPrev[:],
                             ALU.mult, ALU.mult, wait=[c_f, prep_tw])
                    self.stt("vector", oAh[:], kapS[:, csl], -1.0, fHatP[:],
                             ALU.mult, ALU.mult)
                    self.tt("gpsimd", oR[:], rS[:, csl], fIncl[:], ALU.mult)
                    self.tt("gpsimd", oRh[:], rS[:, csl], fHatI[:], ALU.mult)
                    self.tt("vector", oBt[:], bS[:, csl], fTil[:], ALU.mult)
                    self.tt("vector", oKt[:], kkS[:, csl], fTil[:], ALU.mult)
                    self.tt("vector", oBb[:], bS[:, csl], fBar[:], ALU.mult)
                    c_ops = self.tt("vector", oKb[:], kkS[:, csl], fBar[:], ALU.mult)
                    c_prep = [c_ops, ("G", p.n["G"]), ("S", p.n["S"])]

                    for stl in range(NSTL):
                        scol = slice(stl * 128, (stl + 1) * 128)
                        w0 = c_prep if stl == 0 else []
                        cts = []
                        for hh, po in ((0, 0), (1, 64)):
                            cts.append(self.tr(("pb", 0) if hh == 0 else None,
                                               self.pb[0][:, hh * 64:hh * 64 + 64],
                                               vS[po:po + 64, ct * SEGA + stl * 128:
                                                  ct * SEGA + (stl + 1) * 128],
                                               sb["identb"][po:po + 64, po:po + 64],
                                               wait=w0))
                        cv = self.cp("vector", Vtok[:], self.pb[0][:, 0:128],
                                     wait=[cts[-1]], relkey=("pb", 0))
                        fm = []
                        for hh, po in ((0, 0), (1, 64)):
                            fm.append(self.mm(("pf", 1) if hh == 0 else None,
                                              self.pf[1][:, hh * 256: hh * 256 + 128],
                                              oBt[po:po + 64, scol],
                                              oAh[po:po + 64, scol], True, True,
                                              tp=(po, 0), wait=w0))
                            fm.append(self.mm(None,
                                              self.pf[1][:, hh * 256 + 128: hh * 256 + 256],
                                              oBt[po:po + 64, scol],
                                              oRh[po:po + 64, scol], True, True,
                                              tp=(po, 0)))
                        self.tt("vector", WWr[:, 0:256], self.pf[1][:, 0:256],
                                sb["mask_tu"][:], ALU.mult,
                                wait=[fm[-1], ("dlr2", 48)])
                        cw2 = self.tt("vector", WWr[:, 256:512], self.pf[1][:, 256:512],
                                      sb["mask_tu"][:], ALU.mult, relkey=("pf", 1))
                        fu = []
                        for hh, po in ((0, 0), (1, 64)):
                            fu.append(self.mm(("pf", 0) if hh == 0 else None,
                                              self.pf[0][:, hh * 256: hh * 256 + 128],
                                              oKt[po:po + 64, scol],
                                              oAh[po:po + 64, scol], True, True,
                                              tp=(po, 0)))
                            fu.append(self.mm(None,
                                              self.pf[0][:, hh * 256 + 128: hh * 256 + 256],
                                              oKt[po:po + 64, scol],
                                              oRh[po:po + 64, scol], True, True,
                                              tp=(po, 0)))
                        self.tt("vector", UUr[:, 0:256], self.pf[0][:, 0:256],
                                sb["mask_tu"][:], ALU.mult, wait=[fu[-1]])
                        cu2 = self.tt("vector", UUr[:, 256:512], self.pf[0][:, 256:512],
                                      sb["mask_tu"][:], ALU.mult, relkey=("pf", 0))
                        fw = []
                        for hh, po in ((0, 0), (1, 64)):
                            fw.append(self.mm(("pf", 1) if hh == 0 else None,
                                              self.pf[1][:, hh * 128: hh * 128 + 128],
                                              oAh[po:po + 64, scol],
                                              oBt[po:po + 64, scol], True, True,
                                              tp=(po, 0)))
                        self.tt("vector", Wb[:, 0:128], self.pf[1][:, 0:128],
                                sb["mask_sl"][:], ALU.mult, wait=[fw[-1]])
                        cwb = self.tt("vector", Wb[:, 128:256], self.pf[1][:, 128:256],
                                      sb["mask_sl"][:], ALU.mult, relkey=("pf", 1))
                        cta = []
                        ctu = []
                        for hh, po in ((0, 0), (1, 64)):
                            cta.append(self.tr(("pb", 0) if hh == 0 else None,
                                               self.pb[0][:, hh * 64: hh * 64 + 64],
                                               oA[po:po + 64, scol],
                                               sb["identb"][po:po + 64, po:po + 64],
                                               wait=[cv] if hh == 0 else []))
                            ctu.append(self.mm(("pf", 0) if hh == 0 else None,
                                               self.pf[0][:, hh * 64: hh * 64 + 64],
                                               UUr[:, hh * 256: hh * 256 + 128],
                                               Vtok[:, hh * 64: hh * 64 + 64],
                                               True, True,
                                               wait=[cu2, cv] if hh == 0 else []))
                        for hh in range(2):
                            self.cp("vector", Xsb[:, hh * 128: hh * 128 + 64],
                                    self.pb[0][:, hh * 64: hh * 64 + 64],
                                    wait=[cta[-1]] if hh == 0 else [])
                            self.cp("vector", Xsb[:, hh * 128 + 64: hh * 128 + 128],
                                    self.pf[0][:, hh * 64: hh * 64 + 64],
                                    wait=[ctu[-1]] if hh == 0 else [])
                        cx = ("V", p.n["V"])
                        self.setg(("pb", 0), cx)
                        self.setg(("pf", 0), cx)
                        ctb = []
                        for hh, po in ((0, 0), (1, 64)):
                            ctb.append(self.tr(("pb", 0) if hh == 0 else None,
                                               self.pb[0][:, hh * 64: hh * 64 + 64],
                                               oBb[po:po + 64, scol],
                                               sb["identb"][po:po + 64, po:po + 64]))
                            ctb.append(self.tr(None,
                                               self.pb[0][:, 128 + hh * 64:
                                                          128 + hh * 64 + 64],
                                               oKb[po:po + 64, scol],
                                               sb["identb"][po:po + 64, po:po + 64]))
                        cbk = self.cp("vector", BKt[:], self.pb[0][:, 0:256],
                                      wait=[ctb[-1]], relkey=("pb", 0))
                        self.setg(("pb", 0), cbk)
                        prevT = [(WWr, 0), (WWr, 256)]
                        prevU = [(Wb, 0), (Wb, 128)]
                        sqoff = 0
                        for lv in range(3):
                            mms = []
                            for hh in range(2):
                                tT, oT = prevT[hh]
                                tU, oU = prevU[hh]
                                mms.append(self.mm(("pf", 1) if hh == 0 else None,
                                                   self.pf[1][:, hh * 256: hh * 256 + 128],
                                                   tT[:, oT:oT + 128],
                                                   tU[:, oU:oU + 128], True, True,
                                                   wait=[cwb, cw2] if (lv == 0 and hh == 0) else []))
                                mms.append(self.mm(None,
                                                   self.pf[1][:, hh * 256 + 128: hh * 256 + 256],
                                                   tU[:, oU:oU + 128],
                                                   tT[:, oT:oT + 128], True, True))
                            self.cp("vector", SQ[:, sqoff:sqoff + 512],
                                    self.pf[1][:, 0:512], wait=[mms[-1]],
                                    relkey=("pf", 1))
                            prevU = [(SQ, sqoff), (SQ, sqoff + 256)]
                            prevT = [(SQ, sqoff + 128), (SQ, sqoff + 384)]
                            sqoff += 512
                        mms = []
                        for hh in range(2):
                            tT, oT = prevT[hh]
                            tU, oU = prevU[hh]
                            mms.append(self.mm(("pf", 1) if hh == 0 else None,
                                               self.pf[1][:, hh * 128: hh * 128 + 128],
                                               tU[:, oU:oU + 128],
                                               tT[:, oT:oT + 128], True, True))
                        c16 = self.cp("vector", SQ[:, 1536:1792],
                                      self.pf[1][:, 0:256], wait=[mms[-1]],
                                      relkey=("pf", 1))
                        applyT = [[(WWr, 0), (WWr, 256)],
                                  [(SQ, 128), (SQ, 384)],
                                  [(SQ, 640), (SQ, 896)],
                                  [(SQ, 1152), (SQ, 1408)],
                                  [(SQ, 1536), (SQ, 1664)]]
                        for kk_ in range(5):
                            mms = []
                            for hh in range(2):
                                tT, oT = applyT[kk_][hh]
                                mms.append(self.mm(("pf", 0) if hh == 0 else None,
                                                   self.pf[0][:, hh * 128: hh * 128 + 128],
                                                   tT[:, oT:oT + 128],
                                                   Xsb[:, hh * 128: hh * 128 + 128],
                                                   True, True,
                                                   wait=[cx, c16] if (kk_ == 0 and hh == 0) else []))
                            self.tt("vector", Xsb[:], self.pf[0][:, 0:256], Xsb[:],
                                    ALU.add, wait=[mms[-1]], relkey=("pf", 0))
                        ctg = []
                        for hh in range(2):
                            ctg.append(self.tr(("pb", 1) if hh == 0 else None,
                                               self.pb[1][hh * 64: hh * 64 + 64, 0:128],
                                               Xsb[:, hh * 128: hh * 128 + 64],
                                               sb["identb"][:]))
                        cgf = self.cp("vector", Gfm[:], self.pb[1][:, 0:128],
                                      wait=[ctg[-1]], relkey=("pb", 1))
                        copen = self.mm(("pf", 2), self.pf[2][:, 0:128],
                                        sb["zerosq"][:], WWr[:, 0:128], True, False)
                        for cch in range(4):
                            crs = slice(cch * 32, (cch + 1) * 32)
                            zw0 = [("G", czs[1])] if not self._st_init else []
                            self._st_init = True
                            zm = []
                            for hh, po in ((0, 0), (1, 64)):
                                zm.append(self.mm(("pf", 3) if hh == 0 else None,
                                                  self.pf[3][cch * 32:(cch + 1) * 32,
                                                             hh * 64: hh * 64 + 64],
                                                  Gfm[po:po + 64, cch * 32:(cch + 1) * 32],
                                                  stS[po:po + 64, ct * 64:(ct + 1) * 64],
                                                  True, True, tp=(po, cch * 32),
                                                  wait=([cgf] if (cch == 0 and hh == 0) else []) + zw0))
                            for hh in range(2):
                                self.tt("vector", Ztok[crs, hh * 64: hh * 64 + 64],
                                        self.pf[3][crs, hh * 64: hh * 64 + 64],
                                        Xsb[crs, 64 + hh * 128: 128 + hh * 128],
                                        ALU.add, wait=[zm[-1]])
                            czt = ("V", p.n["V"])
                            self.setg(("pf", 3), czt)
                            for hh, po in ((0, 0), (1, 64)):
                                self.mm(None, self.pf[2][po:po + 64, crs],
                                        stS[po:po + 64, ct * 64:(ct + 1) * 64],
                                        oR[po:po + 64, stl * 128 + cch * 32:
                                           stl * 128 + (cch + 1) * 32],
                                        False, False, tp=(po, po))
                            su1 = self.mm(("pf", 4), self.pf[4][:, 0:128],
                                          BKt[crs, 0:128], Ztok[crs, 0:128],
                                          True, False, tp=(cch * 32, 0),
                                          wait=[czt, cbk])
                            su2 = self.mm(None, self.pf[4][:, 0:128],
                                          BKt[crs, 128:256], Vtok[crs, 0:128],
                                          False, True, tp=(cch * 32, 0))
                            for hh, po in ((0, 0), (1, 64)):
                                self.stt("vector", stS[po:po + 64, ct * 64:(ct + 1) * 64],
                                         stS[po:po + 64, ct * 64:(ct + 1) * 64],
                                         PLs[po:po + 64, ct * NCH + stl * 4 + cch:
                                             ct * NCH + stl * 4 + cch + 1],
                                         self.pf[4][po:po + 64, po:po + 64],
                                         ALU.mult, ALU.add, wait=[su2])
                            self.setg(("pf", 4), ("V", p.n["V"]))
                        self.mm(None, self.pf[2][0:64, 0:128], Ztok[:, 0:64],
                                WWr[:, 128:256], False, False, tp=(0, 0))
                        self.mm(None, self.pf[2][0:64, 0:128], Vtok[:, 0:64],
                                UUr[:, 128:256], False, False, tp=(0, 0))
                        self.mm(None, self.pf[2][64:128, 0:128], Ztok[:, 64:128],
                                WWr[:, 384:512], False, False, tp=(0, 64))
                        self.mm(None, self.pf[2][64:128, 0:128], Vtok[:, 64:128],
                                UUr[:, 384:512], False, False, tp=(0, 64))
                        cyc = self.mm(None, self.pf[2][:, 0:128], sb["zerosq"][:],
                                      WWr[:, 0:128], False, True)
                        self.cp("vector", yS[:, ct * SEGA + stl * 128:
                                             ct * SEGA + (stl + 1) * 128],
                                self.pf[2][:, 0:128], wait=[cyc], relkey=("pf", 2))
                c_wkv = ("V", p.n["V"])
                if seg == 0:
                    self.tap("a_yS", yS[:], [128, SW], BF, [c_wkv])

                cs1 = cs2 = None
                for ct in range(NCT):
                    cs1 = self.mm(("pf", 2) if ct == 0 else None, self.pf[2][0:16, 0:SEGA],
                                  sb["gnpat"][:, ct * 16:(ct + 1) * 16],
                                  yS[:, segsl(ct)], ct == 0, ct == 7, wait=[c_wkv])
                for ct in range(NCT):
                    q0 = ct % 2
                    dst = t1g[:] if q0 == 0 else miu[:]
                    gq = self.guard(("gsq", q0))
                    self.act(dst, yS[:, segsl(ct)], AF.Square,
                             wait=([c_wkv] if ct == 0 else []) + ([gq] if gq else []))
                    cs2 = self.mm(("pf", 3) if ct == 0 else None, self.pf[3][0:16, 0:SEGA],
                                  sb["gnpat"][:, ct * 16:(ct + 1) * 16], dst,
                                  ct == 0, ct == 7, wait=[("S", p.n["S"])])
                    self.setg(("gsq", q0), cs2)
                gA = gnr[0:16, 0:SEGA]
                gB = gnr[0:16, SEGA:2 * SEGA]
                gC = gnr[0:16, 2 * SEGA:3 * SEGA]
                self.ts("vector", gA, self.pf[2][0:16, 0:SEGA], 1.0 / 64,
                        0.0, ALU.mult, ALU.add, wait=[cs1], relkey=("pf", 2))
                self.ts("vector", gB, self.pf[3][0:16, 0:SEGA], 1.0 / 64,
                        0.0, ALU.mult, ALU.add, wait=[cs2], relkey=("pf", 3))
                self.tt("vector", gC, gA, gA, ALU.mult)
                cv4 = self.tt("vector", gB, gB, gC, ALU.subtract)
                self.act(gB, gB, AF.Ln, bias=self.col("c_epsgn")[0:16, :], wait=[cv4])
                cr5 = self.act(gB, gB, AF.Exp, scale=-0.5)
                cm5 = self.tt("vector", gC, gA, gB, ALU.mult, wait=[cr5])
                for ct in range(NCT):
                    cb1 = self.mm(("pf", 3), self.pf[3][:, 0:SEGA],
                                  sb["gnpatTf"][0:16, ct * 128:(ct + 1) * 128],
                                  gB, True, True,
                                  wait=[cm5, ("dlr2", 16)] if ct == 0 else [cm5])
                    cb2 = self.mm(("pf", 4), self.pf[4][:, 0:SEGA],
                                  sb["gnpatTf"][0:16, ct * 128:(ct + 1) * 128],
                                  gC, True, True)
                    cb3 = self.mm(("pf", 5), self.pf[5][:, 0:SEGA],
                                  sb["gnpatTf"][0:16, ct * 128:(ct + 1) * 128],
                                  bsum[0:16, :], True, True)
                    self.tt("vector", sb1f[:], yS[:, segsl(ct)], self.pf[3][:, 0:SEGA],
                            ALU.mult, wait=[cb2], relkey=("pf", 3))
                    self.tt("vector", sb1f[:], sb1f[:], self.pf[4][:, 0:SEGA],
                            ALU.subtract, relkey=("pf", 4))
                    self.ts("vector", sb1f[:], sb1f[:], self.col("gn_gamma", ct),
                            0.0, ALU.mult, ALU.add)
                    self.ts("vector", sb1f[:], sb1f[:], self.col("gn_beta", ct),
                            0.0, ALU.add, ALU.add)
                    self.tt("vector", t1g[:], vS[:, segsl(ct)], self.pf[5][:, 0:SEGA],
                            ALU.mult, wait=[cb3], relkey=("pf", 5))
                    self.tt("vector", sb1f[:], sb1f[:], t1g[:], ALU.add)
                    self.tt("vector", xdP[:, segsl(ct)], sb1f[:], gS[:, segsl(ct)],
                            ALU.mult)
                c_zfin = ("V", p.n["V"])
                if seg == 0:
                    self.tap("a_zS", xdP[:], [128, SW], BF, [c_zfin])

                for rb in range(NCT):
                    slot = rb % 2
                    c_st = strip_dma("oWT", rb, slot)
                    bk = rb % 2
                    cm = None
                    for ctc in range(NCT):
                        cm = self.mm(("pf", bk) if ctc == 0 else None,
                                     self.pf[bk][:, 0:SEGA],
                                     wstrip[:, slot * 1024 + ctc * 128:
                                            slot * 1024 + (ctc + 1) * 128],
                                     xdP[:, segsl(ctc)], ctc == 0, ctc == NCT - 1,
                                     wait=[c_st, c_zfin] if ctc == 0 else [])
                    self.tt("vector", sb[f"x{rb}"][:, s0:s1g],
                            sb[f"x{rb}"][:, s0:s1g], self.pf[bk][:, 0:SEGA], ALU.add,
                            wait=[cm], relkey=("pf", bk))
                    self.setg(("astrip", slot), self.p.guards.get(("pf", bk)))
                stE.close()
                self.fence()
        self.fence()

    def finish_phase(self, out):
        p, sb = self.p, self.sb
        stg = self.stack.enter_context(self.nc.sbuf_tensor("ostage", [128, 2 * C], F32))
        bar = self.barrier()
        c_store = {}
        for tt in range(16):
            sl = stg[:, (tt % 2) * C:(tt % 2) * C + C]
            cvs = None
            for ct in range(NCT):
                bk = ("pf", 4 + ct % 2)
                cm = self.tr(bk, self.pf[4 + ct % 2][:, 0:128],
                             sb[f"x{ct}"][:, tt * 128:(tt + 1) * 128], sb["identf"][:],
                             wait=bar if (tt == 0 and ct < 2) else [])
                g = c_store.get(tt % 2)
                cvs = self.cp("vector", sl[:, ct * 128:(ct + 1) * 128],
                              self.pf[4 + ct % 2][:, 0:128],
                              wait=[cm] + ([g] if (g and ct == 0) else []),
                              relkey=bk)
            c_store[tt % 2] = self.dma("D1", out[tt * 128:(tt + 1) * 128, :], sl,
                                       wait=[cvs], sem=f"do{tt % 2}")
        fin = [("do0", p.n["do0"]), ("do1", p.n["do1"])]
        if "dt" in p.n:
            fin.append(("dt", p.n["dt"]))
        self.p.op("sync", lambda e: None, wait=fin)


def host_consts():
    c = {}
    c["identf"] = np.eye(128, dtype=np.float32)
    c["identb_src"] = np.eye(128, dtype=np.float32)
    c["onesrow_src"] = np.ones((1, 128), dtype=np.float32)
    c["onesrowf_src"] = np.ones((1, 128), dtype=np.float32)
    c["onescol_src"] = np.ones((128, 1), dtype=np.float32)
    c["zerosq_src"] = np.zeros((128, 128), dtype=np.float32)
    dp = np.zeros((128, 128), dtype=np.float32)
    for pp in range(128):
        dp[(pp // 32) * 32, pp] = 1.0
    c["dpat_src"] = dp
    m0 = (np.arange(SEG) % LCH == 0).astype(np.float32)
    c["cm_a_src"] = np.broadcast_to(1.0 - m0, (128, SEG)).copy()
    c["cm_b_src"] = np.broadcast_to(m0, (128, SEG)).copy()
    s_i = np.arange(128)[:, None]
    t_i = np.arange(128)[None, :]
    same = (s_i // LCH) == (t_i // LCH)
    e22 = np.float32(np.exp(np.float64(22.0)))
    c["mask_tu_src"] = np.concatenate([
        (same & (s_i < t_i)).astype(np.float32),
        (same & (s_i <= t_i)).astype(np.float32)], axis=1) * e22
    c["mask_sl_src"] = (same & (s_i > t_i)).astype(np.float32) * e22
    gp = np.zeros((128, 128), dtype=np.float32)
    gpt = np.zeros((16, 1024), dtype=np.float32)
    for ct in range(NCT):
        for r in range(128):
            h = 2 * ct + r // 64
            gp[r, ct * 16 + h] = 1.0
            gpt[h, ct * 128 + r] = 1.0
    c["gnpat_src"] = gp
    c["gnpatT_src"] = gpt
    qp = np.zeros((128, 64), dtype=np.float32)
    qpt = np.zeros((16, 512), dtype=np.float32)
    for qt in range(4):
        for r in range(128):
            h = 4 * qt + r // 32
            qp[r, qt * 16 + h] = 1.0
            qpt[h, qt * 128 + r] = 1.0
    c["qpat_src"] = qp
    c["qpatT_src"] = qpt
    es = np.zeros((8, 8 * 128), dtype=np.float32)
    for e in range(8):
        es[e, e * 128:(e + 1) * 128] = 1.0
    c["esel_src"] = es
    return c


def pack_cols(inputs):
    cols, idx = [], {}

    def add(name, vec):
        v = np.asarray(vec, np.float32).reshape(-1, 128)
        idx[name] = len(cols)
        for r in v:
            cols.append(r)

    add("gn_gamma", inputs["L0_attn_gn_gamma"])
    add("gn_beta", inputs["L0_attn_gn_beta"])
    for g in range(6):
        add(f"lam{g}", np.asarray(inputs["L0_attn_lambdas"])[g, 0, 0])
    add("x_miu", np.asarray(inputs["L0_attn_x_miu"])[0, 0])
    add("w_miu", np.asarray(inputs["L0_attn_w_miu"])[0, 0])
    add("a_time", np.asarray(inputs["L0_attn_a_time"])[0, 0])
    add("xtime", np.asarray(inputs["L0_attn_x_time"], np.float32).reshape(-1))
    add("o_b", inputs["L0_ffn_o_b"])
    for e in range(E_EXP):
        add(f"b1_{e}", inputs["L0_ffn_b1"][e])
        add(f"b2_{e}", inputs["L0_ffn_b2"][e])
    qb = np.asarray(inputs["L0_ffn_q_b"], np.float32).reshape(16, 64)[:, :32].reshape(-1)
    add("q_b", qb)
    add("k_b", inputs["L0_ffn_k_b"])
    add("c_eps1", np.full(128, 1e-5, np.float32))
    add("c_epsgn", np.full(128, 1.6e-4, np.float32))
    add("c_one", np.full(128, 1.0, np.float32))
    add("c_mhalf", np.full(128, -0.5, np.float32))
    add("c_mquart", np.full(128, -0.25, np.float32))
    add("c_m11", np.full(128, -11.0, np.float32))
    return np.stack(cols, axis=1).astype(np.float32), idx


def pack_rows(inputs):
    rows, idx = [], {}

    def add(name, vec):
        v = np.asarray(vec, np.float32).reshape(-1)
        idx[name] = (len(rows), v.size)
        rows.extend(v.tolist())

    add("v_b", inputs["L0_ffn_v_b"])
    add("router_b", inputs["L0_ffn_router_b"])
    return np.asarray(rows, np.float32).reshape(1, -1), idx


def make_base(inputs, colarr, rowarr, consts):
    qsel = np.asarray(inputs["L0_ffn_q_w"], np.float32).reshape(C, 16, 64)[:, :, :32]
    return dict(
        q_w_sel=np.ascontiguousarray(qsel.reshape(C, 512)),
        k_w=np.ascontiguousarray(inputs["L0_ffn_k_w"], np.float32),
        v_w=np.ascontiguousarray(inputs["L0_ffn_v_w"], np.float32),
        o_w=np.ascontiguousarray(np.asarray(inputs["L0_ffn_o_w"], np.float32)[:512]),
        router_w_t=np.ascontiguousarray(
            np.asarray(inputs["L0_ffn_router_w"], np.float32).reshape(8, 128, 8)
            .transpose(1, 0, 2).reshape(128, 64)),
        w1=np.ascontiguousarray(inputs["L0_ffn_w1"], np.float32),
        w2=np.ascontiguousarray(inputs["L0_ffn_w2"], np.float32),
        rW=np.ascontiguousarray(inputs["L0_attn_rW"], np.float32),
        kW=np.ascontiguousarray(inputs["L0_attn_kW"], np.float32),
        vW=np.ascontiguousarray(inputs["L0_attn_vW"], np.float32),
        oW=np.ascontiguousarray(inputs["L0_attn_oW"], np.float32),
        g_A=np.ascontiguousarray(inputs["L0_attn_g_A"], np.float32),
        g_B=np.ascontiguousarray(inputs["L0_attn_g_B"], np.float32),
        k_A=np.ascontiguousarray(inputs["L0_attn_k_A"], np.float32),
        k_B=np.ascontiguousarray(inputs["L0_attn_k_B"], np.float32),
        w_A=np.ascontiguousarray(inputs["L0_attn_w_A"], np.float32),
        w_B=np.ascontiguousarray(inputs["L0_attn_w_B"], np.float32),
        a_A=np.ascontiguousarray(inputs["L0_attn_a_A"], np.float32),
        a_B=np.ascontiguousarray(inputs["L0_attn_a_B"], np.float32),
        x_A=np.ascontiguousarray(inputs["L0_attn_x_A"], np.float32),
        x_Bp=np.ascontiguousarray(
            np.asarray(inputs["L0_attn_x_B"], np.float32).reshape(192, 1024)),
        cols=colarr, rows=rowarr, **consts)


def build_all(stages, taps, inputs, stop_after=None, racecheck=False):
    colarr, colidx = pack_cols(inputs)
    rowarr, rowidx = pack_rows(inputs)
    b = Builder(stages=stages, taps=taps, stop_after=stop_after, racecheck=racecheck)
    b.ncols = colarr.shape[1]
    b.nrows = rowarr.shape[1]
    b.colidx = colidx
    b.rowidx = rowidx
    nc = b.build()
    base = make_base(inputs, colarr, rowarr, host_consts())
    return nc, base


def kernel(**inputs):
    nc, base = build_all(("ffn1", "attn", "ffn2"), (), inputs)
    xs = np.asarray(inputs["L0_attn_x"], np.float32)
    in_maps = [dict(base, x=np.ascontiguousarray(xs[i])) for i in range(8)]
    res = run_bass_kernel_spmd(nc, in_maps, core_ids=list(range(8)))
    return np.stack([r["out"] for r in res.results]).astype(np.float32)


# version 03
# speedup vs baseline: 1.1459x; 1.1459x over previous
import os
import numpy as np
import ml_dtypes
import concourse.bass as bass
import concourse.mybir as mybir
from concourse.bass_utils import run_bass_kernel_spmd
from contextlib import ExitStack

BF16N = ml_dtypes.bfloat16
F32 = mybir.dt.float32
BF = mybir.dt.bfloat16
AF = mybir.ActivationFunctionType
ALU = mybir.AluOpType
AX = mybir.AxisListType.X

T_SEQ = 2048
C = 1024
NCT = 8
LCH = 32
SEG = 512
E_EXP = 8
DFF = 2048
RT32 = float(1.0 / np.sqrt(np.float32(32.0)))

ENGS = ("tensor", "vector", "scalar", "gpsimd", "sync")
_PM = os.environ.get("BASS_PRUNE", "TVG")
PRUNE_SEMS = (set() if _PM == "off"
              else {"T", "V", "S", "G"} if _PM == "compute"
              else set(_PM) if _PM.isupper()
              else None)


class _AllSems:
    def __contains__(self, x):
        return True


if PRUNE_SEMS is None:
    PRUNE_SEMS = _AllSems()
INCOF = {"tensor": "T", "vector": "V", "scalar": "S", "gpsimd": "G", "sync": "D1"}


class Plan:
    def __init__(self, nc):
        self.nc = nc
        self.items = {e: [] for e in ENGS}
        self.n = {}
        self.sem = {}
        self.guards = {}
        self.wlog = {}
        self.acc = {}
        self.waited = {}

    def mksem(self, stack, *names):
        self.stack = stack
        for nm in names:
            self.sem[nm] = stack.enter_context(self.nc.semaphore(nm))
            self.n[nm] = 0

    def getsem(self, nm):
        if nm not in self.sem:
            self.sem[nm] = self.stack.enter_context(self.nc.semaphore(nm))
            self.n[nm] = 0
        return self.sem[nm]

    def op(self, eng, fn, wait=(), inc=None, inc_n=None, reads=(), writes=()):
        if inc is not None and inc_n is None:
            inc_n = 16 if inc.startswith("D") else 1
        w = [x for x in wait if x is not None]
        if eng in ("vector", "scalar", "gpsimd") and reads:
            log = self.wlog.setdefault(eng, [])
            idx = len(self.items[eng])
            need = 0
            for rs in reads:
                if rs is None:
                    continue
                for (widx, wcnt, ws) in reversed(log):
                    if idx - widx > 8:
                        break
                    if ws[0] == rs[0] and ws[1] < rs[2] and rs[1] < ws[2]:
                        need = max(need, wcnt)
                        break
            if need:
                w.append((eng[0].upper(), need))
        is_dma = inc is not None and inc_n == 16
        dom = ("dma", inc, self.n.get(inc, 0)) if is_dma else eng
        reads = [r for r in reads if r is not None]
        writes = [ws for ws in writes if ws is not None]
        if reads or writes:
            for rs in reads:
                rec = self.acc.get(rs[0])
                if rec:
                    for (kind, s, e, d2, sn, sv) in rec["recent"]:
                        if kind == "w" and d2 != dom and s < rs[2] and rs[1] < e:
                            w.append((sn, sv))
                    for (d2, sn), (sv, s, e) in rec["oldw"].items():
                        if d2 != dom and s < rs[2] and rs[1] < e:
                            w.append((sn, sv))
            for ws in writes:
                rec = self.acc.get(ws[0])
                if rec:
                    for (kind, s, e, d2, sn, sv) in rec["recent"]:
                        if d2 != dom and s < ws[2] and ws[1] < e:
                            w.append((sn, sv))
                    for (d2, sn), (sv, s, e) in rec["oldw"].items():
                        if d2 != dom and s < ws[2] and ws[1] < e:
                            w.append((sn, sv))
                    for (d2, sn), (sv, s, e) in rec["oldr"].items():
                        if d2 != dom and s < ws[2] and ws[1] < e:
                            w.append((sn, sv))
        if w:
            wm = self.waited.setdefault(eng, {})
            best = {}
            for (sn, sv) in w:
                if sv > best.get(sn, -1):
                    best[sn] = sv
            w = []
            for sn, sv in best.items():
                if sn in PRUNE_SEMS:
                    if sv > wm.get(sn, 0):
                        wm[sn] = sv
                        w.append((sn, sv))
                else:
                    w.append((sn, sv))
        self.items[eng].append((w, fn, inc, inc_n))
        if inc is not None:
            self.n[inc] += inc_n
            myc = self.n[inc]
            if eng in ("vector", "scalar", "gpsimd") and writes:
                log = self.wlog.setdefault(eng, [])
                idx = len(self.items[eng]) - 1
                for ws in writes:
                    log.append((idx, myc, ws))
                if len(log) > 64:
                    del log[:-48]
            for kind, sigs in (("r", reads), ("w", writes)):
                for sg in sigs:
                    rec = self.acc.setdefault(
                        sg[0], {"recent": [], "oldw": {}, "oldr": {}})
                    rec["recent"].append((kind, sg[1], sg[2], dom, inc, myc))
                    if len(rec["recent"]) > 40:
                        for (k2, s, e, d2, sn, sv) in rec["recent"][:-32]:
                            old = rec["oldw" if k2 == "w" else "oldr"]
                            prev = old.get((d2, sn))
                            if prev is None:
                                old[(d2, sn)] = (sv, s, e)
                            else:
                                old[(d2, sn)] = (max(prev[0], sv),
                                                 min(prev[1], s),
                                                 max(prev[2], e))
                        del rec["recent"][:-32]
            return (inc, myc)
        return None

    def realize(self, block):
        sems = self.sem

        def runner(eng_name):
            def run(eng):
                for waits, fn, inc, inc_n in self.items[eng_name]:
                    for (snm, val) in waits:
                        if val > 0:
                            eng.wait_ge(sems[snm], val)
                    inst = fn(eng)
                    if inc is not None:
                        assert inst is not None
                        inst.then_inc(sems[inc], inc_n)
            return run

        block.tensor(runner("tensor"))
        block.vector(runner("vector"))
        block.scalar(runner("scalar"))
        block.gpsimd(runner("gpsimd"))
        block.sync(runner("sync"))


class Builder:
    def __init__(self, stages=("ffn1", "attn", "ffn2"), taps=(), stop_after=None,
                 racecheck=False):
        self.stages = stages
        self.taps = set(taps)
        self.stop_after = stop_after
        self.racecheck = racecheck

    @staticmethod
    def sig(ap):
        try:
            t = ap.tensor
        except Exception:
            return None
        try:
            off = int(ap.offset)
            ext = 1
            for pair in ap.ap:
                step, count = int(pair[0]), int(pair[1])
                ext += abs(step) * (count - 1)
            return (t.name, off, off + ext)
        except Exception:
            return (t.name, 0, 1 << 60)

    @staticmethod
    def wsig(ap):
        try:
            return (ap.tensor.name, 0, 1 << 60)
        except Exception:
            return None

    def guard(self, key):
        return self.p.guards.get(key)

    def setg(self, key, cnt):
        self.p.guards[key] = cnt

    def mm(self, gkey, out, lhsT, rhs, start, stop, wait=(), tp=None):
        w = list(wait)
        if start and gkey is not None:
            g = self.guard(gkey)
            if g:
                w.append(g)

        def fn(t, o=out, a=lhsT, b=rhs, s1=start, s2=stop, q=tp):
            return t.matmul(o, a, b, start=s1, stop=s2, tile_position=q)
        return self.p.op("tensor", fn, wait=w, inc="T",
                         reads=[self.sig(lhsT), self.sig(rhs)],
                         writes=[self.wsig(out) if start else self.sig(out)])

    def tr(self, gkey, out, in_, ident, wait=()):
        w = list(wait)
        if gkey is not None:
            g = self.guard(gkey)
            if g:
                w.append(g)

        def fn(t, o=out, a=in_, i=ident):
            return t.transpose(o, a, i)
        return self.p.op("tensor", fn, wait=w, inc="T",
                         reads=[self.sig(in_), self.sig(ident)],
                         writes=[self.sig(out)])

    def cp(self, eng, out, in_, wait=(), relkey=None):
        def fn(e, o=out, a=in_):
            return e.tensor_copy(o, a)
        c = self.p.op(eng, fn, wait=list(wait), inc=INCOF[eng],
                      reads=[self.sig(in_)], writes=[self.sig(out)])
        if relkey is not None:
            self.setg(relkey, c)
        return c

    def tt(self, eng, out, a, b, alu, wait=(), relkey=None):
        def fn(e, o=out, x=a, y=b, op=alu):
            return e.tensor_tensor(o, x, y, op)
        c = self.p.op(eng, fn, wait=list(wait), inc=INCOF[eng],
                      reads=[self.sig(a), self.sig(b)], writes=[self.sig(out)])
        if relkey is not None:
            self.setg(relkey, c)
        return c

    def ts(self, eng, out, in_, s1, s2, op0, op1, wait=(), relkey=None):
        def fn(e, o=out, a=in_, x=s1, y=s2, p0=op0, p1=op1):
            return e.tensor_scalar(o, a, x, y, p0, p1)
        rds = [self.sig(in_)]
        for scl in (s1, s2):
            if hasattr(scl, "tensor"):
                rds.append(self.sig(scl))
        c = self.p.op(eng, fn, wait=list(wait), inc=INCOF[eng],
                      reads=rds, writes=[self.sig(out)])
        if relkey is not None:
            self.setg(relkey, c)
        return c

    def stt(self, eng, out, in0, scalar, in1, op0, op1, wait=(), relkey=None):
        def fn(e, o=out, a=in0, s=scalar, b=in1, p0=op0, p1=op1):
            return e.scalar_tensor_tensor(o, a, s, b, p0, p1)
        rds = [self.sig(in0), self.sig(in1)]
        if hasattr(scalar, "tensor"):
            rds.append(self.sig(scalar))
        c = self.p.op(eng, fn, wait=list(wait), inc=INCOF[eng],
                      reads=rds, writes=[self.sig(out)])
        if relkey is not None:
            self.setg(relkey, c)
        return c

    def act(self, out, in_, func, bias=0.0, scale=1.0, accum=None, wait=(), relkey=None):
        def fn(s, o=out, a=in_, f=func, b=bias, sc=scale, ac=accum):
            return s.activation(o, a, f, bias=b, scale=sc, accum_out=ac)
        rds = [self.sig(in_)]
        for scl in (bias, scale):
            if hasattr(scl, "tensor"):
                rds.append(self.sig(scl))
        wrs = [self.sig(out)] + ([self.sig(accum)] if accum is not None else [])
        c = self.p.op("scalar", fn, wait=list(wait), inc="S", reads=rds, writes=wrs)
        if relkey is not None:
            self.setg(relkey, c)
        return c

    def red(self, out, in_, alu, negate=False, wait=(), relkey=None):
        def fn(v, o=out, a=in_, op=alu, ng=negate):
            return v.tensor_reduce(o, a, AX, op, negate=ng)
        c = self.p.op("vector", fn, wait=list(wait), inc="V",
                      reads=[self.sig(in_)], writes=[self.sig(out)])
        if relkey is not None:
            self.setg(relkey, c)
        return c

    def scan(self, out, d0, d1, init, op0, op1, wait=(), relkey=None):
        def fn(v, o=out, a=d0, b=d1, i=init, p0=op0, p1=op1):
            return v.tensor_tensor_scan(o, a, b, i, p0, p1)
        c = self.p.op("vector", fn, wait=list(wait), inc="V",
                      reads=[self.sig(d0), self.sig(d1)], writes=[self.sig(out)])
        if relkey is not None:
            self.setg(relkey, c)
        return c

    def recip(self, out, in_, wait=(), relkey=None):
        def fn(v, o=out, a=in_):
            return v.reciprocal(o, a)
        c = self.p.op("vector", fn, wait=list(wait), inc="V",
                      reads=[self.sig(in_)], writes=[self.sig(out)])
        if relkey is not None:
            self.setg(relkey, c)
        return c

    def dma(self, q, out, in_, wait=(), sem="dc"):
        eng = {"D1": "sync", "D2": "gpsimd"}[q]
        self.p.getsem(sem)

        def fn(e, o=out, a=in_):
            return e.dma_start(out=o, in_=a)
        return self.p.op(eng, fn, wait=list(wait), inc=sem, inc_n=16,
                         reads=[self.sig(in_)], writes=[self.sig(out)])

    def bcrow(self, gkey, out, row, nrows=128, wait=()):
        ones = self.sb["onesrowf"] if row.dtype == F32 else self.sb["onesrow"]
        return self.mm(gkey, out, ones[0:1, 0:nrows], row, True, True, wait=wait)

    def zfill(self, gkey, out, rhs, wait=()):
        return self.mm(gkey, out, self.sb["zerosq"][:], rhs, True, False, wait=wait)

    def barrier(self):
        p = self.p
        return [(nm, c) for nm, c in p.n.items() if c > 0]

    def fence(self):
        bar = self.barrier()
        for eng in ENGS:
            self.p.op(eng, lambda e: None, wait=bar)
        return bar

    def tap(self, name, ap, shape, dt, wait):
        if name not in self.taps:
            return
        d = self.nc.dram_tensor(f"dbg_{name}", list(shape), dt,
                                kind="ExternalOutput").ap()
        self.dma("D1", d, ap, wait=wait, sem="dt")

    def col(self, name, k=0):
        i = self.colidx[name] + k
        return self.sb["cols"][:, i:i + 1]

    def row(self, name):
        off, n = self.rowidx[name]
        return self.sb["rows"][0:1, off:off + n]

    def build(self):
        nc = bass.Bass("TRN2", target_bir_lowering=False,
                       detect_race_conditions=self.racecheck)
        self.nc = nc
        self.p = Plan(nc)
        D = {}

        def din(name, shape, dt=F32):
            D[name] = nc.dram_tensor(name, list(shape), dt, kind="ExternalInput").ap()
        din("x", [T_SEQ, C])
        din("q_w_sel", [C, 512]); din("k_w", [C, 512]); din("v_w", [C, 512])
        din("o_w", [512, C]); din("router_w_t", [128, 64])
        din("w1", [E_EXP, C, DFF]); din("w2", [E_EXP, DFF, C])
        for nm in ("rW", "kW", "vW", "oW"):
            din(nm, [C, C])
        din("g_A", [C, 128]); din("g_B", [128, C])
        din("k_A", [C, 64]); din("k_B", [64, C])
        din("w_A", [C, 64]); din("w_B", [64, C])
        din("a_A", [C, 64]); din("a_B", [64, C])
        din("x_A", [C, 192]); din("x_Bp", [192, C])
        din("cols", [128, self.ncols]); din("rows", [1, self.nrows])
        for nm, shp in (("identf", [128, 128]), ("identb_src", [128, 128]),
                        ("onesrow_src", [1, 128]), ("onesrowf_src", [1, 128]), ("onescol_src", [128, 1]),
                        ("zerosq_src", [128, 128]), ("dpat_src", [128, 128]),
                        ("cm_a_src", [128, SEG]), ("cm_b_src", [128, SEG]),
                        ("mask_tu_src", [128, 256]), ("mask_sl_src", [128, 128]),
                        ("gnpat_src", [128, 128]), ("gnpatT_src", [16, 1024]),
                        ("qpat_src", [128, 64]), ("qpatT_src", [16, 512]), ("esel_src", [8, 8 * 128])):
            din(nm, shp)
        out = nc.dram_tensor("out", [T_SEQ, C], F32, kind="ExternalOutput").ap()
        self.scr = {}
        for nm in ("rWT", "kWT", "vWT", "oWT"):
            self.scr[nm] = nc.dram_tensor(f"scr_{nm}", [NCT, NCT, 128, 128], BF,
                                          kind="ExternalOutput").ap()
        self.D = D
        with ExitStack() as stack:
            self.stack = stack
            self.p.mksem(stack, "T", "V", "S", "G")
            self.alloc_persist(stack)
            self.init_phase()
            if "ffn1" in self.stages:
                self.ffn_phase("f1")
            if "attn" in self.stages:
                self.attn_phase()
            if "ffn2" in self.stages:
                self.ffn_phase("f2")
            self.finish_phase(out)
            block = stack.enter_context(nc.Block())
            self.p.realize(block)
        return nc

    def alloc_persist(self, stack):
        nc = self.nc
        E = stack.enter_context
        sb = self.sb = {}
        for ct in range(NCT):
            sb[f"x{ct}"] = E(nc.sbuf_tensor(f"x{ct}", [128, T_SEQ], F32))
        for nm, shp, dt in (("cols", [128, self.ncols], F32),
                            ("rows", [1, self.nrows], BF),
                            ("identf", [128, 128], F32),
                            ("identb", [128, 128], BF),
                            ("onesrow", [1, 128], BF),
                            ("onesrowf", [1, 128], F32),
                            ("onescol", [128, 1], BF),
                            ("zerosq", [128, 128], BF),
                            ("dpat", [128, 128], BF),
                            ("qpat", [128, 64], BF),
                            ("qpatT", [16, 512], BF),
                            ("esel", [8, 8 * 128], BF),
                            ("routerw", [128, 64], BF)):
            sb[nm] = E(nc.sbuf_tensor("sb_" + nm, list(shp), dt))
        self.pf = [E(nc.psum_tensor(f"pf{i}", [128, 512], F32)) for i in range(6)]
        self.pb = [E(nc.psum_tensor(f"pbf{i}", [128, 1024], BF)) for i in range(2)]

    def init_phase(self):
        p, D, sb = self.p, self.D, self.sb
        self.dma("D1", sb["cols"][:], D["cols"])
        self.dma("D1", sb["identf"][:], D["identf"])
        self.dma("D1", self.sb["onesrowf"][:], D["onesrowf_src"])
        for src, dst in (("identb_src", "identb"), ("onesrow_src", "onesrow"),
                         ("onescol_src", "onescol"), ("zerosq_src", "zerosq"),
                         ("dpat_src", "dpat"),
                         ("qpat_src", "qpat"), ("qpatT_src", "qpatT"),
                         ("esel_src", "esel"),
                         ("rows", "rows"), ("router_w_t", "routerw")):
            self.dma("D2", sb[dst][:], D[src], sem="dg")
        self.c_consts1 = ("dc", p.n["dc"])
        self.c_consts = ("dg", p.n["dg"])
        self.init_stack = ExitStack()
        stg = self.init_stack.enter_context(self.nc.sbuf_tensor("xstage", [128, 2 * C], F32))
        for tt in range(16):
            sl = stg[:, (tt % 2) * C:(tt % 2) * C + C]
            g = self.guard(("xstg", tt % 2))
            c_ld = self.dma("D1", sl, D["x"][tt * 128:(tt + 1) * 128, :],
                            wait=[g] if g else [], sem=f"dx{tt % 2}")
            for ct in range(NCT):
                bk = ("pf", 4 + ct % 2)
                cm = self.tr(bk, self.pf[4 + ct % 2][:, 0:128],
                             sl[:, ct * 128:(ct + 1) * 128], sb["identf"][:],
                             wait=[c_ld, self.c_consts1])
                cv = self.cp("vector", sb[f"x{ct}"][:, tt * 128:(tt + 1) * 128],
                             self.pf[4 + ct % 2][:, 0:128], wait=[cm], relkey=bk)
                if ct == NCT - 1:
                    self.setg(("xstg", tt % 2), cv)
        self.c_x = ("V", p.n["V"])
        wrow = self.init_stack.enter_context(self.nc.sbuf_tensor("wrow", [128, 2 * C], BF))
        wtr = self.init_stack.enter_context(self.nc.sbuf_tensor("wtr", [128, 256], BF))
        import os as _os
        _nwt = int(_os.environ.get("KWT", "4"))
        for nm in (("rW", "kW", "vW", "oW")[:_nwt] if "attn" in self.stages else ()):
            for r in range(NCT):
                slr = wrow[:, (r % 2) * C:(r % 2) * C + C]
                g = self.guard(("wrow", r % 2))
                c_ld = self.dma("D2", slr, D[nm][r * 128:(r + 1) * 128, :],
                                wait=([g] if g else []) + [self.c_consts],
                                sem=f"dm{r % 2}")
                last = None
                for cb in range(NCT):
                    sw = cb % 2
                    cm = self.tr(("pbh", sw), self.pb[sw][:, 0:128],
                                 slr[:, cb * 128:(cb + 1) * 128], sb["identb"][:],
                                 wait=[c_ld])
                    g2 = self.guard(("wtr", sw))
                    cv = self.cp("vector", wtr[:, sw * 128:sw * 128 + 128],
                                 self.pb[sw][:, 0:128],
                                 wait=[cm] + ([g2] if g2 else []),
                                 relkey=("pbh", sw))
                    c_st = self.dma("D1", self.scr[nm + "T"][cb, r, :, :],
                                    wtr[:, sw * 128:sw * 128 + 128], wait=[cv],
                                    sem=f"ds{sw}")
                    self.setg(("wtr", sw), c_st)
                    last = cm
                self.setg(("wrow", r % 2), last)
        self.c_wt = [(s, p.n[s]) for s in ("ds0", "ds1") if s in p.n]
        self.init_stack.close()
        self.entry_wait = self.c_wt + [("T", p.n["T"])]

    def emit_ln(self, xln_tile_fn, t0, t1, upd, sc):
        p, sb = self.p, self.sb
        xbf, xsq, t1f, rows = sc["xbf"], sc["xsq"], sc["t1f"], sc["rows"]
        mu, esq, var, rstd, mr = (rows[0:1, i * 512:(i + 1) * 512] for i in range(5))
        ew_s = [("V", p.n["V"])]
        for b0 in range(t0, t1, 512):
            sl = slice(b0, b0 + 512)
            cm2 = cm3 = None
            for ct in range(NCT):
                s0 = ct % 2
                gA = self.guard(("lnxbf", s0))
                gB = self.guard(("lnxsq", s0))
                cv = self.cp("vector", xbf[:, s0 * 512:(s0 + 1) * 512],
                             sb[f"x{ct}"][:, sl], wait=[gA] if gA else [])
                cs = self.act(xsq[:, s0 * 512:(s0 + 1) * 512],
                              sb[f"x{ct}"][:, sl], AF.Square,
                              wait=ew_s + ([gB] if gB else []))
                ew_s = []
                cm2 = self.mm(("pf", 2) if ct == 0 else None, self.pf[2][0:1, :],
                              sb["onescol"][:], xbf[:, s0 * 512:(s0 + 1) * 512],
                              ct == 0, ct == NCT - 1,
                              wait=[cv] + ([self.c_consts] if ct == 0 else []))
                self.setg(("lnxbf", s0), cm2)
                cm3 = self.mm(("pf", 3) if ct == 0 else None, self.pf[3][0:1, :],
                              sb["onescol"][:], xsq[:, s0 * 512:(s0 + 1) * 512],
                              ct == 0, ct == NCT - 1, wait=[cs])
                self.setg(("lnxsq", s0), cm3)
            self.ts("vector", mu, self.pf[2][0:1, :], 1.0 / 1024, 0.0, ALU.mult,
                    ALU.add, wait=[cm2], relkey=("pf", 2))
            self.ts("vector", esq, self.pf[3][0:1, :], 1.0 / 1024, 0.0, ALU.mult,
                    ALU.add, wait=[cm3], relkey=("pf", 3))
            self.tt("vector", var, mu, mu, ALU.mult)
            c4 = self.tt("vector", var, esq, var, ALU.subtract)
            self.act(rstd, var, AF.Ln, bias=self.col("c_eps1")[0:1, :], wait=[c4])
            c6 = self.act(rstd, rstd, AF.Exp, scale=-0.5)
            c7 = self.tt("vector", mr, mu, rstd, ALU.mult, wait=[c6])
            cb1 = self.bcrow(("pf", 4), self.pf[4][:, :], rstd, wait=[c7])
            cb2 = self.bcrow(("pf", 5), self.pf[5][:, :], mr)
            lastv = None
            for ct in range(NCT):
                self.tt("vector", t1f[:], sb[f"x{ct}"][:, sl], self.pf[4][:, :],
                        ALU.mult, wait=[cb2] if ct == 0 else [])
                self.tt("vector", t1f[:], t1f[:], self.pf[5][:, :], ALU.subtract)
                ew = self.entry_wait
                self.entry_wait = []
                lastv = self.cp("vector", xln_tile_fn(ct)[:, sl], t1f[:], wait=ew)
                if upd:
                    lastv = self.stt("vector", sb[f"x{ct}"][:, sl], t1f[:], 0.5,
                                     sb[f"x{ct}"][:, sl], ALU.mult, ALU.add)
            self.setg(("pf", 4), lastv)
            self.setg(("pf", 5), lastv)
        return ("V", p.n["V"])

    def emit_ln_seg(self, xlnS, t0, sc, W):
        p, sb = self.p, self.sb
        xbf, xsq, t1f, rows = sc["xbf"], sc["xsq"], sc["t1f"], sc["rows"]
        mu, esq, var, rstd, mr = (rows[0:1, i * W:(i + 1) * W] for i in range(5))
        sl = slice(t0, t0 + W)
        ew_s = [("V", p.n["V"])]
        cm2 = cm3 = None
        for ct in range(NCT):
            q0 = ct % 2
            gA = self.guard(("lnxbf", q0))
            gB = self.guard(("lnxsq", q0))
            cv = self.cp("vector", xbf[:, q0 * W:(q0 + 1) * W],
                         sb[f"x{ct}"][:, sl], wait=[gA] if gA else [])
            cs = self.act(xsq[:, q0 * W:(q0 + 1) * W],
                          sb[f"x{ct}"][:, sl], AF.Square,
                          wait=ew_s + ([gB] if gB else []))
            ew_s = []
            cm2 = self.mm(("pf", 2) if ct == 0 else None, self.pf[2][0:1, 0:W],
                          sb["onescol"][:], xbf[:, q0 * W:(q0 + 1) * W],
                          ct == 0, ct == NCT - 1, wait=[cv])
            self.setg(("lnxbf", q0), cm2)
            cm3 = self.mm(("pf", 3) if ct == 0 else None, self.pf[3][0:1, 0:W],
                          sb["onescol"][:], xsq[:, q0 * W:(q0 + 1) * W],
                          ct == 0, ct == NCT - 1, wait=[cs])
            self.setg(("lnxsq", q0), cm3)
        self.ts("vector", mu, self.pf[2][0:1, 0:W], 1.0 / 1024, 0.0, ALU.mult,
                ALU.add, wait=[cm2], relkey=("pf", 2))
        self.ts("vector", esq, self.pf[3][0:1, 0:W], 1.0 / 1024, 0.0, ALU.mult,
                ALU.add, wait=[cm3], relkey=("pf", 3))
        self.tt("vector", var, mu, mu, ALU.mult)
        c4 = self.tt("vector", var, esq, var, ALU.subtract)
        self.act(rstd, var, AF.Ln, bias=self.col("c_eps1")[0:1, :], wait=[c4])
        c6 = self.act(rstd, rstd, AF.Exp, scale=-0.5)
        c7 = self.tt("vector", mr, mu, rstd, ALU.mult, wait=[c6])
        cb1 = self.bcrow(("pf", 4), self.pf[4][:, 0:W], rstd, wait=[c7])
        cb2 = self.bcrow(("pf", 5), self.pf[5][:, 0:W], mr)
        lastv = None
        for ct in range(NCT):
            self.tt("vector", t1f[:], sb[f"x{ct}"][:, sl], self.pf[4][:, 0:W],
                    ALU.mult, wait=[cb2] if ct == 0 else [])
            self.tt("vector", t1f[:], t1f[:], self.pf[5][:, 0:W], ALU.subtract)
            lastv = self.cp("vector", xlnS[:, ct * W:(ct + 1) * W], t1f[:])
        self.setg(("pf", 4), lastv)
        self.setg(("pf", 5), lastv)
        return ("V", p.n["V"])

    def ffn_phase(self, tag):
        nc, p, sb, D = self.nc, self.p, self.sb, self.D
        self.fence()
        with ExitStack() as st:
            E = st.enter_context
            xln = E(nc.sbuf_tensor(f"{tag}_xln", [128, NCT * T_SEQ], BF))
            comb = E(nc.sbuf_tensor(f"{tag}_comb", [128, 16 * 8], BF))
            combT = E(nc.sbuf_tensor(f"{tag}_combT", [8, T_SEQ], BF))
            stB = ExitStack()
            EB = stB.enter_context
            Q = EB(nc.sbuf_tensor(f"{tag}_Q", [128, 4 * T_SEQ], BF))
            K = EB(nc.sbuf_tensor(f"{tag}_K", [128, 4 * T_SEQ], BF))
            VT = EB(nc.sbuf_tensor(f"{tag}_VT", [128, 16 * 528], BF))
            owf = EB(nc.sbuf_tensor(f"{tag}_owf", [128, 4 * 1024], BF))
            shbc = EB(nc.sbuf_tensor(f"{tag}_shbc", [128, 16], F32))
            stA = ExitStack()
            EA = stA.enter_context
            sc = dict(
                xbf=EA(nc.sbuf_tensor(f"{tag}_xbf", [128, 2 * 512], BF)),
                xsq=EA(nc.sbuf_tensor(f"{tag}_xsq", [128, 2 * 512], BF)),
                t1f=EA(nc.sbuf_tensor(f"{tag}_t1f", [128, 512], F32)),
                rows=EA(nc.sbuf_tensor(f"{tag}_lnr", [1, 5 * 512], F32)),
            )
            qkw = EA(nc.sbuf_tensor(f"{tag}_qkw", [128, NCT * 512], BF))
            kvw = EA(nc.sbuf_tensor(f"{tag}_kvw", [128, NCT * 1024], BF))
            xlt = lambda ct: xln[:, ct * T_SEQ:(ct + 1) * T_SEQ]

            dqs = f"dq_{tag}"
            for ct in range(NCT):
                self.dma("D2", qkw[:, ct * 512:(ct + 1) * 512],
                         D["q_w_sel"][ct * 128:(ct + 1) * 128, :], sem=dqs)
                self.dma("D2", kvw[:, ct * 1024:ct * 1024 + 512],
                         D["k_w"][ct * 128:(ct + 1) * 128, :], sem=dqs)
                self.dma("D2", kvw[:, ct * 1024 + 512:ct * 1024 + 1024],
                         D["v_w"][ct * 128:(ct + 1) * 128, :], sem=dqs)
            for vt in range(4):
                self.dma("D2", owf[:, vt * 1024:(vt + 1) * 1024],
                         D["o_w"][vt * 128:(vt + 1) * 128, :], sem=dqs)
            c_wq = (dqs, p.n[dqs])
            qk_lds = [c_wq] * NCT
            kv_lds = [(c_wq, c_wq)] * NCT
            ow_lds = [c_wq] * 4

            for stile in range(16):
                for h in range(16):
                    self.cp("vector",
                            VT[:, stile * 528 + h * 33 + 32: stile * 528 + h * 33 + 33],
                            sb["onescol"][:],
                            wait=[self.c_consts] if (stile == 0 and h == 0) else [])

            for half in range(2):
                c_ln = self.emit_ln(xlt, half * 1024, half * 1024 + 1024, True, sc)
                for qt in range(4):
                    for tb in range(half * 2, half * 2 + 2):
                        bk = ("pf", (qt * 2 + tb) % 2)
                        cm = None
                        for ct in range(NCT):
                            cm = self.mm(bk if ct == 0 else None,
                                         self.pf[(qt * 2 + tb) % 2][:, :],
                                         qkw[:, ct * 512 + qt * 128: ct * 512 + (qt + 1) * 128],
                                         xlt(ct)[:, tb * 512:(tb + 1) * 512],
                                         ct == 0, ct == NCT - 1,
                                         wait=[qk_lds[ct], c_ln] if ct == 0 else [])
                        self.act(Q[:, qt * T_SEQ + tb * 512: qt * T_SEQ + (tb + 1) * 512],
                                 self.pf[(qt * 2 + tb) % 2][:, :], AF.Identity,
                                 bias=self.col("q_b", qt), wait=[cm], relkey=bk)
                for kt in range(4):
                    for tb in range(half * 2, half * 2 + 2):
                        bk = ("pf", (kt * 2 + tb) % 2)
                        cm = None
                        for ct in range(NCT):
                            cm = self.mm(bk if ct == 0 else None,
                                         self.pf[(kt * 2 + tb) % 2][:, :],
                                         kvw[:, ct * 1024 + kt * 128: ct * 1024 + (kt + 1) * 128],
                                         xlt(ct)[:, tb * 512:(tb + 1) * 512],
                                         ct == 0, ct == NCT - 1,
                                         wait=[kv_lds[ct][0], c_ln] if ct == 0 else [])
                        self.act(K[:, kt * T_SEQ + tb * 512: kt * T_SEQ + (tb + 1) * 512],
                                 self.pf[(kt * 2 + tb) % 2][:, :], AF.Identity,
                                 bias=self.col("k_b", kt), wait=[cm], relkey=bk)
                for stile in range(half * 8, half * 8 + 8):
                    bk = ("pf", stile % 2)
                    cm = None
                    for ct in range(NCT):
                        cm = self.mm(bk if ct == 0 else None, self.pf[stile % 2][:, :],
                                     xlt(ct)[:, stile * 128:(stile + 1) * 128],
                                     kvw[:, ct * 1024 + 512:ct * 1024 + 1024],
                                     ct == 0, False,
                                     wait=[kv_lds[ct][1], c_ln] if ct == 0 else [])
                    cm = self.mm(None, self.pf[stile % 2][:, :],
                                 sb["onesrow"][0:1, 0:128], self.row("v_b"),
                                 False, True, wait=[self.c_consts])
                    self.cp("vector",
                            VT[:, stile * 528:(stile + 1) * 528]
                            .rearrange("p (h q) -> p h q", q=33)[:, :, 0:32],
                            self.pf[stile % 2][:, :].rearrange("p (h q) -> p h q", q=32),
                            wait=[cm], relkey=bk)
            if self.stop_after == "ln":
                stA.close()
                stB.close()
                self.fence()
                return
            c_q = ("S", p.n["S"])
            c_v = ("V", p.n["V"])
            self.tap(f"{tag}_xln", xln[:, 0:T_SEQ], [128, T_SEQ], BF, self.barrier())
            self.tap(f"{tag}_Q", Q[:, 0:T_SEQ], [128, T_SEQ], BF, [c_q])
            self.tap(f"{tag}_K", K[:, 0:T_SEQ], [128, T_SEQ], BF, [c_q])
            self.tap(f"{tag}_VT", VT[:, 0:528], [128, 528], BF, [c_v])

            if self.stop_after == "qkv":
                stA.close()
                stB.close()
                self.fence()
                return
            stA.close()
            self.fence()
            stC = ExitStack()
            EC = stC.enter_context
            aofm = EC(nc.sbuf_tensor(f"{tag}_ao", [128, 4 * T_SEQ], BF))
            dinv = EC(nc.sbuf_tensor(f"{tag}_dinv", [128, 512], BF))
            dinvbc = EC(nc.sbuf_tensor(f"{tag}_dinvbc", [128, 512], BF))
            esb = EC(nc.sbuf_tensor(f"{tag}_esb", [128, 2 * 512], BF))
            rt = EC(nc.sbuf_tensor(f"{tag}_rt", [128, 64], F32))
            qsqt = EC(nc.sbuf_tensor(f"{tag}_qsq", [128, 2 * 512], BF))
            mmr = EC(nc.sbuf_tensor(f"{tag}_mmr", [16, 12], F32))
            shrow = EC(nc.sbuf_tensor(f"{tag}_shrow", [1, 16], F32))
            for which, src in ((0, Q), (1, K)):
                for tb in range(4):
                    cm = None
                    for qt in range(4):
                        s0 = qt % 2
                        gA = self.guard(("qsq", s0))
                        c_sq = self.act(qsqt[:, s0 * 512:(s0 + 1) * 512],
                                        src[:, qt * T_SEQ + tb * 512:
                                            qt * T_SEQ + (tb + 1) * 512],
                                        AF.Square,
                                        wait=([c_q] if (which + tb + qt == 0) else [])
                                              + ([gA] if gA else []))
                        cm = self.mm(("pf", 5) if qt == 0 else None,
                                     self.pf[5][0:16, 0:512],
                                     sb["qpat"][:, qt * 16:(qt + 1) * 16],
                                     qsqt[:, s0 * 512:(s0 + 1) * 512],
                                     qt == 0, qt == 3, wait=[c_sq])
                        self.setg(("qsq", s0), cm)
                    self.red(mmr[0:16, which * 4 + tb:which * 4 + tb + 1],
                             self.pf[5][0:16, 0:512], ALU.max, wait=[cm],
                             relkey=("pf", 5))
            c_r1 = self.red(mmr[0:16, 8:9], mmr[0:16, 0:4], ALU.max)
            c_r2 = self.red(mmr[0:16, 9:10], mmr[0:16, 4:8], ALU.max)
            c_m = self.tt("vector", mmr[0:16, 2:3], mmr[0:16, 8:9], mmr[0:16, 9:10],
                          ALU.mult)
            c_l = self.act(mmr[0:16, 3:4], mmr[0:16, 2:3], AF.Ln, wait=[c_m])
            c_e = self.act(mmr[0:16, 3:4], mmr[0:16, 3:4], AF.Exp, scale=0.5)
            c_n = self.ts("vector", mmr[0:16, 3:4], mmr[0:16, 3:4], -RT32, 0.0,
                          ALU.mult, ALU.add, wait=[c_e])
            c_t = self.tr(("pf", 5), self.pf[5][0:1, 0:16], mmr[0:16, 3:4],
                          sb["identf"][0:16, 0:16], wait=[c_n])
            c_sr = self.cp("vector", shrow[0:1, 0:16], self.pf[5][0:1, 0:16],
                           wait=[c_t], relkey=("pf", 5))
            c_bc = self.bcrow(("pf", 5), self.pf[5][:, 0:16], shrow[0:1, 0:16],
                              wait=[c_sr])
            c_sh = self.cp("vector", shbc[:], self.pf[5][:, 0:16], wait=[c_bc],
                           relkey=("pf", 5))
            self.tap(f"{tag}_shbc", shbc[:], [128, 16], F32, [c_sh])
            self.tap(f"{tag}_mmr", mmr[:], [16, 12], F32, [c_sh])
            self.tap(f"{tag}_shrow", shrow[:], [1, 16], F32, [c_sh])
            self.tap(f"{tag}_dinv", dinv[:], [128, 512], BF, [("S", p.n["S"] + 400)])

            if self.stop_after == "shift":
                stC.close()
                stB.close()
                self.fence()
                return
            c_z = self.p.op("gpsimd", lambda g, a=dinv[:]: g.memzero(a), inc="G")
            for tb in range(4):
                for qt in range(4):
                    cz1 = self.zfill(("pf", 2), self.pf[2][:, :], esb[:, 0:512] if False
                                     else Q[:, 0:512], wait=[c_q, c_sh])
                    cz2 = self.zfill(("pf", 3), self.pf[3][:, :], Q[:, 0:512])
                    for hpp in range(2):
                        h_a, h_b = qt * 4 + 2 * hpp, qt * 4 + 2 * hpp + 1
                        ro_a, ro_b = (2 * hpp) * 32, (2 * hpp + 1) * 32
                        for stile in range(16):
                            cm_a = self.mm(("pf", 0), self.pf[0][:, :],
                                           K[ro_a:ro_a + 32,
                                             qt * T_SEQ + stile * 128:
                                             qt * T_SEQ + (stile + 1) * 128],
                                           Q[ro_a:ro_a + 32,
                                             qt * T_SEQ + tb * 512:
                                             qt * T_SEQ + (tb + 1) * 512],
                                           True, True, tp=(ro_a, 0))
                            cm_b = self.mm(("pf", 1), self.pf[1][:, :],
                                           K[ro_b:ro_b + 32,
                                             qt * T_SEQ + stile * 128:
                                             qt * T_SEQ + (stile + 1) * 128],
                                           Q[ro_b:ro_b + 32,
                                             qt * T_SEQ + tb * 512:
                                             qt * T_SEQ + (tb + 1) * 512],
                                           True, True, tp=(ro_b, 0))
                            g0, g1 = self.guard(("esb", 0)), self.guard(("esb", 1))
                            ce_a = self.act(esb[:, 0:512], self.pf[0][:, :], AF.Exp,
                                            scale=RT32, bias=shbc[:, h_a:h_a + 1],
                                            wait=[cm_a] + ([g0] if g0 else []),
                                            relkey=("pf", 0))
                            ce_b = self.act(esb[:, 512:1024], self.pf[1][:, :], AF.Exp,
                                            scale=RT32, bias=shbc[:, h_b:h_b + 1],
                                            wait=[cm_b] + ([g1] if g1 else []),
                                            relkey=("pf", 1))
                            first = (stile == 0) and (hpp == 0)
                            c_ao_a = self.mm(None, self.pf[2][ro_a:ro_a + 32, :],
                                             VT[:, stile * 528 + h_a * 33:
                                                stile * 528 + h_a * 33 + 32],
                                             esb[:, 0:512], False, False, tp=(0, ro_a),
                                             wait=[ce_a, c_v, cz1] if first else [ce_a])
                            c_dn_a = self.mm(None, self.pf[3][ro_a:ro_a + 1, :],
                                             VT[:, stile * 528 + h_a * 33 + 32:
                                                stile * 528 + h_a * 33 + 33],
                                             esb[:, 0:512], False, False, tp=(0, ro_a),
                                             wait=[cz2] if first else [])
                            self.setg(("esb", 0), c_dn_a)
                            c_ao_b = self.mm(None, self.pf[2][ro_b:ro_b + 32, :],
                                             VT[:, stile * 528 + h_b * 33:
                                                stile * 528 + h_b * 33 + 32],
                                             esb[:, 512:1024], False, False, tp=(0, ro_b),
                                             wait=[ce_b])
                            c_dn_b = self.mm(None, self.pf[3][ro_b:ro_b + 1, :],
                                             VT[:, stile * 528 + h_b * 33 + 32:
                                                stile * 528 + h_b * 33 + 33],
                                             esb[:, 512:1024], False, False, tp=(0, ro_b))
                            self.setg(("esb", 1), c_dn_b)
                    c_cl2 = self.mm(None, self.pf[2][:, :], sb["zerosq"][:],
                                    Q[:, 0:512], False, True)
                    c_cl3 = self.mm(None, self.pf[3][:, :], sb["zerosq"][:],
                                    Q[:, 0:512], False, True)
                    c_last_mm = c_cl3
                    cacts = []
                    for r in range(4):
                        c1 = self.act(dinv[r * 32:r * 32 + 1, :],
                                      self.pf[3][r * 32:r * 32 + 1, :], AF.Ln,
                                      wait=[c_last_mm, ("G", c_z[1])] if r == 0 else [])
                        c2 = self.act(dinv[r * 32:r * 32 + 1, :],
                                      dinv[r * 32:r * 32 + 1, :], AF.Exp, scale=-1.0)
                        cacts.append(c2)
                    self.setg(("pf", 3), cacts[-1])
                    cbc = self.mm(("pf", 4), self.pf[4][:, :], sb["dpat"][:],
                                  dinv[:, :], True, True, wait=[cacts[-1]])
                    cbv = self.cp("vector", dinvbc[:], self.pf[4][:, :],
                                  wait=[cbc], relkey=("pf", 4))
                    cev = self.tt("vector",
                                  aofm[:, qt * T_SEQ + tb * 512:
                                       qt * T_SEQ + (tb + 1) * 512],
                                  self.pf[2][:, :], dinvbc[:], ALU.mult,
                                  wait=[c_last_mm])
                    self.setg(("pf", 2), cev)
            c_ao = ("V", p.n["V"])
            self.tap(f"{tag}_ao", aofm[:, 0:T_SEQ], [128, T_SEQ], BF, [c_ao])

            if self.stop_after == "attnloop":
                stC.close()
                stB.close()
                self.fence()
                return
            mla = xln
            for cpt in range(NCT):
                for tb in range(4):
                    bk = ("pf", tb % 2)
                    cm = None
                    for vt in range(4):
                        cm = self.mm(bk if vt == 0 else None, self.pf[tb % 2][:, :],
                                     owf[:, vt * 1024 + cpt * 128:
                                         vt * 1024 + (cpt + 1) * 128],
                                     aofm[:, vt * T_SEQ + tb * 512:
                                          vt * T_SEQ + (tb + 1) * 512],
                                     vt == 0, vt == 3,
                                     wait=[ow_lds[vt], c_ao] if vt == 0 else [])
                    self.act(mla[:, cpt * T_SEQ + tb * 512: cpt * T_SEQ + (tb + 1) * 512],
                             self.pf[tb % 2][:, :], AF.Identity,
                             bias=self.col("o_b", cpt), wait=[cm], relkey=bk)
            c_mla = ("S", p.n["S"])
            self.tap(f"{tag}_mla", mla[:, 0:T_SEQ], [128, T_SEQ], BF, [c_mla])

            if self.stop_after == "mla":
                stC.close()
                stB.close()
                self.fence()
                return
            for tt in range(16):
                cm = None
                for ct in range(NCT):
                    cm = self.mm(("pf", 2) if ct == 0 else None, self.pf[2][0:128, 0:8],
                                 mla[:, ct * T_SEQ + tt * 128: ct * T_SEQ + (tt + 1) * 128],
                                 sb["routerw"][:, ct * 8:(ct + 1) * 8],
                                 ct == 0, False, wait=[c_mla] if ct == 0 else [])
                cm = self.mm(None, self.pf[2][0:128, 0:8], sb["onesrow"][0:1, 0:128],
                             self.row("router_b"), False, True)
                self.cp("vector", rt[:, 0:8], self.pf[2][0:128, 0:8], wait=[cm],
                        relkey=("pf", 2))
                self.red(rt[:, 8:9], rt[:, 0:8], ALU.max)
                c3 = self.ts("vector", rt[:, 9:10], rt[:, 8:9], -1.0, 0.0, ALU.mult,
                             ALU.add)
                c3b = self.act(rt[:, 16:24], rt[:, 0:8], AF.Exp, bias=rt[:, 9:10],
                               accum=rt[:, 10:11], wait=[c3])
                c4 = self.ts("vector", rt[:, 24:32], rt[:, 0:8], rt[:, 8:9], 0.0,
                             ALU.is_equal, ALU.add, wait=[c3b])
                self.stt("vector", rt[:, 32:40], rt[:, 24:32], -1e30, rt[:, 0:8],
                         ALU.mult, ALU.add)
                self.red(rt[:, 40:41], rt[:, 32:40], ALU.max)
                c7 = self.ts("vector", rt[:, 48:56], rt[:, 32:40], rt[:, 40:41], 0.0,
                             ALU.is_equal, ALU.add)
                c9 = self.act(rt[:, 42:43], rt[:, 40:41], AF.Exp, bias=rt[:, 9:10],
                              wait=[c7])
                c10 = self.ts("vector", rt[:, 43:44], rt[:, 10:11], 1e-8, 1.0,
                              ALU.mult, ALU.add, wait=[c9])
                self.tt("vector", rt[:, 43:44], rt[:, 43:44], rt[:, 42:43], ALU.add)
                self.recip(rt[:, 44:45], rt[:, 43:44])
                self.tt("vector", rt[:, 45:46], rt[:, 42:43], rt[:, 44:45], ALU.mult)
                self.ts("vector", rt[:, 24:32], rt[:, 24:32], rt[:, 44:45], 0.0,
                        ALU.mult, ALU.add)
                self.ts("vector", rt[:, 48:56], rt[:, 48:56], rt[:, 45:46], 0.0,
                        ALU.mult, ALU.add)
                self.tt("vector", comb[:, tt * 8:(tt + 1) * 8], rt[:, 24:32],
                        rt[:, 48:56], ALU.add)
            c_comb = ("V", p.n["V"])
            self.tap(f"{tag}_comb", comb[:], [128, 128], BF, [c_comb])
            for tt in range(16):
                cm = self.tr(("pbh", 1), self.pb[1][0:8, 0:128],
                             comb[:, tt * 8:(tt + 1) * 8], sb["identb"][:],
                             wait=[c_comb] if tt == 0 else [])
                self.cp("vector", combT[0:8, tt * 128:(tt + 1) * 128],
                        self.pb[1][0:8, 0:128], wait=[cm], relkey=("pbh", 1))
            c_combT = ("V", p.n["V"])

            if self.stop_after == "router":
                stC.close()
                stB.close()
                self.fence()
                return
            stC.close()
            stB.close()
            self.fence()
            h1b = E(nc.sbuf_tensor(f"{tag}_h1", [128, 16 * 1024], BF))
            wstr = E(nc.sbuf_tensor(f"{tag}_wstr", [128, 3 * 4096], BF))
            cbbc = E(nc.sbuf_tensor(f"{tag}_cbbc", [128, 2 * 512], BF))
            etmp = E(nc.sbuf_tensor(f"{tag}_etmp", [128, 512], BF))
            gslot = [0]

            def gr_dma(dram_ap, gwait=()):
                s0 = gslot[0] % 3
                gslot[0] += 1
                dst = wstr[:, s0 * 4096:(s0 + 1) * 4096]
                g = self.guard(("wstr", s0))
                c = self.dma("D2", dst, dram_ap, wait=([g] if g else []) + list(gwait),
                             sem=f"dw{s0}_{tag}")
                return s0, c

            for tblk in range(2):
                for e in range(E_EXP):
                    for th in range(2):
                        tb = tblk * 2 + th
                        cm = self.mm(("pf", 4 + th), self.pf[4 + th][:, :],
                                     sb["esel"][0:8, e * 128:(e + 1) * 128],
                                     combT[0:8, tb * 512:(tb + 1) * 512], True, True,
                                     wait=[c_combT] if (tblk == 0 and e == 0) else [])
                        self.ts("vector", cbbc[:, th * 512:(th + 1) * 512],
                                self.pf[4 + th][:, :], 0.15, 0.0, ALU.mult, ALU.add,
                                wait=[cm], relkey=("pf", 4 + th))
                    c_cb = ("V", p.n["V"])
                    for fq in range(4):
                        s0, c_ld = gr_dma(
                            D["w1"][e].rearrange("(a p) f -> p a f", p=128)
                            [:, :, fq * 512:(fq + 1) * 512],
                            gwait=[c_combT] if (tblk == 0 and e == 0 and fq == 0) else ())
                        lastmm = None
                        for fi in range(4):
                            ft = fq * 4 + fi
                            for th in range(2):
                                tb = tblk * 2 + th
                                bki = (ft * 2 + th) % 2
                                cm = None
                                for ct in range(NCT):
                                    cm = self.mm(
                                        ("pf", bki) if ct == 0 else None,
                                        self.pf[bki][:, :],
                                        wstr[:, s0 * 4096 + ct * 512 + fi * 128:
                                             s0 * 4096 + ct * 512 + (fi + 1) * 128],
                                        mla[:, ct * T_SEQ + tb * 512:
                                            ct * T_SEQ + (tb + 1) * 512],
                                        ct == 0, ct == NCT - 1,
                                        wait=[c_ld, c_mla] if ct == 0 else [])
                                lastmm = cm
                                self.act(
                                    h1b[:, ft * 1024 + th * 512:
                                        ft * 1024 + (th + 1) * 512],
                                    self.pf[bki][:, :], AF.Gelu,
                                    bias=self.col(f"b1_{e}", ft), wait=[cm],
                                    relkey=("pf", bki))
                        self.setg(("wstr", s0), lastmm)
                    c_h1 = ("S", p.n["S"])
                    for fh in range(2):
                        for ch in range(2):
                            s0, c_ld = gr_dma(
                                D["w2"][e].rearrange("(a p) c -> p a c", p=128)
                                [:, fh * 8:(fh + 1) * 8, ch * 512:(ch + 1) * 512])
                            lastmm = None
                            for th in range(2):
                                tb = tblk * 2 + th
                                for cq in range(4):
                                    cp_ = ch * 4 + cq
                                    bki = 2 + (th * 4 + cq) % 2
                                    cm = None
                                    for fi in range(8):
                                        ft = fh * 8 + fi
                                        cm = self.mm(
                                            ("pf", bki) if fi == 0 else None,
                                            self.pf[bki][:, :],
                                            wstr[:, s0 * 4096 + fi * 512 + cq * 128:
                                                 s0 * 4096 + fi * 512 + (cq + 1) * 128],
                                            h1b[:, ft * 1024 + th * 512:
                                                ft * 1024 + (th + 1) * 512],
                                            fi == 0, fi == 7,
                                            wait=[c_ld, c_h1] if fi == 0 else [])
                                    lastmm = cm
                                    if fh == 0:
                                        cs = self.ts("vector", etmp[:],
                                                     self.pf[bki][:, :], 1.0, 0.0,
                                                     ALU.mult, ALU.add,
                                                     wait=[cm], relkey=("pf", bki))
                                    else:
                                        cs = self.ts("vector", etmp[:],
                                                     self.pf[bki][:, :],
                                                     self.col(f"b2_{e}", cp_), 0.0,
                                                     ALU.add, ALU.add,
                                                     wait=[cm], relkey=("pf", bki))
                                    self.tt("vector", etmp[:], etmp[:],
                                            cbbc[:, th * 512:(th + 1) * 512],
                                            ALU.mult,
                                            wait=[c_cb])
                                    self.tt("vector",
                                            sb[f"x{cp_}"][:, tb * 512:(tb + 1) * 512],
                                            sb[f"x{cp_}"][:, tb * 512:(tb + 1) * 512],
                                            etmp[:], ALU.add)
                            self.setg(("wstr", s0), lastmm)
            self.tap(f"{tag}_x", sb["x0"][:], [128, T_SEQ], F32, [("V", p.n["V"])])
        self.fence()

    def attn_phase(self):
        nc, p, sb, D = self.nc, self.p, self.sb, self.D
        SEGA = 256
        NSEGA = T_SEQ // SEGA
        NSTL = SEGA // 128
        NCH = SEGA // 32
        self.fence()
        with ExitStack() as st:
            E = st.enter_context
            for nm, shp in (("cm_a", [128, SEGA]), ("gnpat", [128, 128])):
                sb[nm] = E(nc.sbuf_tensor("sb_" + nm, shp, BF))
            for nm, shp in (("mask_tu", [128, 256]), ("mask_sl", [128, 128])):
                sb[nm] = E(nc.sbuf_tensor("sb_" + nm, shp, F32))
            sb["gnpatTf"] = E(nc.sbuf_tensor("sb_gnpatTf", [16, 1024], F32))
            self.dma("D2", sb["cm_a"][:], D["cm_a_src"][:, 0:SEGA], sem="dlr")
            self.dma("D2", sb["gnpat"][:], D["gnpat_src"], sem="dlr")
            for nm in ("mask_tu", "mask_sl", "gnpatTf"):
                srcn = "gnpatT_src" if nm == "gnpatTf" else nm + "_src"
                self.dma("D1", sb[nm][:], D[srcn], sem="dlr2")
            ncols = E(nc.sbuf_tensor("att_ncols", [128, 8], F32))
            _i0 = self.colidx["w_miu"]
            c_neg = self.ts("vector", ncols[:, 0:8], self.sb["cols"][:, _i0:_i0 + 8],
                            -1.0, 0.0,
                            ALU.mult, ALU.add)
            SW = NCT * SEGA
            rS = E(nc.sbuf_tensor("a_rS", [128, SW], BF))
            vS = E(nc.sbuf_tensor("a_vS", [128, SW], BF))
            kapS = E(nc.sbuf_tensor("a_kapS", [128, SW], BF))
            kkS = E(nc.sbuf_tensor("a_kkS", [128, SW], BF))
            bS = E(nc.sbuf_tensor("a_bS", [128, SW], BF))
            gS = E(nc.sbuf_tensor("a_gS", [128, SW], BF))
            wSf = E(nc.sbuf_tensor("a_wSf", [128, SW], F32))
            xdP = E(nc.sbuf_tensor("a_xdP", [128, NCT * SEGA], BF))
            mixA = E(nc.sbuf_tensor("a_mixA", [128, SEGA], BF))
            mixB = E(nc.sbuf_tensor("a_mixB", [64, SEGA], BF))
            bsum = E(nc.sbuf_tensor("a_bsum", [16, SEGA], F32))
            gnr = E(nc.sbuf_tensor("a_gnr", [16, 3 * SEGA], F32))
            stS = E(nc.sbuf_tensor("a_stS", [128, NCT * 64], BF))
            PLs = E(nc.sbuf_tensor("a_PLs", [128, NCT * NCH], F32))
            prevcol = E(nc.sbuf_tensor("a_prev", [128, NCT], BF))
            wstrip = E(nc.sbuf_tensor("a_wstrip", [128, 2 * 1024], BF))
            t64 = E(nc.sbuf_tensor("a_t64", [128, SEGA], BF))
            ag = E(nc.sbuf_tensor("a_ag", [128, SEGA], BF))
            miu = E(nc.sbuf_tensor("a_miu", [128, SEGA], BF))
            t1g = E(nc.sbuf_tensor("a_t1g", [128, SEGA], BF))
            l1t = E(nc.sbuf_tensor("a_l1t", [128, SEGA], F32))
            sb1f = E(nc.sbuf_tensor("a_sb1f", [128, SEGA], F32))
            yS = kkS

            czs = self.p.op("gpsimd", lambda g, a=stS[:]: g.memzero(a), inc="G")
            czp = self.p.op("gpsimd", lambda g, a=prevcol[:]: g.memzero(a), inc="G")
            self._st_init = False

            def strip_dma(mat, rb, slot, wait=()):
                ap = self.scr[mat][:, rb].transpose([1, 0, 2])
                g = self.guard(("astrip", slot))
                dst = wstrip[:, slot * 1024:(slot + 1) * 1024].rearrange(
                    "p (a f) -> p a f", f=128)
                return self.dma("D1", dst, ap,
                                wait=([g] if g else []) + list(wait),
                                sem=f"da{slot}")

            for seg in range(NSEGA):
                s0, s1g = seg * SEGA, (seg + 1) * SEGA
                segsl = lambda ct: slice(ct * SEGA, (ct + 1) * SEGA)
                stD = ExitStack()
                ED = stD.enter_context
                xlnS = ED(nc.sbuf_tensor(f"a_xlnS{seg}", [128, SW], BF))
                dxS = ED(nc.sbuf_tensor(f"a_dxS{seg}", [128, SW], BF))
                sc = dict(
                    xbf=ED(nc.sbuf_tensor(f"a_xbf{seg}", [128, 2 * SEGA], BF)),
                    xsq=ED(nc.sbuf_tensor(f"a_xsq{seg}", [128, 2 * SEGA], BF)),
                    t1f=ED(nc.sbuf_tensor(f"a_t1f{seg}", [128, SEGA], F32)),
                    rows=ED(nc.sbuf_tensor(f"a_lnr{seg}", [1, 5 * SEGA], F32)),
                )
                lr = {}
                for nm, shp, dsrc in (("xaw", [128, 8 * 192], "x_A"),
                                      ("kaw", [128, 8 * 64], "k_A"),
                                      ("waw", [128, 8 * 64], "w_A"),
                                      ("aaw", [128, 8 * 64], "a_A"),
                                      ("gaw", [128, 8 * 128], "g_A")):
                    lr[nm] = ED(nc.sbuf_tensor(f"{nm}{seg}", shp, BF))
                    w = shp[1] // 8
                    for ct in range(NCT):
                        self.dma("D2", lr[nm][:, ct * w:(ct + 1) * w],
                                 D[dsrc][ct * 128:(ct + 1) * 128, :], sem="dlr")
                for nm, shp, dsrc in (("xbA", [128, 1024], None),
                                      ("xbB", [64, 1024], None),
                                      ("kbw", [64, 1024], "k_B"),
                                      ("wbw", [64, 1024], "w_B"),
                                      ("abw", [64, 1024], "a_B"),
                                      ("gbw", [128, 1024], "g_B")):
                    lr[nm] = ED(nc.sbuf_tensor(f"{nm}{seg}", shp, BF))
                    if dsrc is not None:
                        self.dma("D2", lr[nm][:], D[dsrc], sem="dlr")
                self.dma("D2", lr["xbA"][:], D["x_Bp"][0:128, :], sem="dlr")
                self.dma("D2", lr["xbB"][:], D["x_Bp"][128:192, :], sem="dlr")
                c_lr = ("dlr", p.n["dlr"])

                c_ln = self.emit_ln_seg(xlnS, s0, sc, SEGA)
                lastdx = None
                for ct in range(NCT):
                    self.tt("vector", dxS[:, ct * SEGA + 1: ct * SEGA + SEGA],
                            xlnS[:, ct * SEGA: ct * SEGA + SEGA - 1],
                            xlnS[:, ct * SEGA + 1: ct * SEGA + SEGA], ALU.subtract)
                    lastdx = self.tt("vector", dxS[:, ct * SEGA: ct * SEGA + 1],
                                     prevcol[:, ct:ct + 1],
                                     xlnS[:, ct * SEGA: ct * SEGA + 1], ALU.subtract,
                                     wait=[("G", czp[1])] if seg == 0 else [])
                    self.cp("vector", prevcol[:, ct:ct + 1],
                            xlnS[:, ct * SEGA + SEGA - 1: ct * SEGA + SEGA])
                segtw = [("T", p.n["T"])]
                for ct in range(NCT):
                    self.ts("vector", t1g[:], dxS[:, segsl(ct)], self.col("x_miu", ct),
                            0.0, ALU.mult, ALU.add, wait=segtw)
                    segtw = []
                    self.tt("vector", xdP[:, segsl(ct)], t1g[:],
                            xlnS[:, segsl(ct)], ALU.add)
                c_xm = ("V", p.n["V"])
                cmA = cmB = None
                for ct in range(NCT):
                    cmA = self.mm(("pf", 0) if ct == 0 else None, self.pf[0][:, 0:SEGA],
                                  lr["xaw"][:, ct * 192: ct * 192 + 128],
                                  xdP[:, segsl(ct)], ct == 0, ct == 7,
                                  wait=[c_lr, c_xm] if ct == 0 else [])
                for ct in range(NCT):
                    cmB = self.mm(("pf", 1) if ct == 0 else None, self.pf[1][0:64, 0:SEGA],
                                  lr["xaw"][:, ct * 192 + 128: ct * 192 + 192],
                                  xdP[:, segsl(ct)], ct == 0, ct == 7)
                self.act(mixA[:], self.pf[0][:, 0:SEGA], AF.Tanh, wait=[cmA],
                         relkey=("pf", 0))
                c_mix = self.act(mixB[0:64, :], self.pf[1][0:64, 0:SEGA], AF.Tanh,
                                 wait=[cmB], relkey=("pf", 1))

                def xd_build(g6, cdep):
                    for ct in range(NCT):
                        bk = ct % 2
                        if g6 < 4:
                            lhs = lr["xbA"][g6 * 32:(g6 + 1) * 32,
                                            ct * 128:(ct + 1) * 128]
                            rhs = mixA[g6 * 32:(g6 + 1) * 32, :]
                            tp = (g6 * 32, 0)
                        else:
                            lhs = lr["xbB"][(g6 - 4) * 32:(g6 - 3) * 32,
                                            ct * 128:(ct + 1) * 128]
                            rhs = mixB[(g6 - 4) * 32:(g6 - 3) * 32, :]
                            tp = ((g6 - 4) * 32, 0)
                        cm = self.mm(("pf", 2 + bk), self.pf[2 + bk][:, 0:SEGA],
                                     lhs, rhs, True, True, tp=tp,
                                     wait=[cdep] if ct == 0 else [])
                        cs = self.act(miu[:], self.pf[2 + bk][:, 0:SEGA], AF.Identity,
                                      bias=self.col(f"lam{g6}", ct), wait=[cm],
                                      relkey=("pf", 2 + bk))
                        cg = self.tt("gpsimd", t1g[:], dxS[:, segsl(ct)], miu[:],
                                     ALU.mult, wait=[cs])
                        self.tt("vector", xdP[:, segsl(ct)],
                                t1g[:], xlnS[:, segsl(ct)], ALU.add, wait=[cg])
                    return ("V", p.n["V"])

                def cxc_proj(mat, cdep=None):
                    for rb in range(NCT):
                        slot = rb % 2
                        c_st = strip_dma(mat, rb, slot)
                        bk = rb % 2
                        cm = None
                        for ct in range(NCT):
                            w0 = [c_st] if ct == 0 else []
                            if ct == 0 and rb == 0 and cdep is not None:
                                w0.append(cdep)
                            cm = self.mm(("pf", bk) if ct == 0 else None,
                                         self.pf[bk][:, 0:SEGA],
                                         wstrip[:, slot * 1024 + ct * 128:
                                                slot * 1024 + (ct + 1) * 128],
                                         xdP[:, segsl(ct)], ct == 0, False,
                                         wait=w0)
                        yield rb, bk, cm
                        self.setg(("astrip", slot), self.p.guards.get(("pf", bk)))

                c_xd = xd_build(2, c_mix)
                cm = None
                for ct in range(NCT):
                    cm = self.mm(("pf", 4) if ct == 0 else None, self.pf[4][0:64, 0:SEGA],
                                 lr["kaw"][:, ct * 64:(ct + 1) * 64],
                                 xdP[:, segsl(ct)], ct == 0, ct == 7,
                                 wait=[c_xd] if ct == 0 else [])
                c_t64 = self.act(t64[0:64, :], self.pf[4][0:64, 0:SEGA], AF.Tanh,
                                 wait=[cm], relkey=("pf", 4))
                for rb, bk, cm in cxc_proj("kWT"):
                    cm2 = self.mm(None, self.pf[bk][:, 0:SEGA],
                                  lr["kbw"][0:64, rb * 128:(rb + 1) * 128],
                                  t64[0:64, :], False, True, wait=[c_t64])
                    self.cp("vector", kapS[:, segsl(rb)], self.pf[bk][:, 0:SEGA],
                            wait=[cm2], relkey=("pf", bk))
                c_kk0 = ("V", p.n["V"])
                for ct in range(NCT):
                    q0 = ct % 2
                    gq = self.guard(("ksq", q0))
                    dst = t1g[:] if q0 == 0 else miu[:]
                    self.act(dst, kapS[:, segsl(ct)], AF.Square,
                             wait=([c_kk0] if ct == 0 else []) + ([gq] if gq else []))
                    cmk = self.mm(("pf", 4) if ct == 0 else None, self.pf[4][0:1, 0:SEGA],
                                  sb["onescol"][:], dst, ct == 0, ct == 7,
                                  wait=[("S", p.n["S"])])
                    self.setg(("ksq", q0), cmk)
                cr = sc["rows"]
                self.act(cr[0:1, 0:SEGA], self.pf[4][0:1, 0:SEGA], AF.Ln,
                         wait=[("T", p.n["T"])], relkey=("pf", 4))
                self.act(cr[0:1, 0:SEGA], cr[0:1, 0:SEGA], AF.Exp, scale=0.5)
                c_nr = self.ts("vector", cr[0:1, SEGA:2 * SEGA], cr[0:1, 0:SEGA],
                               1e-12, 0.0, ALU.max, ALU.add,
                               wait=[("S", p.n["S"])])
                c_rec = self.recip(cr[0:1, 2 * SEGA:3 * SEGA], cr[0:1, SEGA:2 * SEGA])
                c_bc = self.bcrow(("pf", 5), self.pf[5][:, 0:SEGA],
                                  cr[0:1, 2 * SEGA:3 * SEGA], wait=[c_rec])
                for ct in range(NCT):
                    self.tt("vector", kapS[:, segsl(ct)], kapS[:, segsl(ct)],
                            self.pf[5][:, 0:SEGA], ALU.mult,
                            wait=[c_bc] if ct == 0 else [])
                c_kap = ("V", p.n["V"])
                self.setg(("pf", 5), c_kap)

                c_xd = xd_build(1, c_kap)
                cm = None
                for ct in range(NCT):
                    cm = self.mm(("pf", 4) if ct == 0 else None, self.pf[4][0:64, 0:SEGA],
                                 lr["waw"][:, ct * 64:(ct + 1) * 64],
                                 xdP[:, segsl(ct)], ct == 0, ct == 7,
                                 wait=[c_xd] if ct == 0 else [])
                c_t64 = self.act(t64[0:64, :], self.pf[4][0:64, 0:SEGA], AF.Tanh,
                                 wait=[cm], relkey=("pf", 4))
                if seg == 7:
                    self.tap("w_t64", t64[0:64, :], [64, SEGA], BF, [c_t64])
                for rb in range(NCT):
                    bk = 2 + rb % 2
                    cm2 = self.mm(("pf", bk), self.pf[bk][:, 0:SEGA],
                                  lr["wbw"][0:64, rb * 128:(rb + 1) * 128],
                                  t64[0:64, :], True, True, wait=[c_t64])
                    wtap = []
                    if seg == 7 and rb == 1 and "w_pf1" in self.taps:
                        ctp = self.cp("vector", sc["t1f"][:], self.pf[bk][:, 0:SEGA],
                                      wait=[cm2])
                        self.tap("w_pf1", sc["t1f"][:], [128, SEGA], F32, [ctp])
                        wtap = [("dt", p.n["dt"])]
                    self.act(l1t[:], self.pf[bk][:, 0:SEGA], AF.Exp, scale=-1.0,
                             bias=ncols[:, rb:rb + 1], wait=[cm2, c_neg] + wtap,
                             relkey=("pf", bk))
                    if seg == 7 and rb == 1:
                        self.tap("w_l1e", l1t[:], [128, SEGA], F32,
                                 [("S", p.n["S"])])
                    cl1 = self.act(l1t[:], l1t[:], AF.Ln, bias=self.col("c_one"))
                    self.ts("vector", wSf[:, segsl(rb)], l1t[:], -1.0, -0.5,
                            ALU.mult, ALU.add, wait=[cl1])
                    ce2 = self.act(t1g[:], l1t[:], AF.Exp, scale=-0.5,
                                   bias=self.col("c_mquart"))
                    self.tt("vector", kkS[:, segsl(rb)], kapS[:, segsl(rb)], t1g[:],
                            ALU.mult, wait=[ce2])
                c_w = ("V", p.n["V"])

                c_xd = xd_build(4, c_w)
                cm = None
                for ct in range(NCT):
                    cm = self.mm(("pf", 4) if ct == 0 else None, self.pf[4][0:64, 0:SEGA],
                                 lr["aaw"][:, ct * 64:(ct + 1) * 64],
                                 xdP[:, segsl(ct)], ct == 0, ct == 7,
                                 wait=[c_xd] if ct == 0 else [])
                c_t64 = self.act(t64[0:64, :], self.pf[4][0:64, 0:SEGA], AF.Identity,
                                 wait=[cm], relkey=("pf", 4))
                for rb in range(NCT):
                    bk = 2 + rb % 2
                    cm2 = self.mm(("pf", bk), self.pf[bk][:, 0:SEGA],
                                  lr["abw"][0:64, rb * 128:(rb + 1) * 128],
                                  t64[0:64, :], True, True, wait=[c_t64])
                    wtap = []
                    if seg == 7 and rb == 0 and "a_pf0" in self.taps:
                        ctp = self.cp("vector", sc["t1f"][:], self.pf[bk][:, 0:SEGA],
                                      wait=[cm2])
                        self.tap("a_pf0", sc["t1f"][:], [128, SEGA], F32, [ctp])
                        self.tap("a_t64", t64[0:64, :], [64, SEGA], BF, [ctp])
                        wtap = [("dt", p.n["dt"])]
                    ca = self.act(ag[:], self.pf[bk][:, 0:SEGA], AF.Sigmoid,
                                  bias=self.col("a_time", rb), wait=[cm2] + wtap,
                                  relkey=("pf", bk))
                    if seg == 7 and rb == 0:
                        self.tap("a_ag0", ag[:], [128, SEGA], BF, [ca])
                    self.stt("vector", bS[:, segsl(rb)], kapS[:, segsl(rb)], 2.0,
                             ag[:], ALU.mult, ALU.mult, wait=[ca])
                c_b = ("V", p.n["V"])

                c_xd = xd_build(5, c_b)
                cm = None
                for ct in range(NCT):
                    cm = self.mm(("pf", 4) if ct == 0 else None, self.pf[4][:, 0:SEGA],
                                 lr["gaw"][:, ct * 128:(ct + 1) * 128],
                                 xdP[:, segsl(ct)], ct == 0, ct == 7,
                                 wait=[c_xd] if ct == 0 else [])
                c_tg = self.act(t64[:], self.pf[4][:, 0:SEGA], AF.Tanh, wait=[cm],
                                relkey=("pf", 4))
                for rb in range(NCT):
                    bk = 2 + rb % 2
                    cm2 = self.mm(("pf", bk), self.pf[bk][:, 0:SEGA],
                                  lr["gbw"][:, rb * 128:(rb + 1) * 128],
                                  t64[:], True, True, wait=[c_tg])
                    self.cp("vector", gS[:, segsl(rb)], self.pf[bk][:, 0:SEGA],
                            wait=[cm2], relkey=("pf", bk))
                c_xd = xd_build(3, ("V", p.n["V"]))
                for rb, bk, cm in cxc_proj("vWT", c_xd):
                    cmv = self.mm(None, self.pf[bk][:, 0:SEGA], sb["zerosq"][:],
                                  xdP[:, 0:SEGA], False, True)
                    self.cp("vector", vS[:, segsl(rb)], self.pf[bk][:, 0:SEGA],
                            wait=[cmv], relkey=("pf", bk))
                c_xd = xd_build(0, ("V", p.n["V"]))
                for rb, bk, cm in cxc_proj("rWT", c_xd):
                    cmr = self.mm(None, self.pf[bk][:, 0:SEGA], sb["zerosq"][:],
                                  xdP[:, 0:SEGA], False, True)
                    self.cp("vector", rS[:, segsl(rb)], self.pf[bk][:, 0:SEGA],
                            wait=[cmr], relkey=("pf", bk))
                c_proj = ("V", p.n["V"])
                if seg == 0:
                    for nm_, tn_ in (("a_rS", rS), ("a_kapS", kapS), ("a_vS", vS),
                                     ("a_gS", gS), ("a_bS", bS), ("a_kkS", kkS)):
                        self.tap(nm_, tn_[:], [128, SW], BF, [c_proj])
                    self.tap("a_wSf", wSf[:], [128, SW], F32, [c_proj])

                cm = None
                for ct in range(NCT):
                    q0 = ct % 2
                    dst = t1g[:] if q0 == 0 else miu[:]
                    gq = self.guard(("bns", q0))
                    self.tt("vector", dst, rS[:, segsl(ct)], kapS[:, segsl(ct)],
                            ALU.mult,
                            wait=([c_proj] if ct == 0 else []) + ([gq] if gq else []))
                    self.ts("vector", dst, dst, self.col("xtime", ct), 0.0,
                            ALU.mult, ALU.add)
                    cm = self.mm(("pf", 4) if ct == 0 else None, self.pf[4][0:16, 0:SEGA],
                                 sb["gnpat"][:, ct * 16:(ct + 1) * 16], dst,
                                 ct == 0, ct == 7, wait=[("V", p.n["V"]), c_lr])
                    self.setg(("bns", q0), cm)
                self.cp("vector", bsum[0:16, :], self.pf[4][0:16, 0:SEGA], wait=[cm],
                        relkey=("pf", 4))
                stD.close()
                self.fence()

                stE = ExitStack()
                EE = stE.enter_context
                WWr = EE(nc.sbuf_tensor(f"a_WWr{seg}", [128, 512], BF))
                UUr = EE(nc.sbuf_tensor(f"a_UUr{seg}", [128, 512], BF))
                Wb = EE(nc.sbuf_tensor(f"a_Wb{seg}", [128, 256], BF))
                SQ = EE(nc.sbuf_tensor(f"a_SQ{seg}", [128, 14 * 128], BF))
                Xsb = EE(nc.sbuf_tensor(f"a_Xsb{seg}", [128, 256], BF))
                Vtok = EE(nc.sbuf_tensor(f"a_Vtok{seg}", [128, 128], BF))
                BKt = EE(nc.sbuf_tensor(f"a_BKt{seg}", [128, 256], BF))
                Gfm = EE(nc.sbuf_tensor(f"a_Gfm{seg}", [128, 128], BF))
                Ztok = EE(nc.sbuf_tensor(f"a_Ztok{seg}", [128, 128], BF))
                cwF = EE(nc.sbuf_tensor(f"a_cwF{seg}", [128, SEGA], F32))
                cwP = EE(nc.sbuf_tensor(f"a_cwP{seg}", [128, SEGA], F32))
                subt = EE(nc.sbuf_tensor(f"a_subt{seg}", [128, SEGA], F32))
                crefb = EE(nc.sbuf_tensor(f"a_crefb{seg}", [128, SEGA], F32))
                refc = EE(nc.sbuf_tensor(f"a_refc{seg}", [128, 2 * NCH], F32))
                fPrev = EE(nc.sbuf_tensor(f"a_fPrev{seg}", [128, SEGA], BF))
                fIncl = EE(nc.sbuf_tensor(f"a_fIncl{seg}", [128, SEGA], BF))
                fHatP = EE(nc.sbuf_tensor(f"a_fHatP{seg}", [128, SEGA], BF))
                fHatI = EE(nc.sbuf_tensor(f"a_fHatI{seg}", [128, SEGA], BF))
                fTil = EE(nc.sbuf_tensor(f"a_fTil{seg}", [128, SEGA], BF))
                fBar = EE(nc.sbuf_tensor(f"a_fBar{seg}", [128, SEGA], BF))
                oA = EE(nc.sbuf_tensor(f"a_oA{seg}", [128, SEGA], BF))
                oAh = EE(nc.sbuf_tensor(f"a_oAh{seg}", [128, SEGA], BF))
                oR = EE(nc.sbuf_tensor(f"a_oR{seg}", [128, SEGA], BF))
                oRh = EE(nc.sbuf_tensor(f"a_oRh{seg}", [128, SEGA], BF))
                oBt = EE(nc.sbuf_tensor(f"a_oBt{seg}", [128, SEGA], BF))
                oKt = EE(nc.sbuf_tensor(f"a_oKt{seg}", [128, SEGA], BF))
                oBb = EE(nc.sbuf_tensor(f"a_oBb{seg}", [128, SEGA], BF))
                oKb = EE(nc.sbuf_tensor(f"a_oKb{seg}", [128, SEGA], BF))

                for ct in range(NCT):
                    csl = segsl(ct)
                    prep_tw = ("T", p.n["T"])
                    c_sc = self.scan(cwF[:], sb["cm_a"][:], wSf[:, csl], 0.0,
                                     ALU.mult, ALU.add,
                                     wait=[c_proj, c_lr, prep_tw])
                    self.tt("vector", cwP[:], cwF[:], wSf[:, csl], ALU.subtract)
                    self.cp("vector", refc[:, 0:NCH],
                            cwF[:].rearrange("q (c w) -> q c w", w=32)[:, :, 15:16]
                            .squeeze(2))
                    self.cp("vector", refc[:, NCH:2 * NCH],
                            cwF[:].rearrange("q (c w) -> q c w", w=32)[:, :, 31:32]
                            .squeeze(2))
                    self.act(PLs[:, ct * NCH:(ct + 1) * NCH], refc[:, NCH:2 * NCH],
                             AF.Exp, wait=[("V", p.n["V"])])
                    cbd = self.cp("vector",
                                  crefb[:].rearrange("q (c w) -> q c w", w=32),
                                  refc[:, 0:NCH].rearrange("q (c w) -> q c w", w=1)
                                  .broadcast_to([128, NCH, 32]))
                    c1 = self.tt("vector", subt[:], cwP[:], crefb[:], ALU.subtract)
                    self.act(fHatP[:], subt[:], AF.Exp, bias=self.col("c_m11"),
                             wait=[c1])
                    c2 = self.tt("vector", subt[:], cwF[:], crefb[:], ALU.subtract,
                                 wait=[("S", p.n["S"])])
                    self.act(fHatI[:], subt[:], AF.Exp, bias=self.col("c_m11"),
                             wait=[c2])
                    c2b = self.act(fTil[:], subt[:], AF.Exp, scale=-1.0,
                                   bias=self.col("c_m11"))
                    cbd2 = self.cp("vector",
                                   crefb[:].rearrange("q (c w) -> q c w", w=32),
                                   refc[:, NCH:2 * NCH]
                                   .rearrange("q (c w) -> q c w", w=1)
                                   .broadcast_to([128, NCH, 32]), wait=[c2b])
                    c3 = self.tt("vector", subt[:], crefb[:], cwF[:], ALU.subtract)
                    self.act(fBar[:], subt[:], AF.Exp, wait=[c3])
                    self.act(fPrev[:], cwP[:], AF.Exp)
                    c_f = self.act(fIncl[:], cwF[:], AF.Exp)
                    self.stt("vector", oA[:], kapS[:, csl], -1.0, fPrev[:],
                             ALU.mult, ALU.mult, wait=[c_f, prep_tw])
                    self.stt("vector", oAh[:], kapS[:, csl], -1.0, fHatP[:],
                             ALU.mult, ALU.mult)
                    self.tt("gpsimd", oR[:], rS[:, csl], fIncl[:], ALU.mult)
                    self.tt("gpsimd", oRh[:], rS[:, csl], fHatI[:], ALU.mult)
                    self.tt("vector", oBt[:], bS[:, csl], fTil[:], ALU.mult)
                    self.tt("vector", oKt[:], kkS[:, csl], fTil[:], ALU.mult)
                    self.tt("vector", oBb[:], bS[:, csl], fBar[:], ALU.mult)
                    c_ops = self.tt("vector", oKb[:], kkS[:, csl], fBar[:], ALU.mult)
                    c_prep = [c_ops, ("G", p.n["G"]), ("S", p.n["S"])]

                    for stl in range(NSTL):
                        scol = slice(stl * 128, (stl + 1) * 128)
                        w0 = c_prep if stl == 0 else []
                        cts = []
                        for hh, po in ((0, 0), (1, 64)):
                            cts.append(self.tr(("pb", 0) if hh == 0 else None,
                                               self.pb[0][:, hh * 64:hh * 64 + 64],
                                               vS[po:po + 64, ct * SEGA + stl * 128:
                                                  ct * SEGA + (stl + 1) * 128],
                                               sb["identb"][po:po + 64, po:po + 64],
                                               wait=w0))
                        cv = self.cp("vector", Vtok[:], self.pb[0][:, 0:128],
                                     wait=[cts[-1]], relkey=("pb", 0))
                        fm = []
                        for hh, po in ((0, 0), (1, 64)):
                            fm.append(self.mm(("pf", 1) if hh == 0 else None,
                                              self.pf[1][:, hh * 256: hh * 256 + 128],
                                              oBt[po:po + 64, scol],
                                              oAh[po:po + 64, scol], True, True,
                                              tp=(po, 0), wait=w0))
                            fm.append(self.mm(None,
                                              self.pf[1][:, hh * 256 + 128: hh * 256 + 256],
                                              oBt[po:po + 64, scol],
                                              oRh[po:po + 64, scol], True, True,
                                              tp=(po, 0)))
                        self.tt("vector", WWr[:, 0:256], self.pf[1][:, 0:256],
                                sb["mask_tu"][:], ALU.mult,
                                wait=[fm[-1], ("dlr2", 48)])
                        cw2 = self.tt("vector", WWr[:, 256:512], self.pf[1][:, 256:512],
                                      sb["mask_tu"][:], ALU.mult, relkey=("pf", 1))
                        fu = []
                        for hh, po in ((0, 0), (1, 64)):
                            fu.append(self.mm(("pf", 0) if hh == 0 else None,
                                              self.pf[0][:, hh * 256: hh * 256 + 128],
                                              oKt[po:po + 64, scol],
                                              oAh[po:po + 64, scol], True, True,
                                              tp=(po, 0)))
                            fu.append(self.mm(None,
                                              self.pf[0][:, hh * 256 + 128: hh * 256 + 256],
                                              oKt[po:po + 64, scol],
                                              oRh[po:po + 64, scol], True, True,
                                              tp=(po, 0)))
                        self.tt("vector", UUr[:, 0:256], self.pf[0][:, 0:256],
                                sb["mask_tu"][:], ALU.mult, wait=[fu[-1]])
                        cu2 = self.tt("vector", UUr[:, 256:512], self.pf[0][:, 256:512],
                                      sb["mask_tu"][:], ALU.mult, relkey=("pf", 0))
                        fw = []
                        for hh, po in ((0, 0), (1, 64)):
                            fw.append(self.mm(("pf", 1) if hh == 0 else None,
                                              self.pf[1][:, hh * 128: hh * 128 + 128],
                                              oAh[po:po + 64, scol],
                                              oBt[po:po + 64, scol], True, True,
                                              tp=(po, 0)))
                        self.tt("vector", Wb[:, 0:128], self.pf[1][:, 0:128],
                                sb["mask_sl"][:], ALU.mult, wait=[fw[-1]])
                        cwb = self.tt("vector", Wb[:, 128:256], self.pf[1][:, 128:256],
                                      sb["mask_sl"][:], ALU.mult, relkey=("pf", 1))
                        cta = []
                        ctu = []
                        for hh, po in ((0, 0), (1, 64)):
                            cta.append(self.tr(("pb", 0) if hh == 0 else None,
                                               self.pb[0][:, hh * 64: hh * 64 + 64],
                                               oA[po:po + 64, scol],
                                               sb["identb"][po:po + 64, po:po + 64],
                                               wait=[cv] if hh == 0 else []))
                            ctu.append(self.mm(("pf", 0) if hh == 0 else None,
                                               self.pf[0][:, hh * 64: hh * 64 + 64],
                                               UUr[:, hh * 256: hh * 256 + 128],
                                               Vtok[:, hh * 64: hh * 64 + 64],
                                               True, True,
                                               wait=[cu2, cv] if hh == 0 else []))
                        for hh in range(2):
                            self.cp("vector", Xsb[:, hh * 128: hh * 128 + 64],
                                    self.pb[0][:, hh * 64: hh * 64 + 64],
                                    wait=[cta[-1]] if hh == 0 else [])
                            self.cp("vector", Xsb[:, hh * 128 + 64: hh * 128 + 128],
                                    self.pf[0][:, hh * 64: hh * 64 + 64],
                                    wait=[ctu[-1]] if hh == 0 else [])
                        cx = ("V", p.n["V"])
                        self.setg(("pb", 0), cx)
                        self.setg(("pf", 0), cx)
                        ctb = []
                        for hh, po in ((0, 0), (1, 64)):
                            ctb.append(self.tr(("pb", 0) if hh == 0 else None,
                                               self.pb[0][:, hh * 64: hh * 64 + 64],
                                               oBb[po:po + 64, scol],
                                               sb["identb"][po:po + 64, po:po + 64]))
                            ctb.append(self.tr(None,
                                               self.pb[0][:, 128 + hh * 64:
                                                          128 + hh * 64 + 64],
                                               oKb[po:po + 64, scol],
                                               sb["identb"][po:po + 64, po:po + 64]))
                        cbk = self.cp("vector", BKt[:], self.pb[0][:, 0:256],
                                      wait=[ctb[-1]], relkey=("pb", 0))
                        self.setg(("pb", 0), cbk)
                        prevT = [(WWr, 0), (WWr, 256)]
                        prevU = [(Wb, 0), (Wb, 128)]
                        sqoff = 0
                        for lv in range(3):
                            mms = []
                            for hh in range(2):
                                tT, oT = prevT[hh]
                                tU, oU = prevU[hh]
                                mms.append(self.mm(("pf", 1) if hh == 0 else None,
                                                   self.pf[1][:, hh * 256: hh * 256 + 128],
                                                   tT[:, oT:oT + 128],
                                                   tU[:, oU:oU + 128], True, True,
                                                   wait=[cwb, cw2] if (lv == 0 and hh == 0) else []))
                                mms.append(self.mm(None,
                                                   self.pf[1][:, hh * 256 + 128: hh * 256 + 256],
                                                   tU[:, oU:oU + 128],
                                                   tT[:, oT:oT + 128], True, True))
                            self.cp("vector", SQ[:, sqoff:sqoff + 512],
                                    self.pf[1][:, 0:512], wait=[mms[-1]],
                                    relkey=("pf", 1))
                            prevU = [(SQ, sqoff), (SQ, sqoff + 256)]
                            prevT = [(SQ, sqoff + 128), (SQ, sqoff + 384)]
                            sqoff += 512
                        mms = []
                        for hh in range(2):
                            tT, oT = prevT[hh]
                            tU, oU = prevU[hh]
                            mms.append(self.mm(("pf", 1) if hh == 0 else None,
                                               self.pf[1][:, hh * 128: hh * 128 + 128],
                                               tU[:, oU:oU + 128],
                                               tT[:, oT:oT + 128], True, True))
                        c16 = self.cp("vector", SQ[:, 1536:1792],
                                      self.pf[1][:, 0:256], wait=[mms[-1]],
                                      relkey=("pf", 1))
                        applyT = [[(WWr, 0), (WWr, 256)],
                                  [(SQ, 128), (SQ, 384)],
                                  [(SQ, 640), (SQ, 896)],
                                  [(SQ, 1152), (SQ, 1408)],
                                  [(SQ, 1536), (SQ, 1664)]]
                        for kk_ in range(5):
                            mms = []
                            for hh in range(2):
                                tT, oT = applyT[kk_][hh]
                                mms.append(self.mm(("pf", 0) if hh == 0 else None,
                                                   self.pf[0][:, hh * 128: hh * 128 + 128],
                                                   tT[:, oT:oT + 128],
                                                   Xsb[:, hh * 128: hh * 128 + 128],
                                                   True, True,
                                                   wait=[cx, c16] if (kk_ == 0 and hh == 0) else []))
                            self.tt("vector", Xsb[:], self.pf[0][:, 0:256], Xsb[:],
                                    ALU.add, wait=[mms[-1]], relkey=("pf", 0))
                        ctg = []
                        for hh in range(2):
                            ctg.append(self.tr(("pb", 1) if hh == 0 else None,
                                               self.pb[1][hh * 64: hh * 64 + 64, 0:128],
                                               Xsb[:, hh * 128: hh * 128 + 64],
                                               sb["identb"][:]))
                        cgf = self.cp("vector", Gfm[:], self.pb[1][:, 0:128],
                                      wait=[ctg[-1]], relkey=("pb", 1))
                        copen = self.mm(("pf", 2), self.pf[2][:, 0:128],
                                        sb["zerosq"][:], WWr[:, 0:128], True, False)
                        for cch in range(4):
                            crs = slice(cch * 32, (cch + 1) * 32)
                            zw0 = [("G", czs[1])] if not self._st_init else []
                            self._st_init = True
                            zm = []
                            for hh, po in ((0, 0), (1, 64)):
                                zm.append(self.mm(("pf", 3) if hh == 0 else None,
                                                  self.pf[3][cch * 32:(cch + 1) * 32,
                                                             hh * 64: hh * 64 + 64],
                                                  Gfm[po:po + 64, cch * 32:(cch + 1) * 32],
                                                  stS[po:po + 64, ct * 64:(ct + 1) * 64],
                                                  True, True, tp=(po, cch * 32),
                                                  wait=([cgf] if (cch == 0 and hh == 0) else []) + zw0))
                            for hh in range(2):
                                self.tt("vector", Ztok[crs, hh * 64: hh * 64 + 64],
                                        self.pf[3][crs, hh * 64: hh * 64 + 64],
                                        Xsb[crs, 64 + hh * 128: 128 + hh * 128],
                                        ALU.add, wait=[zm[-1]])
                            czt = ("V", p.n["V"])
                            self.setg(("pf", 3), czt)
                            for hh, po in ((0, 0), (1, 64)):
                                self.mm(None, self.pf[2][po:po + 64, crs],
                                        stS[po:po + 64, ct * 64:(ct + 1) * 64],
                                        oR[po:po + 64, stl * 128 + cch * 32:
                                           stl * 128 + (cch + 1) * 32],
                                        False, False, tp=(po, po))
                            su1 = self.mm(("pf", 4), self.pf[4][:, 0:128],
                                          BKt[crs, 0:128], Ztok[crs, 0:128],
                                          True, False, tp=(cch * 32, 0),
                                          wait=[czt, cbk])
                            su2 = self.mm(None, self.pf[4][:, 0:128],
                                          BKt[crs, 128:256], Vtok[crs, 0:128],
                                          False, True, tp=(cch * 32, 0))
                            for hh, po in ((0, 0), (1, 64)):
                                self.stt("vector", stS[po:po + 64, ct * 64:(ct + 1) * 64],
                                         stS[po:po + 64, ct * 64:(ct + 1) * 64],
                                         PLs[po:po + 64, ct * NCH + stl * 4 + cch:
                                             ct * NCH + stl * 4 + cch + 1],
                                         self.pf[4][po:po + 64, po:po + 64],
                                         ALU.mult, ALU.add, wait=[su2])
                            self.setg(("pf", 4), ("V", p.n["V"]))
                        self.mm(None, self.pf[2][0:64, 0:128], Ztok[:, 0:64],
                                WWr[:, 128:256], False, False, tp=(0, 0))
                        self.mm(None, self.pf[2][0:64, 0:128], Vtok[:, 0:64],
                                UUr[:, 128:256], False, False, tp=(0, 0))
                        self.mm(None, self.pf[2][64:128, 0:128], Ztok[:, 64:128],
                                WWr[:, 384:512], False, False, tp=(0, 64))
                        self.mm(None, self.pf[2][64:128, 0:128], Vtok[:, 64:128],
                                UUr[:, 384:512], False, False, tp=(0, 64))
                        cyc = self.mm(None, self.pf[2][:, 0:128], sb["zerosq"][:],
                                      WWr[:, 0:128], False, True)
                        self.cp("vector", yS[:, ct * SEGA + stl * 128:
                                             ct * SEGA + (stl + 1) * 128],
                                self.pf[2][:, 0:128], wait=[cyc], relkey=("pf", 2))
                c_wkv = ("V", p.n["V"])
                if seg == 0:
                    self.tap("a_yS", yS[:], [128, SW], BF, [c_wkv])

                cs1 = cs2 = None
                for ct in range(NCT):
                    cs1 = self.mm(("pf", 2) if ct == 0 else None, self.pf[2][0:16, 0:SEGA],
                                  sb["gnpat"][:, ct * 16:(ct + 1) * 16],
                                  yS[:, segsl(ct)], ct == 0, ct == 7, wait=[c_wkv])
                for ct in range(NCT):
                    q0 = ct % 2
                    dst = t1g[:] if q0 == 0 else miu[:]
                    gq = self.guard(("gsq", q0))
                    self.act(dst, yS[:, segsl(ct)], AF.Square,
                             wait=([c_wkv] if ct == 0 else []) + ([gq] if gq else []))
                    cs2 = self.mm(("pf", 3) if ct == 0 else None, self.pf[3][0:16, 0:SEGA],
                                  sb["gnpat"][:, ct * 16:(ct + 1) * 16], dst,
                                  ct == 0, ct == 7, wait=[("S", p.n["S"])])
                    self.setg(("gsq", q0), cs2)
                gA = gnr[0:16, 0:SEGA]
                gB = gnr[0:16, SEGA:2 * SEGA]
                gC = gnr[0:16, 2 * SEGA:3 * SEGA]
                self.ts("vector", gA, self.pf[2][0:16, 0:SEGA], 1.0 / 64,
                        0.0, ALU.mult, ALU.add, wait=[cs1], relkey=("pf", 2))
                self.ts("vector", gB, self.pf[3][0:16, 0:SEGA], 1.0 / 64,
                        0.0, ALU.mult, ALU.add, wait=[cs2], relkey=("pf", 3))
                self.tt("vector", gC, gA, gA, ALU.mult)
                cv4 = self.tt("vector", gB, gB, gC, ALU.subtract)
                self.act(gB, gB, AF.Ln, bias=self.col("c_epsgn")[0:16, :], wait=[cv4])
                cr5 = self.act(gB, gB, AF.Exp, scale=-0.5)
                cm5 = self.tt("vector", gC, gA, gB, ALU.mult, wait=[cr5])
                for ct in range(NCT):
                    cb1 = self.mm(("pf", 3), self.pf[3][:, 0:SEGA],
                                  sb["gnpatTf"][0:16, ct * 128:(ct + 1) * 128],
                                  gB, True, True,
                                  wait=[cm5, ("dlr2", 16)] if ct == 0 else [cm5])
                    cb2 = self.mm(("pf", 4), self.pf[4][:, 0:SEGA],
                                  sb["gnpatTf"][0:16, ct * 128:(ct + 1) * 128],
                                  gC, True, True)
                    cb3 = self.mm(("pf", 5), self.pf[5][:, 0:SEGA],
                                  sb["gnpatTf"][0:16, ct * 128:(ct + 1) * 128],
                                  bsum[0:16, :], True, True)
                    self.tt("vector", sb1f[:], yS[:, segsl(ct)], self.pf[3][:, 0:SEGA],
                            ALU.mult, wait=[cb2], relkey=("pf", 3))
                    self.tt("vector", sb1f[:], sb1f[:], self.pf[4][:, 0:SEGA],
                            ALU.subtract, relkey=("pf", 4))
                    self.ts("vector", sb1f[:], sb1f[:], self.col("gn_gamma", ct),
                            0.0, ALU.mult, ALU.add)
                    self.ts("vector", sb1f[:], sb1f[:], self.col("gn_beta", ct),
                            0.0, ALU.add, ALU.add)
                    self.tt("vector", t1g[:], vS[:, segsl(ct)], self.pf[5][:, 0:SEGA],
                            ALU.mult, wait=[cb3], relkey=("pf", 5))
                    self.tt("vector", sb1f[:], sb1f[:], t1g[:], ALU.add)
                    self.tt("vector", xdP[:, segsl(ct)], sb1f[:], gS[:, segsl(ct)],
                            ALU.mult)
                c_zfin = ("V", p.n["V"])
                if seg == 0:
                    self.tap("a_zS", xdP[:], [128, SW], BF, [c_zfin])

                for rb in range(NCT):
                    slot = rb % 2
                    c_st = strip_dma("oWT", rb, slot)
                    bk = rb % 2
                    cm = None
                    for ctc in range(NCT):
                        cm = self.mm(("pf", bk) if ctc == 0 else None,
                                     self.pf[bk][:, 0:SEGA],
                                     wstrip[:, slot * 1024 + ctc * 128:
                                            slot * 1024 + (ctc + 1) * 128],
                                     xdP[:, segsl(ctc)], ctc == 0, ctc == NCT - 1,
                                     wait=[c_st, c_zfin] if ctc == 0 else [])
                    self.tt("vector", sb[f"x{rb}"][:, s0:s1g],
                            sb[f"x{rb}"][:, s0:s1g], self.pf[bk][:, 0:SEGA], ALU.add,
                            wait=[cm], relkey=("pf", bk))
                    self.setg(("astrip", slot), self.p.guards.get(("pf", bk)))
                stE.close()
                self.fence()
        self.fence()

    def finish_phase(self, out):
        p, sb = self.p, self.sb
        stg = self.stack.enter_context(self.nc.sbuf_tensor("ostage", [128, 2 * C], F32))
        bar = self.barrier()
        c_store = {}
        for tt in range(16):
            sl = stg[:, (tt % 2) * C:(tt % 2) * C + C]
            cvs = None
            for ct in range(NCT):
                bk = ("pf", 4 + ct % 2)
                cm = self.tr(bk, self.pf[4 + ct % 2][:, 0:128],
                             sb[f"x{ct}"][:, tt * 128:(tt + 1) * 128], sb["identf"][:],
                             wait=bar if (tt == 0 and ct < 2) else [])
                g = c_store.get(tt % 2)
                cvs = self.cp("vector", sl[:, ct * 128:(ct + 1) * 128],
                              self.pf[4 + ct % 2][:, 0:128],
                              wait=[cm] + ([g] if (g and ct == 0) else []),
                              relkey=bk)
            c_store[tt % 2] = self.dma("D1", out[tt * 128:(tt + 1) * 128, :], sl,
                                       wait=[cvs], sem=f"do{tt % 2}")
        fin = [("do0", p.n["do0"]), ("do1", p.n["do1"])]
        if "dt" in p.n:
            fin.append(("dt", p.n["dt"]))
        self.p.op("sync", lambda e: None, wait=fin)


def host_consts():
    c = {}
    c["identf"] = np.eye(128, dtype=np.float32)
    c["identb_src"] = np.eye(128, dtype=np.float32)
    c["onesrow_src"] = np.ones((1, 128), dtype=np.float32)
    c["onesrowf_src"] = np.ones((1, 128), dtype=np.float32)
    c["onescol_src"] = np.ones((128, 1), dtype=np.float32)
    c["zerosq_src"] = np.zeros((128, 128), dtype=np.float32)
    dp = np.zeros((128, 128), dtype=np.float32)
    for pp in range(128):
        dp[(pp // 32) * 32, pp] = 1.0
    c["dpat_src"] = dp
    m0 = (np.arange(SEG) % LCH == 0).astype(np.float32)
    c["cm_a_src"] = np.broadcast_to(1.0 - m0, (128, SEG)).copy()
    c["cm_b_src"] = np.broadcast_to(m0, (128, SEG)).copy()
    s_i = np.arange(128)[:, None]
    t_i = np.arange(128)[None, :]
    same = (s_i // LCH) == (t_i // LCH)
    e22 = np.float32(np.exp(np.float64(22.0)))
    c["mask_tu_src"] = np.concatenate([
        (same & (s_i < t_i)).astype(np.float32),
        (same & (s_i <= t_i)).astype(np.float32)], axis=1) * e22
    c["mask_sl_src"] = (same & (s_i > t_i)).astype(np.float32) * e22
    gp = np.zeros((128, 128), dtype=np.float32)
    gpt = np.zeros((16, 1024), dtype=np.float32)
    for ct in range(NCT):
        for r in range(128):
            h = 2 * ct + r // 64
            gp[r, ct * 16 + h] = 1.0
            gpt[h, ct * 128 + r] = 1.0
    c["gnpat_src"] = gp
    c["gnpatT_src"] = gpt
    qp = np.zeros((128, 64), dtype=np.float32)
    qpt = np.zeros((16, 512), dtype=np.float32)
    for qt in range(4):
        for r in range(128):
            h = 4 * qt + r // 32
            qp[r, qt * 16 + h] = 1.0
            qpt[h, qt * 128 + r] = 1.0
    c["qpat_src"] = qp
    c["qpatT_src"] = qpt
    es = np.zeros((8, 8 * 128), dtype=np.float32)
    for e in range(8):
        es[e, e * 128:(e + 1) * 128] = 1.0
    c["esel_src"] = es
    return c


def pack_cols(inputs):
    cols, idx = [], {}

    def add(name, vec):
        v = np.asarray(vec, np.float32).reshape(-1, 128)
        idx[name] = len(cols)
        for r in v:
            cols.append(r)

    add("gn_gamma", inputs["L0_attn_gn_gamma"])
    add("gn_beta", inputs["L0_attn_gn_beta"])
    for g in range(6):
        add(f"lam{g}", np.asarray(inputs["L0_attn_lambdas"])[g, 0, 0])
    add("x_miu", np.asarray(inputs["L0_attn_x_miu"])[0, 0])
    add("w_miu", np.asarray(inputs["L0_attn_w_miu"])[0, 0])
    add("a_time", np.asarray(inputs["L0_attn_a_time"])[0, 0])
    add("xtime", np.asarray(inputs["L0_attn_x_time"], np.float32).reshape(-1))
    add("o_b", inputs["L0_ffn_o_b"])
    for e in range(E_EXP):
        add(f"b1_{e}", inputs["L0_ffn_b1"][e])
        add(f"b2_{e}", inputs["L0_ffn_b2"][e])
    qb = np.asarray(inputs["L0_ffn_q_b"], np.float32).reshape(16, 64)[:, :32].reshape(-1)
    add("q_b", qb)
    add("k_b", inputs["L0_ffn_k_b"])
    add("c_eps1", np.full(128, 1e-5, np.float32))
    add("c_epsgn", np.full(128, 1.6e-4, np.float32))
    add("c_one", np.full(128, 1.0, np.float32))
    add("c_mhalf", np.full(128, -0.5, np.float32))
    add("c_mquart", np.full(128, -0.25, np.float32))
    add("c_m11", np.full(128, -11.0, np.float32))
    return np.stack(cols, axis=1).astype(np.float32), idx


def pack_rows(inputs):
    rows, idx = [], {}

    def add(name, vec):
        v = np.asarray(vec, np.float32).reshape(-1)
        idx[name] = (len(rows), v.size)
        rows.extend(v.tolist())

    add("v_b", inputs["L0_ffn_v_b"])
    add("router_b", inputs["L0_ffn_router_b"])
    return np.asarray(rows, np.float32).reshape(1, -1), idx


def make_base(inputs, colarr, rowarr, consts):
    qsel = np.asarray(inputs["L0_ffn_q_w"], np.float32).reshape(C, 16, 64)[:, :, :32]
    return dict(
        q_w_sel=np.ascontiguousarray(qsel.reshape(C, 512)),
        k_w=np.ascontiguousarray(inputs["L0_ffn_k_w"], np.float32),
        v_w=np.ascontiguousarray(inputs["L0_ffn_v_w"], np.float32),
        o_w=np.ascontiguousarray(np.asarray(inputs["L0_ffn_o_w"], np.float32)[:512]),
        router_w_t=np.ascontiguousarray(
            np.asarray(inputs["L0_ffn_router_w"], np.float32).reshape(8, 128, 8)
            .transpose(1, 0, 2).reshape(128, 64)),
        w1=np.ascontiguousarray(inputs["L0_ffn_w1"], np.float32),
        w2=np.ascontiguousarray(inputs["L0_ffn_w2"], np.float32),
        rW=np.ascontiguousarray(inputs["L0_attn_rW"], np.float32),
        kW=np.ascontiguousarray(inputs["L0_attn_kW"], np.float32),
        vW=np.ascontiguousarray(inputs["L0_attn_vW"], np.float32),
        oW=np.ascontiguousarray(inputs["L0_attn_oW"], np.float32),
        g_A=np.ascontiguousarray(inputs["L0_attn_g_A"], np.float32),
        g_B=np.ascontiguousarray(inputs["L0_attn_g_B"], np.float32),
        k_A=np.ascontiguousarray(inputs["L0_attn_k_A"], np.float32),
        k_B=np.ascontiguousarray(inputs["L0_attn_k_B"], np.float32),
        w_A=np.ascontiguousarray(inputs["L0_attn_w_A"], np.float32),
        w_B=np.ascontiguousarray(inputs["L0_attn_w_B"], np.float32),
        a_A=np.ascontiguousarray(inputs["L0_attn_a_A"], np.float32),
        a_B=np.ascontiguousarray(inputs["L0_attn_a_B"], np.float32),
        x_A=np.ascontiguousarray(inputs["L0_attn_x_A"], np.float32),
        x_Bp=np.ascontiguousarray(
            np.asarray(inputs["L0_attn_x_B"], np.float32).reshape(192, 1024)),
        cols=colarr, rows=rowarr, **consts)


def build_all(stages, taps, inputs, stop_after=None, racecheck=False):
    colarr, colidx = pack_cols(inputs)
    rowarr, rowidx = pack_rows(inputs)
    b = Builder(stages=stages, taps=taps, stop_after=stop_after, racecheck=racecheck)
    b.ncols = colarr.shape[1]
    b.nrows = rowarr.shape[1]
    b.colidx = colidx
    b.rowidx = rowidx
    nc = b.build()
    base = make_base(inputs, colarr, rowarr, host_consts())
    return nc, base


def kernel(**inputs):
    nc, base = build_all(("ffn1", "attn", "ffn2"), (), inputs)
    xs = np.asarray(inputs["L0_attn_x"], np.float32)
    in_maps = [dict(base, x=np.ascontiguousarray(xs[i])) for i in range(8)]
    res = run_bass_kernel_spmd(nc, in_maps, core_ids=list(range(8)))
    return np.stack([r["out"] for r in res.results]).astype(np.float32)


# version 04 (final)
# speedup vs baseline: 1.1534x; 1.0066x over previous
import os
import numpy as np
import ml_dtypes
import concourse.bass as bass
import concourse.mybir as mybir
from concourse.bass_utils import run_bass_kernel_spmd
from contextlib import ExitStack

BF16N = ml_dtypes.bfloat16
F32 = mybir.dt.float32
BF = mybir.dt.bfloat16
AF = mybir.ActivationFunctionType
ALU = mybir.AluOpType
AX = mybir.AxisListType.X

T_SEQ = 2048
C = 1024
NCT = 8
LCH = 32
SEG = 512
E_EXP = 8
DFF = 2048
RT32 = float(1.0 / np.sqrt(np.float32(32.0)))

ENGS = ("tensor", "vector", "scalar", "gpsimd", "sync")
_PM = os.environ.get("BASS_PRUNE", "TVG")
PRUNE_SEMS = (set() if _PM == "off"
              else {"T", "V", "S", "G"} if _PM == "compute"
              else set(_PM) if _PM.isupper()
              else None)


class _AllSems:
    def __contains__(self, x):
        return True


if PRUNE_SEMS is None:
    PRUNE_SEMS = _AllSems()
INCOF = {"tensor": "T", "vector": "V", "scalar": "S", "gpsimd": "G", "sync": "D1"}


class Plan:
    def __init__(self, nc):
        self.nc = nc
        self.items = {e: [] for e in ENGS}
        self.n = {}
        self.sem = {}
        self.guards = {}
        self.wlog = {}
        self.acc = {}
        self.waited = {}

    def mksem(self, stack, *names):
        self.stack = stack
        for nm in names:
            self.sem[nm] = stack.enter_context(self.nc.semaphore(nm))
            self.n[nm] = 0

    def getsem(self, nm):
        if nm not in self.sem:
            self.sem[nm] = self.stack.enter_context(self.nc.semaphore(nm))
            self.n[nm] = 0
        return self.sem[nm]

    def op(self, eng, fn, wait=(), inc=None, inc_n=None, reads=(), writes=()):
        if inc is not None and inc_n is None:
            inc_n = 16 if inc.startswith("D") else 1
        w = [x for x in wait if x is not None]
        if eng in ("vector", "scalar", "gpsimd") and reads:
            log = self.wlog.setdefault(eng, [])
            idx = len(self.items[eng])
            need = 0
            for rs in reads:
                if rs is None:
                    continue
                for (widx, wcnt, ws) in reversed(log):
                    if idx - widx > 8:
                        break
                    if ws[0] == rs[0] and ws[1] < rs[2] and rs[1] < ws[2]:
                        need = max(need, wcnt)
                        break
            if need:
                w.append((eng[0].upper(), need))
        is_dma = inc is not None and inc_n == 16
        dom = ("dma", inc, self.n.get(inc, 0)) if is_dma else eng
        reads = [r for r in reads if r is not None]
        writes = [ws for ws in writes if ws is not None]
        if reads or writes:
            for rs in reads:
                rec = self.acc.get(rs[0])
                if rec:
                    for (kind, s, e, d2, sn, sv) in rec["recent"]:
                        if kind == "w" and d2 != dom and s < rs[2] and rs[1] < e:
                            w.append((sn, sv))
                    for (d2, sn), (sv, s, e) in rec["oldw"].items():
                        if d2 != dom and s < rs[2] and rs[1] < e:
                            w.append((sn, sv))
            for ws in writes:
                rec = self.acc.get(ws[0])
                if rec:
                    for (kind, s, e, d2, sn, sv) in rec["recent"]:
                        if d2 != dom and s < ws[2] and ws[1] < e:
                            w.append((sn, sv))
                    for (d2, sn), (sv, s, e) in rec["oldw"].items():
                        if d2 != dom and s < ws[2] and ws[1] < e:
                            w.append((sn, sv))
                    for (d2, sn), (sv, s, e) in rec["oldr"].items():
                        if d2 != dom and s < ws[2] and ws[1] < e:
                            w.append((sn, sv))
        if w:
            wm = self.waited.setdefault(eng, {})
            best = {}
            for (sn, sv) in w:
                if sv > best.get(sn, -1):
                    best[sn] = sv
            w = []
            for sn, sv in best.items():
                if sn in PRUNE_SEMS:
                    if sv > wm.get(sn, 0):
                        wm[sn] = sv
                        w.append((sn, sv))
                else:
                    w.append((sn, sv))
        self.items[eng].append((w, fn, inc, inc_n))
        if inc is not None:
            self.n[inc] += inc_n
            myc = self.n[inc]
            if eng in ("vector", "scalar", "gpsimd") and writes:
                log = self.wlog.setdefault(eng, [])
                idx = len(self.items[eng]) - 1
                for ws in writes:
                    log.append((idx, myc, ws))
                if len(log) > 64:
                    del log[:-48]
            for kind, sigs in (("r", reads), ("w", writes)):
                for sg in sigs:
                    rec = self.acc.setdefault(
                        sg[0], {"recent": [], "oldw": {}, "oldr": {}})
                    rec["recent"].append((kind, sg[1], sg[2], dom, inc, myc))
                    if len(rec["recent"]) > 40:
                        for (k2, s, e, d2, sn, sv) in rec["recent"][:-32]:
                            old = rec["oldw" if k2 == "w" else "oldr"]
                            prev = old.get((d2, sn))
                            if prev is None:
                                old[(d2, sn)] = (sv, s, e)
                            else:
                                old[(d2, sn)] = (max(prev[0], sv),
                                                 min(prev[1], s),
                                                 max(prev[2], e))
                        del rec["recent"][:-32]
            return (inc, myc)
        return None

    def realize(self, block):
        sems = self.sem

        def runner(eng_name):
            def run(eng):
                for waits, fn, inc, inc_n in self.items[eng_name]:
                    for (snm, val) in waits:
                        if val > 0:
                            eng.wait_ge(sems[snm], val)
                    inst = fn(eng)
                    if inc is not None:
                        assert inst is not None
                        inst.then_inc(sems[inc], inc_n)
            return run

        block.tensor(runner("tensor"))
        block.vector(runner("vector"))
        block.scalar(runner("scalar"))
        block.gpsimd(runner("gpsimd"))
        block.sync(runner("sync"))


class Builder:
    def __init__(self, stages=("ffn1", "attn", "ffn2"), taps=(), stop_after=None,
                 racecheck=False):
        self.stages = stages
        self.taps = set(taps)
        self.stop_after = stop_after
        self.racecheck = racecheck

    @staticmethod
    def sig(ap):
        try:
            t = ap.tensor
        except Exception:
            return None
        try:
            off = int(ap.offset)
            ext = 1
            for pair in ap.ap:
                step, count = int(pair[0]), int(pair[1])
                ext += abs(step) * (count - 1)
            return (t.name, off, off + ext)
        except Exception:
            return (t.name, 0, 1 << 60)

    @staticmethod
    def wsig(ap):
        try:
            return (ap.tensor.name, 0, 1 << 60)
        except Exception:
            return None

    def guard(self, key):
        return self.p.guards.get(key)

    def setg(self, key, cnt):
        self.p.guards[key] = cnt

    def mm(self, gkey, out, lhsT, rhs, start, stop, wait=(), tp=None):
        w = list(wait)
        if start and gkey is not None:
            g = self.guard(gkey)
            if g:
                w.append(g)

        def fn(t, o=out, a=lhsT, b=rhs, s1=start, s2=stop, q=tp):
            return t.matmul(o, a, b, start=s1, stop=s2, tile_position=q)
        return self.p.op("tensor", fn, wait=w, inc="T",
                         reads=[self.sig(lhsT), self.sig(rhs)],
                         writes=[self.wsig(out) if start else self.sig(out)])

    def tr(self, gkey, out, in_, ident, wait=()):
        w = list(wait)
        if gkey is not None:
            g = self.guard(gkey)
            if g:
                w.append(g)

        def fn(t, o=out, a=in_, i=ident):
            return t.transpose(o, a, i)
        return self.p.op("tensor", fn, wait=w, inc="T",
                         reads=[self.sig(in_), self.sig(ident)],
                         writes=[self.sig(out)])

    def cp(self, eng, out, in_, wait=(), relkey=None):
        def fn(e, o=out, a=in_):
            return e.tensor_copy(o, a)
        c = self.p.op(eng, fn, wait=list(wait), inc=INCOF[eng],
                      reads=[self.sig(in_)], writes=[self.sig(out)])
        if relkey is not None:
            self.setg(relkey, c)
        return c

    def tt(self, eng, out, a, b, alu, wait=(), relkey=None):
        def fn(e, o=out, x=a, y=b, op=alu):
            return e.tensor_tensor(o, x, y, op)
        c = self.p.op(eng, fn, wait=list(wait), inc=INCOF[eng],
                      reads=[self.sig(a), self.sig(b)], writes=[self.sig(out)])
        if relkey is not None:
            self.setg(relkey, c)
        return c

    def ts(self, eng, out, in_, s1, s2, op0, op1, wait=(), relkey=None):
        def fn(e, o=out, a=in_, x=s1, y=s2, p0=op0, p1=op1):
            return e.tensor_scalar(o, a, x, y, p0, p1)
        rds = [self.sig(in_)]
        for scl in (s1, s2):
            if hasattr(scl, "tensor"):
                rds.append(self.sig(scl))
        c = self.p.op(eng, fn, wait=list(wait), inc=INCOF[eng],
                      reads=rds, writes=[self.sig(out)])
        if relkey is not None:
            self.setg(relkey, c)
        return c

    def stt(self, eng, out, in0, scalar, in1, op0, op1, wait=(), relkey=None):
        def fn(e, o=out, a=in0, s=scalar, b=in1, p0=op0, p1=op1):
            return e.scalar_tensor_tensor(o, a, s, b, p0, p1)
        rds = [self.sig(in0), self.sig(in1)]
        if hasattr(scalar, "tensor"):
            rds.append(self.sig(scalar))
        c = self.p.op(eng, fn, wait=list(wait), inc=INCOF[eng],
                      reads=rds, writes=[self.sig(out)])
        if relkey is not None:
            self.setg(relkey, c)
        return c

    def act(self, out, in_, func, bias=0.0, scale=1.0, accum=None, wait=(), relkey=None):
        def fn(s, o=out, a=in_, f=func, b=bias, sc=scale, ac=accum):
            return s.activation(o, a, f, bias=b, scale=sc, accum_out=ac)
        rds = [self.sig(in_)]
        for scl in (bias, scale):
            if hasattr(scl, "tensor"):
                rds.append(self.sig(scl))
        wrs = [self.sig(out)] + ([self.sig(accum)] if accum is not None else [])
        c = self.p.op("scalar", fn, wait=list(wait), inc="S", reads=rds, writes=wrs)
        if relkey is not None:
            self.setg(relkey, c)
        return c

    def red(self, out, in_, alu, negate=False, wait=(), relkey=None):
        def fn(v, o=out, a=in_, op=alu, ng=negate):
            return v.tensor_reduce(o, a, AX, op, negate=ng)
        c = self.p.op("vector", fn, wait=list(wait), inc="V",
                      reads=[self.sig(in_)], writes=[self.sig(out)])
        if relkey is not None:
            self.setg(relkey, c)
        return c

    def scan(self, out, d0, d1, init, op0, op1, wait=(), relkey=None):
        def fn(v, o=out, a=d0, b=d1, i=init, p0=op0, p1=op1):
            return v.tensor_tensor_scan(o, a, b, i, p0, p1)
        c = self.p.op("vector", fn, wait=list(wait), inc="V",
                      reads=[self.sig(d0), self.sig(d1)], writes=[self.sig(out)])
        if relkey is not None:
            self.setg(relkey, c)
        return c

    def recip(self, out, in_, wait=(), relkey=None):
        def fn(v, o=out, a=in_):
            return v.reciprocal(o, a)
        c = self.p.op("vector", fn, wait=list(wait), inc="V",
                      reads=[self.sig(in_)], writes=[self.sig(out)])
        if relkey is not None:
            self.setg(relkey, c)
        return c

    def dma(self, q, out, in_, wait=(), sem="dc"):
        eng = {"D1": "sync", "D2": "gpsimd"}[q]
        self.p.getsem(sem)

        def fn(e, o=out, a=in_):
            return e.dma_start(out=o, in_=a)
        return self.p.op(eng, fn, wait=list(wait), inc=sem, inc_n=16,
                         reads=[self.sig(in_)], writes=[self.sig(out)])

    def bcrow(self, gkey, out, row, nrows=128, wait=()):
        ones = self.sb["onesrowf"] if row.dtype == F32 else self.sb["onesrow"]
        return self.mm(gkey, out, ones[0:1, 0:nrows], row, True, True, wait=wait)

    def zfill(self, gkey, out, rhs, wait=()):
        return self.mm(gkey, out, self.sb["zerosq"][:], rhs, True, False, wait=wait)

    def barrier(self):
        p = self.p
        return [(nm, c) for nm, c in p.n.items() if c > 0]

    def fence(self):
        bar = self.barrier()
        for eng in ENGS:
            self.p.op(eng, lambda e: None, wait=bar)
        return bar

    def tap(self, name, ap, shape, dt, wait):
        if name not in self.taps:
            return
        d = self.nc.dram_tensor(f"dbg_{name}", list(shape), dt,
                                kind="ExternalOutput").ap()
        self.dma("D1", d, ap, wait=wait, sem="dt")

    def col(self, name, k=0):
        i = self.colidx[name] + k
        return self.sb["cols"][:, i:i + 1]

    def row(self, name):
        off, n = self.rowidx[name]
        return self.sb["rows"][0:1, off:off + n]

    def build(self):
        nc = bass.Bass("TRN2", target_bir_lowering=False,
                       detect_race_conditions=self.racecheck)
        self.nc = nc
        self.p = Plan(nc)
        D = {}

        def din(name, shape, dt=F32):
            D[name] = nc.dram_tensor(name, list(shape), dt, kind="ExternalInput").ap()
        din("x", [T_SEQ, C])
        din("q_w_sel", [C, 512]); din("k_w", [C, 512]); din("v_w", [C, 512])
        din("o_w", [512, C]); din("router_w_t", [128, 64])
        din("w1", [E_EXP, C, DFF]); din("w2", [E_EXP, DFF, C])
        for nm in ("rW", "kW", "vW", "oW"):
            din(nm, [C, C])
        din("g_A", [C, 128]); din("g_B", [128, C])
        din("k_A", [C, 64]); din("k_B", [64, C])
        din("w_A", [C, 64]); din("w_B", [64, C])
        din("a_A", [C, 64]); din("a_B", [64, C])
        din("x_A", [C, 192]); din("x_Bp", [192, C])
        din("cols", [128, self.ncols]); din("rows", [1, self.nrows])
        for nm, shp in (("identf", [128, 128]), ("identb_src", [128, 128]),
                        ("onesrow_src", [1, 128]), ("onesrowf_src", [1, 128]), ("onescol_src", [128, 1]),
                        ("zerosq_src", [128, 128]), ("dpat_src", [128, 128]),
                        ("cm_a_src", [128, SEG]), ("cm_b_src", [128, SEG]),
                        ("mask_tu_src", [128, 256]), ("mask_sl_src", [128, 128]),
                        ("gnpat_src", [128, 128]), ("gnpatT_src", [16, 1024]),
                        ("qpat_src", [128, 64]), ("qpatT_src", [16, 512]), ("esel_src", [8, 8 * 128])):
            din(nm, shp)
        out = nc.dram_tensor("out", [T_SEQ, C], F32, kind="ExternalOutput").ap()
        self.scr = {}
        for nm in ("rWT", "kWT", "vWT", "oWT"):
            self.scr[nm] = nc.dram_tensor(f"scr_{nm}", [NCT, NCT, 128, 128], BF,
                                          kind="ExternalOutput").ap()
        self.D = D
        with ExitStack() as stack:
            self.stack = stack
            self.p.mksem(stack, "T", "V", "S", "G")
            self.alloc_persist(stack)
            self.init_phase()
            if "ffn1" in self.stages:
                self.ffn_phase("f1")
            if "attn" in self.stages:
                self.attn_phase()
            if "ffn2" in self.stages:
                self.ffn_phase("f2")
            self.finish_phase(out)
            block = stack.enter_context(nc.Block())
            self.p.realize(block)
        return nc

    def alloc_persist(self, stack):
        nc = self.nc
        E = stack.enter_context
        sb = self.sb = {}
        for ct in range(NCT):
            sb[f"x{ct}"] = E(nc.sbuf_tensor(f"x{ct}", [128, T_SEQ], F32))
        for nm, shp, dt in (("cols", [128, self.ncols], F32),
                            ("rows", [1, self.nrows], BF),
                            ("identf", [128, 128], F32),
                            ("identb", [128, 128], BF),
                            ("onesrow", [1, 128], BF),
                            ("onesrowf", [1, 128], F32),
                            ("onescol", [128, 1], BF),
                            ("zerosq", [128, 128], BF),
                            ("dpat", [128, 128], BF),
                            ("qpat", [128, 64], BF),
                            ("qpatT", [16, 512], BF),
                            ("esel", [8, 8 * 128], BF),
                            ("routerw", [128, 64], BF)):
            sb[nm] = E(nc.sbuf_tensor("sb_" + nm, list(shp), dt))
        self.pf = [E(nc.psum_tensor(f"pf{i}", [128, 512], F32)) for i in range(6)]
        self.pb = [E(nc.psum_tensor(f"pbf{i}", [128, 1024], BF)) for i in range(2)]

    def init_phase(self):
        p, D, sb = self.p, self.D, self.sb
        self.dma("D1", sb["cols"][:], D["cols"])
        self.dma("D1", sb["identf"][:], D["identf"])
        self.dma("D1", self.sb["onesrowf"][:], D["onesrowf_src"])
        for src, dst in (("identb_src", "identb"), ("onesrow_src", "onesrow"),
                         ("onescol_src", "onescol"), ("zerosq_src", "zerosq"),
                         ("dpat_src", "dpat"),
                         ("qpat_src", "qpat"), ("qpatT_src", "qpatT"),
                         ("esel_src", "esel"),
                         ("rows", "rows"), ("router_w_t", "routerw")):
            self.dma("D2", sb[dst][:], D[src], sem="dg")
        self.c_consts1 = ("dc", p.n["dc"])
        self.c_consts = ("dg", p.n["dg"])
        self.init_stack = ExitStack()
        stg = self.init_stack.enter_context(self.nc.sbuf_tensor("xstage", [128, 2 * C], F32))
        for tt in range(16):
            sl = stg[:, (tt % 2) * C:(tt % 2) * C + C]
            g = self.guard(("xstg", tt % 2))
            c_ld = self.dma("D1", sl, D["x"][tt * 128:(tt + 1) * 128, :],
                            wait=[g] if g else [], sem=f"dx{tt % 2}")
            for ct in range(NCT):
                bk = ("pf", 4 + ct % 2)
                cm = self.tr(bk, self.pf[4 + ct % 2][:, 0:128],
                             sl[:, ct * 128:(ct + 1) * 128], sb["identf"][:],
                             wait=[c_ld, self.c_consts1])
                cv = self.cp("vector", sb[f"x{ct}"][:, tt * 128:(tt + 1) * 128],
                             self.pf[4 + ct % 2][:, 0:128], wait=[cm], relkey=bk)
                if ct == NCT - 1:
                    self.setg(("xstg", tt % 2), cv)
        self.c_x = ("V", p.n["V"])
        wrow = self.init_stack.enter_context(self.nc.sbuf_tensor("wrow", [128, 2 * C], BF))
        wtr = self.init_stack.enter_context(self.nc.sbuf_tensor("wtr", [128, 256], BF))
        import os as _os
        _nwt = int(_os.environ.get("KWT", "4"))
        for nm in (("rW", "kW", "vW", "oW")[:_nwt] if "attn" in self.stages else ()):
            for r in range(NCT):
                slr = wrow[:, (r % 2) * C:(r % 2) * C + C]
                g = self.guard(("wrow", r % 2))
                c_ld = self.dma("D2", slr, D[nm][r * 128:(r + 1) * 128, :],
                                wait=([g] if g else []) + [self.c_consts],
                                sem=f"dm{r % 2}")
                last = None
                for cb in range(NCT):
                    sw = cb % 2
                    cm = self.tr(("pbh", sw), self.pb[sw][:, 0:128],
                                 slr[:, cb * 128:(cb + 1) * 128], sb["identb"][:],
                                 wait=[c_ld])
                    g2 = self.guard(("wtr", sw))
                    cv = self.cp("vector", wtr[:, sw * 128:sw * 128 + 128],
                                 self.pb[sw][:, 0:128],
                                 wait=[cm] + ([g2] if g2 else []),
                                 relkey=("pbh", sw))
                    c_st = self.dma("D1", self.scr[nm + "T"][cb, r, :, :],
                                    wtr[:, sw * 128:sw * 128 + 128], wait=[cv],
                                    sem=f"ds{sw}")
                    self.setg(("wtr", sw), c_st)
                    last = cm
                self.setg(("wrow", r % 2), last)
        self.c_wt = [(s, p.n[s]) for s in ("ds0", "ds1") if s in p.n]
        self.init_stack.close()
        self.entry_wait = self.c_wt + [("T", p.n["T"])]

    def emit_ln(self, xln_tile_fn, t0, t1, upd, sc):
        p, sb = self.p, self.sb
        xbf, xsq, t1f, rows = sc["xbf"], sc["xsq"], sc["t1f"], sc["rows"]
        mu, esq, var, rstd, mr = (rows[0:1, i * 512:(i + 1) * 512] for i in range(5))
        ew_s = [("V", p.n["V"])]
        for b0 in range(t0, t1, 512):
            sl = slice(b0, b0 + 512)
            cm2 = cm3 = None
            for ct in range(NCT):
                s0 = ct % 2
                gA = self.guard(("lnxbf", s0))
                gB = self.guard(("lnxsq", s0))
                cv = self.cp("vector", xbf[:, s0 * 512:(s0 + 1) * 512],
                             sb[f"x{ct}"][:, sl], wait=[gA] if gA else [])
                cs = self.act(xsq[:, s0 * 512:(s0 + 1) * 512],
                              sb[f"x{ct}"][:, sl], AF.Square,
                              wait=ew_s + ([gB] if gB else []))
                ew_s = []
                cm2 = self.mm(("pf", 2) if ct == 0 else None, self.pf[2][0:1, :],
                              sb["onescol"][:], xbf[:, s0 * 512:(s0 + 1) * 512],
                              ct == 0, ct == NCT - 1,
                              wait=[cv] + ([self.c_consts] if ct == 0 else []))
                self.setg(("lnxbf", s0), cm2)
                cm3 = self.mm(("pf", 3) if ct == 0 else None, self.pf[3][0:1, :],
                              sb["onescol"][:], xsq[:, s0 * 512:(s0 + 1) * 512],
                              ct == 0, ct == NCT - 1, wait=[cs])
                self.setg(("lnxsq", s0), cm3)
            self.ts("vector", mu, self.pf[2][0:1, :], 1.0 / 1024, 0.0, ALU.mult,
                    ALU.add, wait=[cm2], relkey=("pf", 2))
            self.ts("vector", esq, self.pf[3][0:1, :], 1.0 / 1024, 0.0, ALU.mult,
                    ALU.add, wait=[cm3], relkey=("pf", 3))
            self.tt("vector", var, mu, mu, ALU.mult)
            c4 = self.tt("vector", var, esq, var, ALU.subtract)
            self.act(rstd, var, AF.Ln, bias=self.col("c_eps1")[0:1, :], wait=[c4])
            c6 = self.act(rstd, rstd, AF.Exp, scale=-0.5)
            c7 = self.tt("vector", mr, mu, rstd, ALU.mult, wait=[c6])
            cb1 = self.bcrow(("pf", 4), self.pf[4][:, :], rstd, wait=[c7])
            cb2 = self.bcrow(("pf", 5), self.pf[5][:, :], mr)
            lastv = None
            for ct in range(NCT):
                self.tt("vector", t1f[:], sb[f"x{ct}"][:, sl], self.pf[4][:, :],
                        ALU.mult, wait=[cb2] if ct == 0 else [])
                self.tt("vector", t1f[:], t1f[:], self.pf[5][:, :], ALU.subtract)
                ew = self.entry_wait
                self.entry_wait = []
                lastv = self.cp("vector", xln_tile_fn(ct)[:, sl], t1f[:], wait=ew)
                if upd:
                    lastv = self.stt("vector", sb[f"x{ct}"][:, sl], t1f[:], 0.5,
                                     sb[f"x{ct}"][:, sl], ALU.mult, ALU.add)
            self.setg(("pf", 4), lastv)
            self.setg(("pf", 5), lastv)
        return ("V", p.n["V"])

    def emit_ln_seg(self, xlnS, t0, sc, W):
        p, sb = self.p, self.sb
        xbf, xsq, t1f, rows = sc["xbf"], sc["xsq"], sc["t1f"], sc["rows"]
        mu, esq, var, rstd, mr = (rows[0:1, i * W:(i + 1) * W] for i in range(5))
        sl = slice(t0, t0 + W)
        ew_s = [("V", p.n["V"])]
        cm2 = cm3 = None
        for ct in range(NCT):
            q0 = ct % 2
            gA = self.guard(("lnxbf", q0))
            gB = self.guard(("lnxsq", q0))
            cv = self.cp("vector", xbf[:, q0 * W:(q0 + 1) * W],
                         sb[f"x{ct}"][:, sl], wait=[gA] if gA else [])
            cs = self.act(xsq[:, q0 * W:(q0 + 1) * W],
                          sb[f"x{ct}"][:, sl], AF.Square,
                          wait=ew_s + ([gB] if gB else []))
            ew_s = []
            cm2 = self.mm(("pf", 2) if ct == 0 else None, self.pf[2][0:1, 0:W],
                          sb["onescol"][:], xbf[:, q0 * W:(q0 + 1) * W],
                          ct == 0, ct == NCT - 1, wait=[cv])
            self.setg(("lnxbf", q0), cm2)
            cm3 = self.mm(("pf", 3) if ct == 0 else None, self.pf[3][0:1, 0:W],
                          sb["onescol"][:], xsq[:, q0 * W:(q0 + 1) * W],
                          ct == 0, ct == NCT - 1, wait=[cs])
            self.setg(("lnxsq", q0), cm3)
        self.ts("vector", mu, self.pf[2][0:1, 0:W], 1.0 / 1024, 0.0, ALU.mult,
                ALU.add, wait=[cm2], relkey=("pf", 2))
        self.ts("vector", esq, self.pf[3][0:1, 0:W], 1.0 / 1024, 0.0, ALU.mult,
                ALU.add, wait=[cm3], relkey=("pf", 3))
        self.tt("vector", var, mu, mu, ALU.mult)
        c4 = self.tt("vector", var, esq, var, ALU.subtract)
        self.act(rstd, var, AF.Ln, bias=self.col("c_eps1")[0:1, :], wait=[c4])
        c6 = self.act(rstd, rstd, AF.Exp, scale=-0.5)
        c7 = self.tt("vector", mr, mu, rstd, ALU.mult, wait=[c6])
        cb1 = self.bcrow(("pf", 4), self.pf[4][:, 0:W], rstd, wait=[c7])
        cb2 = self.bcrow(("pf", 5), self.pf[5][:, 0:W], mr)
        lastv = None
        for ct in range(NCT):
            self.tt("vector", t1f[:], sb[f"x{ct}"][:, sl], self.pf[4][:, 0:W],
                    ALU.mult, wait=[cb2] if ct == 0 else [])
            self.tt("vector", t1f[:], t1f[:], self.pf[5][:, 0:W], ALU.subtract)
            lastv = self.cp("vector", xlnS[:, ct * W:(ct + 1) * W], t1f[:])
        self.setg(("pf", 4), lastv)
        self.setg(("pf", 5), lastv)
        return ("V", p.n["V"])

    def ffn_phase(self, tag):
        nc, p, sb, D = self.nc, self.p, self.sb, self.D
        self.fence()
        with ExitStack() as st:
            E = st.enter_context
            xln = E(nc.sbuf_tensor(f"{tag}_xln", [128, NCT * T_SEQ], BF))
            comb = E(nc.sbuf_tensor(f"{tag}_comb", [128, 16 * 8], BF))
            combT = E(nc.sbuf_tensor(f"{tag}_combT", [8, T_SEQ], BF))
            stB = ExitStack()
            EB = stB.enter_context
            Q = EB(nc.sbuf_tensor(f"{tag}_Q", [128, 4 * T_SEQ], BF))
            K = EB(nc.sbuf_tensor(f"{tag}_K", [128, 4 * T_SEQ], BF))
            VT = EB(nc.sbuf_tensor(f"{tag}_VT", [128, 16 * 528], BF))
            owf = EB(nc.sbuf_tensor(f"{tag}_owf", [128, 4 * 1024], BF))
            shbc = EB(nc.sbuf_tensor(f"{tag}_shbc", [128, 16], F32))
            stA = ExitStack()
            EA = stA.enter_context
            sc = dict(
                xbf=EA(nc.sbuf_tensor(f"{tag}_xbf", [128, 2 * 512], BF)),
                xsq=EA(nc.sbuf_tensor(f"{tag}_xsq", [128, 2 * 512], BF)),
                t1f=EA(nc.sbuf_tensor(f"{tag}_t1f", [128, 512], F32)),
                rows=EA(nc.sbuf_tensor(f"{tag}_lnr", [1, 5 * 512], F32)),
            )
            qkw = EA(nc.sbuf_tensor(f"{tag}_qkw", [128, NCT * 512], BF))
            kvw = EA(nc.sbuf_tensor(f"{tag}_kvw", [128, NCT * 1024], BF))
            xlt = lambda ct: xln[:, ct * T_SEQ:(ct + 1) * T_SEQ]

            dqs = f"dq_{tag}"
            for ct in range(NCT):
                self.dma("D2", qkw[:, ct * 512:(ct + 1) * 512],
                         D["q_w_sel"][ct * 128:(ct + 1) * 128, :], sem=dqs)
                self.dma("D2", kvw[:, ct * 1024:ct * 1024 + 512],
                         D["k_w"][ct * 128:(ct + 1) * 128, :], sem=dqs)
                self.dma("D2", kvw[:, ct * 1024 + 512:ct * 1024 + 1024],
                         D["v_w"][ct * 128:(ct + 1) * 128, :], sem=dqs)
            for vt in range(4):
                self.dma("D2", owf[:, vt * 1024:(vt + 1) * 1024],
                         D["o_w"][vt * 128:(vt + 1) * 128, :], sem=dqs)
            c_wq = (dqs, p.n[dqs])
            qk_lds = [c_wq] * NCT
            kv_lds = [(c_wq, c_wq)] * NCT
            ow_lds = [c_wq] * 4

            for stile in range(16):
                for h in range(16):
                    self.cp("vector",
                            VT[:, stile * 528 + h * 33 + 32: stile * 528 + h * 33 + 33],
                            sb["onescol"][:],
                            wait=[self.c_consts] if (stile == 0 and h == 0) else [])

            for half in range(2):
                c_ln = self.emit_ln(xlt, half * 1024, half * 1024 + 1024, True, sc)
                for qt in range(4):
                    for tb in range(half * 2, half * 2 + 2):
                        bk = ("pf", (qt * 2 + tb) % 2)
                        cm = None
                        for ct in range(NCT):
                            cm = self.mm(bk if ct == 0 else None,
                                         self.pf[(qt * 2 + tb) % 2][:, :],
                                         qkw[:, ct * 512 + qt * 128: ct * 512 + (qt + 1) * 128],
                                         xlt(ct)[:, tb * 512:(tb + 1) * 512],
                                         ct == 0, ct == NCT - 1,
                                         wait=[qk_lds[ct], c_ln] if ct == 0 else [])
                        self.act(Q[:, qt * T_SEQ + tb * 512: qt * T_SEQ + (tb + 1) * 512],
                                 self.pf[(qt * 2 + tb) % 2][:, :], AF.Identity,
                                 bias=self.col("q_b", qt), wait=[cm], relkey=bk)
                for kt in range(4):
                    for tb in range(half * 2, half * 2 + 2):
                        bk = ("pf", (kt * 2 + tb) % 2)
                        cm = None
                        for ct in range(NCT):
                            cm = self.mm(bk if ct == 0 else None,
                                         self.pf[(kt * 2 + tb) % 2][:, :],
                                         kvw[:, ct * 1024 + kt * 128: ct * 1024 + (kt + 1) * 128],
                                         xlt(ct)[:, tb * 512:(tb + 1) * 512],
                                         ct == 0, ct == NCT - 1,
                                         wait=[kv_lds[ct][0], c_ln] if ct == 0 else [])
                        self.act(K[:, kt * T_SEQ + tb * 512: kt * T_SEQ + (tb + 1) * 512],
                                 self.pf[(kt * 2 + tb) % 2][:, :], AF.Identity,
                                 bias=self.col("k_b", kt), wait=[cm], relkey=bk)
                for stile in range(half * 8, half * 8 + 8):
                    bk = ("pf", stile % 2)
                    cm = None
                    for ct in range(NCT):
                        cm = self.mm(bk if ct == 0 else None, self.pf[stile % 2][:, :],
                                     xlt(ct)[:, stile * 128:(stile + 1) * 128],
                                     kvw[:, ct * 1024 + 512:ct * 1024 + 1024],
                                     ct == 0, False,
                                     wait=[kv_lds[ct][1], c_ln] if ct == 0 else [])
                    cm = self.mm(None, self.pf[stile % 2][:, :],
                                 sb["onesrow"][0:1, 0:128], self.row("v_b"),
                                 False, True, wait=[self.c_consts])
                    self.cp("vector",
                            VT[:, stile * 528:(stile + 1) * 528]
                            .rearrange("p (h q) -> p h q", q=33)[:, :, 0:32],
                            self.pf[stile % 2][:, :].rearrange("p (h q) -> p h q", q=32),
                            wait=[cm], relkey=bk)
            if self.stop_after == "ln":
                stA.close()
                stB.close()
                self.fence()
                return
            c_q = ("S", p.n["S"])
            c_v = ("V", p.n["V"])
            self.tap(f"{tag}_xln", xln[:, 0:T_SEQ], [128, T_SEQ], BF, self.barrier())
            self.tap(f"{tag}_Q", Q[:, 0:T_SEQ], [128, T_SEQ], BF, [c_q])
            self.tap(f"{tag}_K", K[:, 0:T_SEQ], [128, T_SEQ], BF, [c_q])
            self.tap(f"{tag}_VT", VT[:, 0:528], [128, 528], BF, [c_v])

            if self.stop_after == "qkv":
                stA.close()
                stB.close()
                self.fence()
                return
            stA.close()
            self.fence()
            stC = ExitStack()
            EC = stC.enter_context
            aofm = EC(nc.sbuf_tensor(f"{tag}_ao", [128, 4 * T_SEQ], BF))
            dinv = EC(nc.sbuf_tensor(f"{tag}_dinv", [128, 512], BF))
            dinvbc = EC(nc.sbuf_tensor(f"{tag}_dinvbc", [128, 512], BF))
            esb = EC(nc.sbuf_tensor(f"{tag}_esb", [128, 2 * 512], BF))
            rt = EC(nc.sbuf_tensor(f"{tag}_rt", [128, 64], F32))
            qsqt = EC(nc.sbuf_tensor(f"{tag}_qsq", [128, 2 * 512], BF))
            mmr = EC(nc.sbuf_tensor(f"{tag}_mmr", [16, 12], F32))
            shrow = EC(nc.sbuf_tensor(f"{tag}_shrow", [1, 16], F32))
            for which, src in ((0, Q), (1, K)):
                for tb in range(4):
                    cm = None
                    for qt in range(4):
                        s0 = qt % 2
                        gA = self.guard(("qsq", s0))
                        c_sq = self.act(qsqt[:, s0 * 512:(s0 + 1) * 512],
                                        src[:, qt * T_SEQ + tb * 512:
                                            qt * T_SEQ + (tb + 1) * 512],
                                        AF.Square,
                                        wait=([c_q] if (which + tb + qt == 0) else [])
                                              + ([gA] if gA else []))
                        cm = self.mm(("pf", 5) if qt == 0 else None,
                                     self.pf[5][0:16, 0:512],
                                     sb["qpat"][:, qt * 16:(qt + 1) * 16],
                                     qsqt[:, s0 * 512:(s0 + 1) * 512],
                                     qt == 0, qt == 3, wait=[c_sq])
                        self.setg(("qsq", s0), cm)
                    self.red(mmr[0:16, which * 4 + tb:which * 4 + tb + 1],
                             self.pf[5][0:16, 0:512], ALU.max, wait=[cm],
                             relkey=("pf", 5))
            c_r1 = self.red(mmr[0:16, 8:9], mmr[0:16, 0:4], ALU.max)
            c_r2 = self.red(mmr[0:16, 9:10], mmr[0:16, 4:8], ALU.max)
            c_m = self.tt("vector", mmr[0:16, 2:3], mmr[0:16, 8:9], mmr[0:16, 9:10],
                          ALU.mult)
            c_l = self.act(mmr[0:16, 3:4], mmr[0:16, 2:3], AF.Ln, wait=[c_m])
            c_e = self.act(mmr[0:16, 3:4], mmr[0:16, 3:4], AF.Exp, scale=0.5)
            c_n = self.ts("vector", mmr[0:16, 3:4], mmr[0:16, 3:4], -RT32, 0.0,
                          ALU.mult, ALU.add, wait=[c_e])
            c_t = self.tr(("pf", 5), self.pf[5][0:1, 0:16], mmr[0:16, 3:4],
                          sb["identf"][0:16, 0:16], wait=[c_n])
            c_sr = self.cp("vector", shrow[0:1, 0:16], self.pf[5][0:1, 0:16],
                           wait=[c_t], relkey=("pf", 5))
            c_bc = self.bcrow(("pf", 5), self.pf[5][:, 0:16], shrow[0:1, 0:16],
                              wait=[c_sr])
            c_sh = self.cp("vector", shbc[:], self.pf[5][:, 0:16], wait=[c_bc],
                           relkey=("pf", 5))
            self.tap(f"{tag}_shbc", shbc[:], [128, 16], F32, [c_sh])
            self.tap(f"{tag}_mmr", mmr[:], [16, 12], F32, [c_sh])
            self.tap(f"{tag}_shrow", shrow[:], [1, 16], F32, [c_sh])
            self.tap(f"{tag}_dinv", dinv[:], [128, 512], BF, [("S", p.n["S"] + 400)])

            if self.stop_after == "shift":
                stC.close()
                stB.close()
                self.fence()
                return
            c_z = self.p.op("gpsimd", lambda g, a=dinv[:]: g.memzero(a), inc="G")
            for tb in range(4):
                for qt in range(4):
                    cz1 = self.zfill(("pf", 2), self.pf[2][:, :], esb[:, 0:512] if False
                                     else Q[:, 0:512], wait=[c_q, c_sh])
                    cz2 = self.zfill(("pf", 3), self.pf[3][:, :], Q[:, 0:512])
                    for hpp in range(2):
                        h_a, h_b = qt * 4 + 2 * hpp, qt * 4 + 2 * hpp + 1
                        ro_a, ro_b = (2 * hpp) * 32, (2 * hpp + 1) * 32
                        for stile in range(16):
                            cm_a = self.mm(("pf", 0), self.pf[0][:, :],
                                           K[ro_a:ro_a + 32,
                                             qt * T_SEQ + stile * 128:
                                             qt * T_SEQ + (stile + 1) * 128],
                                           Q[ro_a:ro_a + 32,
                                             qt * T_SEQ + tb * 512:
                                             qt * T_SEQ + (tb + 1) * 512],
                                           True, True, tp=(ro_a, 0))
                            cm_b = self.mm(("pf", 1), self.pf[1][:, :],
                                           K[ro_b:ro_b + 32,
                                             qt * T_SEQ + stile * 128:
                                             qt * T_SEQ + (stile + 1) * 128],
                                           Q[ro_b:ro_b + 32,
                                             qt * T_SEQ + tb * 512:
                                             qt * T_SEQ + (tb + 1) * 512],
                                           True, True, tp=(ro_b, 0))
                            g0, g1 = self.guard(("esb", 0)), self.guard(("esb", 1))
                            ce_a = self.act(esb[:, 0:512], self.pf[0][:, :], AF.Exp,
                                            scale=RT32, bias=shbc[:, h_a:h_a + 1],
                                            wait=[cm_a] + ([g0] if g0 else []),
                                            relkey=("pf", 0))
                            ce_b = self.act(esb[:, 512:1024], self.pf[1][:, :], AF.Exp,
                                            scale=RT32, bias=shbc[:, h_b:h_b + 1],
                                            wait=[cm_b] + ([g1] if g1 else []),
                                            relkey=("pf", 1))
                            first = (stile == 0) and (hpp == 0)
                            c_ao_a = self.mm(None, self.pf[2][ro_a:ro_a + 32, :],
                                             VT[:, stile * 528 + h_a * 33:
                                                stile * 528 + h_a * 33 + 32],
                                             esb[:, 0:512], False, False, tp=(0, ro_a),
                                             wait=[ce_a, c_v, cz1] if first else [ce_a])
                            c_dn_a = self.mm(None, self.pf[3][ro_a:ro_a + 1, :],
                                             VT[:, stile * 528 + h_a * 33 + 32:
                                                stile * 528 + h_a * 33 + 33],
                                             esb[:, 0:512], False, False, tp=(0, ro_a),
                                             wait=[cz2] if first else [])
                            self.setg(("esb", 0), c_dn_a)
                            c_ao_b = self.mm(None, self.pf[2][ro_b:ro_b + 32, :],
                                             VT[:, stile * 528 + h_b * 33:
                                                stile * 528 + h_b * 33 + 32],
                                             esb[:, 512:1024], False, False, tp=(0, ro_b),
                                             wait=[ce_b])
                            c_dn_b = self.mm(None, self.pf[3][ro_b:ro_b + 1, :],
                                             VT[:, stile * 528 + h_b * 33 + 32:
                                                stile * 528 + h_b * 33 + 33],
                                             esb[:, 512:1024], False, False, tp=(0, ro_b))
                            self.setg(("esb", 1), c_dn_b)
                    c_cl2 = self.mm(None, self.pf[2][:, :], sb["zerosq"][:],
                                    Q[:, 0:512], False, True)
                    c_cl3 = self.mm(None, self.pf[3][:, :], sb["zerosq"][:],
                                    Q[:, 0:512], False, True)
                    c_last_mm = c_cl3
                    cacts = []
                    for r in range(4):
                        c1 = self.act(dinv[r * 32:r * 32 + 1, :],
                                      self.pf[3][r * 32:r * 32 + 1, :], AF.Ln,
                                      wait=[c_last_mm, ("G", c_z[1])] if r == 0 else [])
                        c2 = self.act(dinv[r * 32:r * 32 + 1, :],
                                      dinv[r * 32:r * 32 + 1, :], AF.Exp, scale=-1.0)
                        cacts.append(c2)
                    self.setg(("pf", 3), cacts[-1])
                    cbc = self.mm(("pf", 4), self.pf[4][:, :], sb["dpat"][:],
                                  dinv[:, :], True, True, wait=[cacts[-1]])
                    cbv = self.cp("vector", dinvbc[:], self.pf[4][:, :],
                                  wait=[cbc], relkey=("pf", 4))
                    cev = self.tt("vector",
                                  aofm[:, qt * T_SEQ + tb * 512:
                                       qt * T_SEQ + (tb + 1) * 512],
                                  self.pf[2][:, :], dinvbc[:], ALU.mult,
                                  wait=[c_last_mm])
                    self.setg(("pf", 2), cev)
            c_ao = ("V", p.n["V"])
            self.tap(f"{tag}_ao", aofm[:, 0:T_SEQ], [128, T_SEQ], BF, [c_ao])

            if self.stop_after == "attnloop":
                stC.close()
                stB.close()
                self.fence()
                return
            mla = xln
            for cpt in range(NCT):
                for tb in range(4):
                    bk = ("pf", tb % 2)
                    cm = None
                    for vt in range(4):
                        cm = self.mm(bk if vt == 0 else None, self.pf[tb % 2][:, :],
                                     owf[:, vt * 1024 + cpt * 128:
                                         vt * 1024 + (cpt + 1) * 128],
                                     aofm[:, vt * T_SEQ + tb * 512:
                                          vt * T_SEQ + (tb + 1) * 512],
                                     vt == 0, vt == 3,
                                     wait=[ow_lds[vt], c_ao] if vt == 0 else [])
                    self.act(mla[:, cpt * T_SEQ + tb * 512: cpt * T_SEQ + (tb + 1) * 512],
                             self.pf[tb % 2][:, :], AF.Identity,
                             bias=self.col("o_b", cpt), wait=[cm], relkey=bk)
            c_mla = ("S", p.n["S"])
            self.tap(f"{tag}_mla", mla[:, 0:T_SEQ], [128, T_SEQ], BF, [c_mla])

            if self.stop_after == "mla":
                stC.close()
                stB.close()
                self.fence()
                return
            for tt in range(16):
                cm = None
                for ct in range(NCT):
                    cm = self.mm(("pf", 2) if ct == 0 else None, self.pf[2][0:128, 0:8],
                                 mla[:, ct * T_SEQ + tt * 128: ct * T_SEQ + (tt + 1) * 128],
                                 sb["routerw"][:, ct * 8:(ct + 1) * 8],
                                 ct == 0, False, wait=[c_mla] if ct == 0 else [])
                cm = self.mm(None, self.pf[2][0:128, 0:8], sb["onesrow"][0:1, 0:128],
                             self.row("router_b"), False, True)
                self.cp("vector", rt[:, 0:8], self.pf[2][0:128, 0:8], wait=[cm],
                        relkey=("pf", 2))
                self.red(rt[:, 8:9], rt[:, 0:8], ALU.max)
                c3 = self.ts("vector", rt[:, 9:10], rt[:, 8:9], -1.0, 0.0, ALU.mult,
                             ALU.add)
                c3b = self.act(rt[:, 16:24], rt[:, 0:8], AF.Exp, bias=rt[:, 9:10],
                               accum=rt[:, 10:11], wait=[c3])
                c4 = self.ts("vector", rt[:, 24:32], rt[:, 0:8], rt[:, 8:9], 0.0,
                             ALU.is_equal, ALU.add, wait=[c3b])
                self.stt("vector", rt[:, 32:40], rt[:, 24:32], -1e30, rt[:, 0:8],
                         ALU.mult, ALU.add)
                self.red(rt[:, 40:41], rt[:, 32:40], ALU.max)
                c7 = self.ts("vector", rt[:, 48:56], rt[:, 32:40], rt[:, 40:41], 0.0,
                             ALU.is_equal, ALU.add)
                c9 = self.act(rt[:, 42:43], rt[:, 40:41], AF.Exp, bias=rt[:, 9:10],
                              wait=[c7])
                c10 = self.ts("vector", rt[:, 43:44], rt[:, 10:11], 1e-8, 1.0,
                              ALU.mult, ALU.add, wait=[c9])
                self.tt("vector", rt[:, 43:44], rt[:, 43:44], rt[:, 42:43], ALU.add)
                self.recip(rt[:, 44:45], rt[:, 43:44])
                self.tt("vector", rt[:, 45:46], rt[:, 42:43], rt[:, 44:45], ALU.mult)
                self.ts("vector", rt[:, 24:32], rt[:, 24:32], rt[:, 44:45], 0.0,
                        ALU.mult, ALU.add)
                self.ts("vector", rt[:, 48:56], rt[:, 48:56], rt[:, 45:46], 0.0,
                        ALU.mult, ALU.add)
                self.tt("vector", comb[:, tt * 8:(tt + 1) * 8], rt[:, 24:32],
                        rt[:, 48:56], ALU.add)
            c_comb = ("V", p.n["V"])
            self.tap(f"{tag}_comb", comb[:], [128, 128], BF, [c_comb])
            for tt in range(16):
                cm = self.tr(("pbh", 1), self.pb[1][0:8, 0:128],
                             comb[:, tt * 8:(tt + 1) * 8], sb["identb"][:],
                             wait=[c_comb] if tt == 0 else [])
                self.cp("vector", combT[0:8, tt * 128:(tt + 1) * 128],
                        self.pb[1][0:8, 0:128], wait=[cm], relkey=("pbh", 1))
            c_combT = ("V", p.n["V"])

            if self.stop_after == "router":
                stC.close()
                stB.close()
                self.fence()
                return
            stC.close()
            stB.close()
            self.fence()
            h1b = E(nc.sbuf_tensor(f"{tag}_h1", [128, 16 * 1024], BF))
            wstr = E(nc.sbuf_tensor(f"{tag}_wstr", [128, 3 * 4096], BF))
            cbbc = E(nc.sbuf_tensor(f"{tag}_cbbc", [128, 2 * 512], BF))
            etmp = E(nc.sbuf_tensor(f"{tag}_etmp", [128, 512], BF))
            etmp2 = E(nc.sbuf_tensor(f"{tag}_etmp2", [128, 512], BF))
            gslot = [0]

            def gr_dma(dram_ap, gwait=()):
                s0 = gslot[0] % 3
                gslot[0] += 1
                dst = wstr[:, s0 * 4096:(s0 + 1) * 4096]
                g = self.guard(("wstr", s0))
                c = self.dma("D2", dst, dram_ap, wait=([g] if g else []) + list(gwait),
                             sem=f"dw{s0}_{tag}")
                return s0, c

            for tblk in range(2):
                for e in range(E_EXP):
                    for th in range(2):
                        tb = tblk * 2 + th
                        cm = self.mm(("pf", 4 + th), self.pf[4 + th][:, :],
                                     sb["esel"][0:8, e * 128:(e + 1) * 128],
                                     combT[0:8, tb * 512:(tb + 1) * 512], True, True,
                                     wait=[c_combT] if (tblk == 0 and e == 0) else [])
                        self.ts("vector", cbbc[:, th * 512:(th + 1) * 512],
                                self.pf[4 + th][:, :], 0.15, 0.0, ALU.mult, ALU.add,
                                wait=[cm], relkey=("pf", 4 + th))
                    c_cb = ("V", p.n["V"])
                    for fq in range(4):
                        s0, c_ld = gr_dma(
                            D["w1"][e].rearrange("(a p) f -> p a f", p=128)
                            [:, :, fq * 512:(fq + 1) * 512],
                            gwait=[c_combT] if (tblk == 0 and e == 0 and fq == 0) else ())
                        lastmm = None
                        for fi in range(4):
                            ft = fq * 4 + fi
                            for th in range(2):
                                tb = tblk * 2 + th
                                bki = (ft * 2 + th) % 2
                                cm = None
                                for ct in range(NCT):
                                    cm = self.mm(
                                        ("pf", bki) if ct == 0 else None,
                                        self.pf[bki][:, :],
                                        wstr[:, s0 * 4096 + ct * 512 + fi * 128:
                                             s0 * 4096 + ct * 512 + (fi + 1) * 128],
                                        mla[:, ct * T_SEQ + tb * 512:
                                            ct * T_SEQ + (tb + 1) * 512],
                                        ct == 0, ct == NCT - 1,
                                        wait=[c_ld, c_mla] if ct == 0 else [])
                                lastmm = cm
                                self.act(
                                    h1b[:, ft * 1024 + th * 512:
                                        ft * 1024 + (th + 1) * 512],
                                    self.pf[bki][:, :], AF.Gelu,
                                    bias=self.col(f"b1_{e}", ft), wait=[cm],
                                    relkey=("pf", bki))
                        self.setg(("wstr", s0), lastmm)
                    c_h1 = ("S", p.n["S"])
                    for fh in range(2):
                        for ch in range(2):
                            s0, c_ld = gr_dma(
                                D["w2"][e].rearrange("(a p) c -> p a c", p=128)
                                [:, fh * 8:(fh + 1) * 8, ch * 512:(ch + 1) * 512])
                            lastmm = None
                            for th in range(2):
                                tb = tblk * 2 + th
                                for cq in range(4):
                                    cp_ = ch * 4 + cq
                                    bki = 2 + (th * 4 + cq) % 2
                                    cm = None
                                    for fi in range(8):
                                        ft = fh * 8 + fi
                                        cm = self.mm(
                                            ("pf", bki) if fi == 0 else None,
                                            self.pf[bki][:, :],
                                            wstr[:, s0 * 4096 + fi * 512 + cq * 128:
                                                 s0 * 4096 + fi * 512 + (cq + 1) * 128],
                                            h1b[:, ft * 1024 + th * 512:
                                                ft * 1024 + (th + 1) * 512],
                                            fi == 0, fi == 7,
                                            wait=[c_ld, c_h1] if fi == 0 else [])
                                    lastmm = cm
                                    et = etmp if (th * 4 + cq) % 2 == 0 else etmp2
                                    if fh == 0:
                                        cs = self.act(et[:], self.pf[bki][:, :],
                                                      AF.Identity,
                                                      wait=[cm], relkey=("pf", bki))
                                    else:
                                        cs = self.act(et[:], self.pf[bki][:, :],
                                                      AF.Identity,
                                                      bias=self.col(f"b2_{e}", cp_),
                                                      wait=[cm], relkey=("pf", bki))
                                    self.tt("vector", et[:], et[:],
                                            cbbc[:, th * 512:(th + 1) * 512],
                                            ALU.mult,
                                            wait=[c_cb, cs])
                                    self.tt("vector",
                                            sb[f"x{cp_}"][:, tb * 512:(tb + 1) * 512],
                                            sb[f"x{cp_}"][:, tb * 512:(tb + 1) * 512],
                                            et[:], ALU.add)
                            self.setg(("wstr", s0), lastmm)
            self.tap(f"{tag}_x", sb["x0"][:], [128, T_SEQ], F32, [("V", p.n["V"])])
        self.fence()

    def attn_phase(self):
        nc, p, sb, D = self.nc, self.p, self.sb, self.D
        SEGA = 256
        NSEGA = T_SEQ // SEGA
        NSTL = SEGA // 128
        NCH = SEGA // 32
        self.fence()
        with ExitStack() as st:
            E = st.enter_context
            for nm, shp in (("cm_a", [128, SEGA]), ("gnpat", [128, 128])):
                sb[nm] = E(nc.sbuf_tensor("sb_" + nm, shp, BF))
            for nm, shp in (("mask_tu", [128, 256]), ("mask_sl", [128, 128])):
                sb[nm] = E(nc.sbuf_tensor("sb_" + nm, shp, F32))
            sb["gnpatTf"] = E(nc.sbuf_tensor("sb_gnpatTf", [16, 1024], F32))
            self.dma("D2", sb["cm_a"][:], D["cm_a_src"][:, 0:SEGA], sem="dlr")
            self.dma("D2", sb["gnpat"][:], D["gnpat_src"], sem="dlr")
            for nm in ("mask_tu", "mask_sl", "gnpatTf"):
                srcn = "gnpatT_src" if nm == "gnpatTf" else nm + "_src"
                self.dma("D1", sb[nm][:], D[srcn], sem="dlr2")
            ncols = E(nc.sbuf_tensor("att_ncols", [128, 8], F32))
            _i0 = self.colidx["w_miu"]
            c_neg = self.ts("vector", ncols[:, 0:8], self.sb["cols"][:, _i0:_i0 + 8],
                            -1.0, 0.0,
                            ALU.mult, ALU.add)
            SW = NCT * SEGA
            rS = E(nc.sbuf_tensor("a_rS", [128, SW], BF))
            vS = E(nc.sbuf_tensor("a_vS", [128, SW], BF))
            kapS = E(nc.sbuf_tensor("a_kapS", [128, SW], BF))
            kkS = E(nc.sbuf_tensor("a_kkS", [128, SW], BF))
            bS = E(nc.sbuf_tensor("a_bS", [128, SW], BF))
            gS = E(nc.sbuf_tensor("a_gS", [128, SW], BF))
            wSf = E(nc.sbuf_tensor("a_wSf", [128, SW], F32))
            xdP = E(nc.sbuf_tensor("a_xdP", [128, NCT * SEGA], BF))
            mixA = E(nc.sbuf_tensor("a_mixA", [128, SEGA], BF))
            mixB = E(nc.sbuf_tensor("a_mixB", [64, SEGA], BF))
            bsum = E(nc.sbuf_tensor("a_bsum", [16, SEGA], F32))
            gnr = E(nc.sbuf_tensor("a_gnr", [16, 3 * SEGA], F32))
            stS = E(nc.sbuf_tensor("a_stS", [128, NCT * 64], BF))
            PLs = E(nc.sbuf_tensor("a_PLs", [128, NCT * NCH], F32))
            prevcol = E(nc.sbuf_tensor("a_prev", [128, NCT], BF))
            wstrip = E(nc.sbuf_tensor("a_wstrip", [128, 2 * 1024], BF))
            t64 = E(nc.sbuf_tensor("a_t64", [128, SEGA], BF))
            ag = E(nc.sbuf_tensor("a_ag", [128, SEGA], BF))
            miu = E(nc.sbuf_tensor("a_miu", [128, SEGA], BF))
            t1g = E(nc.sbuf_tensor("a_t1g", [128, SEGA], BF))
            l1t = E(nc.sbuf_tensor("a_l1t", [128, SEGA], F32))
            sb1f = E(nc.sbuf_tensor("a_sb1f", [128, SEGA], F32))
            yS = kkS

            czs = self.p.op("gpsimd", lambda g, a=stS[:]: g.memzero(a), inc="G")
            czp = self.p.op("gpsimd", lambda g, a=prevcol[:]: g.memzero(a), inc="G")
            self._st_init = False

            def strip_dma(mat, rb, slot, wait=()):
                ap = self.scr[mat][:, rb].transpose([1, 0, 2])
                g = self.guard(("astrip", slot))
                dst = wstrip[:, slot * 1024:(slot + 1) * 1024].rearrange(
                    "p (a f) -> p a f", f=128)
                return self.dma("D1", dst, ap,
                                wait=([g] if g else []) + list(wait),
                                sem=f"da{slot}")

            for seg in range(NSEGA):
                s0, s1g = seg * SEGA, (seg + 1) * SEGA
                segsl = lambda ct: slice(ct * SEGA, (ct + 1) * SEGA)
                stD = ExitStack()
                ED = stD.enter_context
                xlnS = ED(nc.sbuf_tensor(f"a_xlnS{seg}", [128, SW], BF))
                dxS = ED(nc.sbuf_tensor(f"a_dxS{seg}", [128, SW], BF))
                sc = dict(
                    xbf=ED(nc.sbuf_tensor(f"a_xbf{seg}", [128, 2 * SEGA], BF)),
                    xsq=ED(nc.sbuf_tensor(f"a_xsq{seg}", [128, 2 * SEGA], BF)),
                    t1f=ED(nc.sbuf_tensor(f"a_t1f{seg}", [128, SEGA], F32)),
                    rows=ED(nc.sbuf_tensor(f"a_lnr{seg}", [1, 5 * SEGA], F32)),
                )
                lr = {}
                for nm, shp, dsrc in (("xaw", [128, 8 * 192], "x_A"),
                                      ("kaw", [128, 8 * 64], "k_A"),
                                      ("waw", [128, 8 * 64], "w_A"),
                                      ("aaw", [128, 8 * 64], "a_A"),
                                      ("gaw", [128, 8 * 128], "g_A")):
                    lr[nm] = ED(nc.sbuf_tensor(f"{nm}{seg}", shp, BF))
                    w = shp[1] // 8
                    for ct in range(NCT):
                        self.dma("D2", lr[nm][:, ct * w:(ct + 1) * w],
                                 D[dsrc][ct * 128:(ct + 1) * 128, :], sem="dlr")
                for nm, shp, dsrc in (("xbA", [128, 1024], None),
                                      ("xbB", [64, 1024], None),
                                      ("kbw", [64, 1024], "k_B"),
                                      ("wbw", [64, 1024], "w_B"),
                                      ("abw", [64, 1024], "a_B"),
                                      ("gbw", [128, 1024], "g_B")):
                    lr[nm] = ED(nc.sbuf_tensor(f"{nm}{seg}", shp, BF))
                    if dsrc is not None:
                        self.dma("D2", lr[nm][:], D[dsrc], sem="dlr")
                self.dma("D2", lr["xbA"][:], D["x_Bp"][0:128, :], sem="dlr")
                self.dma("D2", lr["xbB"][:], D["x_Bp"][128:192, :], sem="dlr")
                c_lr = ("dlr", p.n["dlr"])

                c_ln = self.emit_ln_seg(xlnS, s0, sc, SEGA)
                lastdx = None
                for ct in range(NCT):
                    self.tt("vector", dxS[:, ct * SEGA + 1: ct * SEGA + SEGA],
                            xlnS[:, ct * SEGA: ct * SEGA + SEGA - 1],
                            xlnS[:, ct * SEGA + 1: ct * SEGA + SEGA], ALU.subtract)
                    lastdx = self.tt("vector", dxS[:, ct * SEGA: ct * SEGA + 1],
                                     prevcol[:, ct:ct + 1],
                                     xlnS[:, ct * SEGA: ct * SEGA + 1], ALU.subtract,
                                     wait=[("G", czp[1])] if seg == 0 else [])
                    self.cp("vector", prevcol[:, ct:ct + 1],
                            xlnS[:, ct * SEGA + SEGA - 1: ct * SEGA + SEGA])
                segtw = [("T", p.n["T"])]
                for ct in range(NCT):
                    self.ts("vector", t1g[:], dxS[:, segsl(ct)], self.col("x_miu", ct),
                            0.0, ALU.mult, ALU.add, wait=segtw)
                    segtw = []
                    self.tt("vector", xdP[:, segsl(ct)], t1g[:],
                            xlnS[:, segsl(ct)], ALU.add)
                c_xm = ("V", p.n["V"])
                cmA = cmB = None
                for ct in range(NCT):
                    cmA = self.mm(("pf", 0) if ct == 0 else None, self.pf[0][:, 0:SEGA],
                                  lr["xaw"][:, ct * 192: ct * 192 + 128],
                                  xdP[:, segsl(ct)], ct == 0, ct == 7,
                                  wait=[c_lr, c_xm] if ct == 0 else [])
                for ct in range(NCT):
                    cmB = self.mm(("pf", 1) if ct == 0 else None, self.pf[1][0:64, 0:SEGA],
                                  lr["xaw"][:, ct * 192 + 128: ct * 192 + 192],
                                  xdP[:, segsl(ct)], ct == 0, ct == 7)
                self.act(mixA[:], self.pf[0][:, 0:SEGA], AF.Tanh, wait=[cmA],
                         relkey=("pf", 0))
                c_mix = self.act(mixB[0:64, :], self.pf[1][0:64, 0:SEGA], AF.Tanh,
                                 wait=[cmB], relkey=("pf", 1))

                def xd_build(g6, cdep):
                    for ct in range(NCT):
                        bk = ct % 2
                        if g6 < 4:
                            lhs = lr["xbA"][g6 * 32:(g6 + 1) * 32,
                                            ct * 128:(ct + 1) * 128]
                            rhs = mixA[g6 * 32:(g6 + 1) * 32, :]
                            tp = (g6 * 32, 0)
                        else:
                            lhs = lr["xbB"][(g6 - 4) * 32:(g6 - 3) * 32,
                                            ct * 128:(ct + 1) * 128]
                            rhs = mixB[(g6 - 4) * 32:(g6 - 3) * 32, :]
                            tp = ((g6 - 4) * 32, 0)
                        cm = self.mm(("pf", 2 + bk), self.pf[2 + bk][:, 0:SEGA],
                                     lhs, rhs, True, True, tp=tp,
                                     wait=[cdep] if ct == 0 else [])
                        cs = self.act(miu[:], self.pf[2 + bk][:, 0:SEGA], AF.Identity,
                                      bias=self.col(f"lam{g6}", ct), wait=[cm],
                                      relkey=("pf", 2 + bk))
                        cg = self.tt("gpsimd", t1g[:], dxS[:, segsl(ct)], miu[:],
                                     ALU.mult, wait=[cs])
                        self.tt("vector", xdP[:, segsl(ct)],
                                t1g[:], xlnS[:, segsl(ct)], ALU.add, wait=[cg])
                    return ("V", p.n["V"])

                def cxc_proj(mat, cdep=None):
                    for rb in range(NCT):
                        slot = rb % 2
                        c_st = strip_dma(mat, rb, slot)
                        bk = rb % 2
                        cm = None
                        for ct in range(NCT):
                            w0 = [c_st] if ct == 0 else []
                            if ct == 0 and rb == 0 and cdep is not None:
                                w0.append(cdep)
                            cm = self.mm(("pf", bk) if ct == 0 else None,
                                         self.pf[bk][:, 0:SEGA],
                                         wstrip[:, slot * 1024 + ct * 128:
                                                slot * 1024 + (ct + 1) * 128],
                                         xdP[:, segsl(ct)], ct == 0, False,
                                         wait=w0)
                        yield rb, bk, cm
                        self.setg(("astrip", slot), self.p.guards.get(("pf", bk)))

                c_xd = xd_build(2, c_mix)
                cm = None
                for ct in range(NCT):
                    cm = self.mm(("pf", 4) if ct == 0 else None, self.pf[4][0:64, 0:SEGA],
                                 lr["kaw"][:, ct * 64:(ct + 1) * 64],
                                 xdP[:, segsl(ct)], ct == 0, ct == 7,
                                 wait=[c_xd] if ct == 0 else [])
                c_t64 = self.act(t64[0:64, :], self.pf[4][0:64, 0:SEGA], AF.Tanh,
                                 wait=[cm], relkey=("pf", 4))
                for rb, bk, cm in cxc_proj("kWT"):
                    cm2 = self.mm(None, self.pf[bk][:, 0:SEGA],
                                  lr["kbw"][0:64, rb * 128:(rb + 1) * 128],
                                  t64[0:64, :], False, True, wait=[c_t64])
                    self.cp("vector", kapS[:, segsl(rb)], self.pf[bk][:, 0:SEGA],
                            wait=[cm2], relkey=("pf", bk))
                c_kk0 = ("V", p.n["V"])
                for ct in range(NCT):
                    q0 = ct % 2
                    gq = self.guard(("ksq", q0))
                    dst = t1g[:] if q0 == 0 else miu[:]
                    self.act(dst, kapS[:, segsl(ct)], AF.Square,
                             wait=([c_kk0] if ct == 0 else []) + ([gq] if gq else []))
                    cmk = self.mm(("pf", 4) if ct == 0 else None, self.pf[4][0:1, 0:SEGA],
                                  sb["onescol"][:], dst, ct == 0, ct == 7,
                                  wait=[("S", p.n["S"])])
                    self.setg(("ksq", q0), cmk)
                cr = sc["rows"]
                self.act(cr[0:1, 0:SEGA], self.pf[4][0:1, 0:SEGA], AF.Ln,
                         wait=[("T", p.n["T"])], relkey=("pf", 4))
                self.act(cr[0:1, 0:SEGA], cr[0:1, 0:SEGA], AF.Exp, scale=0.5)
                c_nr = self.ts("vector", cr[0:1, SEGA:2 * SEGA], cr[0:1, 0:SEGA],
                               1e-12, 0.0, ALU.max, ALU.add,
                               wait=[("S", p.n["S"])])
                c_rec = self.recip(cr[0:1, 2 * SEGA:3 * SEGA], cr[0:1, SEGA:2 * SEGA])
                c_bc = self.bcrow(("pf", 5), self.pf[5][:, 0:SEGA],
                                  cr[0:1, 2 * SEGA:3 * SEGA], wait=[c_rec])
                for ct in range(NCT):
                    self.tt("vector", kapS[:, segsl(ct)], kapS[:, segsl(ct)],
                            self.pf[5][:, 0:SEGA], ALU.mult,
                            wait=[c_bc] if ct == 0 else [])
                c_kap = ("V", p.n["V"])
                self.setg(("pf", 5), c_kap)

                c_xd = xd_build(1, c_kap)
                cm = None
                for ct in range(NCT):
                    cm = self.mm(("pf", 4) if ct == 0 else None, self.pf[4][0:64, 0:SEGA],
                                 lr["waw"][:, ct * 64:(ct + 1) * 64],
                                 xdP[:, segsl(ct)], ct == 0, ct == 7,
                                 wait=[c_xd] if ct == 0 else [])
                c_t64 = self.act(t64[0:64, :], self.pf[4][0:64, 0:SEGA], AF.Tanh,
                                 wait=[cm], relkey=("pf", 4))
                if seg == 7:
                    self.tap("w_t64", t64[0:64, :], [64, SEGA], BF, [c_t64])
                for rb in range(NCT):
                    bk = 2 + rb % 2
                    cm2 = self.mm(("pf", bk), self.pf[bk][:, 0:SEGA],
                                  lr["wbw"][0:64, rb * 128:(rb + 1) * 128],
                                  t64[0:64, :], True, True, wait=[c_t64])
                    wtap = []
                    if seg == 7 and rb == 1 and "w_pf1" in self.taps:
                        ctp = self.cp("vector", sc["t1f"][:], self.pf[bk][:, 0:SEGA],
                                      wait=[cm2])
                        self.tap("w_pf1", sc["t1f"][:], [128, SEGA], F32, [ctp])
                        wtap = [("dt", p.n["dt"])]
                    self.act(l1t[:], self.pf[bk][:, 0:SEGA], AF.Exp, scale=-1.0,
                             bias=ncols[:, rb:rb + 1], wait=[cm2, c_neg] + wtap,
                             relkey=("pf", bk))
                    if seg == 7 and rb == 1:
                        self.tap("w_l1e", l1t[:], [128, SEGA], F32,
                                 [("S", p.n["S"])])
                    cl1 = self.act(l1t[:], l1t[:], AF.Ln, bias=self.col("c_one"))
                    self.ts("vector", wSf[:, segsl(rb)], l1t[:], -1.0, -0.5,
                            ALU.mult, ALU.add, wait=[cl1])
                    ce2 = self.act(t1g[:], l1t[:], AF.Exp, scale=-0.5,
                                   bias=self.col("c_mquart"))
                    self.tt("vector", kkS[:, segsl(rb)], kapS[:, segsl(rb)], t1g[:],
                            ALU.mult, wait=[ce2])
                c_w = ("V", p.n["V"])

                c_xd = xd_build(4, c_w)
                cm = None
                for ct in range(NCT):
                    cm = self.mm(("pf", 4) if ct == 0 else None, self.pf[4][0:64, 0:SEGA],
                                 lr["aaw"][:, ct * 64:(ct + 1) * 64],
                                 xdP[:, segsl(ct)], ct == 0, ct == 7,
                                 wait=[c_xd] if ct == 0 else [])
                c_t64 = self.act(t64[0:64, :], self.pf[4][0:64, 0:SEGA], AF.Identity,
                                 wait=[cm], relkey=("pf", 4))
                for rb in range(NCT):
                    bk = 2 + rb % 2
                    cm2 = self.mm(("pf", bk), self.pf[bk][:, 0:SEGA],
                                  lr["abw"][0:64, rb * 128:(rb + 1) * 128],
                                  t64[0:64, :], True, True, wait=[c_t64])
                    wtap = []
                    if seg == 7 and rb == 0 and "a_pf0" in self.taps:
                        ctp = self.cp("vector", sc["t1f"][:], self.pf[bk][:, 0:SEGA],
                                      wait=[cm2])
                        self.tap("a_pf0", sc["t1f"][:], [128, SEGA], F32, [ctp])
                        self.tap("a_t64", t64[0:64, :], [64, SEGA], BF, [ctp])
                        wtap = [("dt", p.n["dt"])]
                    ca = self.act(ag[:], self.pf[bk][:, 0:SEGA], AF.Sigmoid,
                                  bias=self.col("a_time", rb), wait=[cm2] + wtap,
                                  relkey=("pf", bk))
                    if seg == 7 and rb == 0:
                        self.tap("a_ag0", ag[:], [128, SEGA], BF, [ca])
                    self.stt("vector", bS[:, segsl(rb)], kapS[:, segsl(rb)], 2.0,
                             ag[:], ALU.mult, ALU.mult, wait=[ca])
                c_b = ("V", p.n["V"])

                c_xd = xd_build(5, c_b)
                cm = None
                for ct in range(NCT):
                    cm = self.mm(("pf", 4) if ct == 0 else None, self.pf[4][:, 0:SEGA],
                                 lr["gaw"][:, ct * 128:(ct + 1) * 128],
                                 xdP[:, segsl(ct)], ct == 0, ct == 7,
                                 wait=[c_xd] if ct == 0 else [])
                c_tg = self.act(t64[:], self.pf[4][:, 0:SEGA], AF.Tanh, wait=[cm],
                                relkey=("pf", 4))
                for rb in range(NCT):
                    bk = 2 + rb % 2
                    cm2 = self.mm(("pf", bk), self.pf[bk][:, 0:SEGA],
                                  lr["gbw"][:, rb * 128:(rb + 1) * 128],
                                  t64[:], True, True, wait=[c_tg])
                    self.cp("vector", gS[:, segsl(rb)], self.pf[bk][:, 0:SEGA],
                            wait=[cm2], relkey=("pf", bk))
                c_xd = xd_build(3, ("V", p.n["V"]))
                for rb, bk, cm in cxc_proj("vWT", c_xd):
                    cmv = self.mm(None, self.pf[bk][:, 0:SEGA], sb["zerosq"][:],
                                  xdP[:, 0:SEGA], False, True)
                    self.cp("vector", vS[:, segsl(rb)], self.pf[bk][:, 0:SEGA],
                            wait=[cmv], relkey=("pf", bk))
                c_xd = xd_build(0, ("V", p.n["V"]))
                for rb, bk, cm in cxc_proj("rWT", c_xd):
                    cmr = self.mm(None, self.pf[bk][:, 0:SEGA], sb["zerosq"][:],
                                  xdP[:, 0:SEGA], False, True)
                    self.cp("vector", rS[:, segsl(rb)], self.pf[bk][:, 0:SEGA],
                            wait=[cmr], relkey=("pf", bk))
                c_proj = ("V", p.n["V"])
                if seg == 0:
                    for nm_, tn_ in (("a_rS", rS), ("a_kapS", kapS), ("a_vS", vS),
                                     ("a_gS", gS), ("a_bS", bS), ("a_kkS", kkS)):
                        self.tap(nm_, tn_[:], [128, SW], BF, [c_proj])
                    self.tap("a_wSf", wSf[:], [128, SW], F32, [c_proj])

                cm = None
                for ct in range(NCT):
                    q0 = ct % 2
                    dst = t1g[:] if q0 == 0 else miu[:]
                    gq = self.guard(("bns", q0))
                    self.tt("vector", dst, rS[:, segsl(ct)], kapS[:, segsl(ct)],
                            ALU.mult,
                            wait=([c_proj] if ct == 0 else []) + ([gq] if gq else []))
                    self.ts("vector", dst, dst, self.col("xtime", ct), 0.0,
                            ALU.mult, ALU.add)
                    cm = self.mm(("pf", 4) if ct == 0 else None, self.pf[4][0:16, 0:SEGA],
                                 sb["gnpat"][:, ct * 16:(ct + 1) * 16], dst,
                                 ct == 0, ct == 7, wait=[("V", p.n["V"]), c_lr])
                    self.setg(("bns", q0), cm)
                self.cp("vector", bsum[0:16, :], self.pf[4][0:16, 0:SEGA], wait=[cm],
                        relkey=("pf", 4))
                stD.close()
                self.fence()

                stE = ExitStack()
                EE = stE.enter_context
                WWr = EE(nc.sbuf_tensor(f"a_WWr{seg}", [128, 512], BF))
                UUr = EE(nc.sbuf_tensor(f"a_UUr{seg}", [128, 512], BF))
                Wb = EE(nc.sbuf_tensor(f"a_Wb{seg}", [128, 256], BF))
                SQ = EE(nc.sbuf_tensor(f"a_SQ{seg}", [128, 14 * 128], BF))
                Xsb = EE(nc.sbuf_tensor(f"a_Xsb{seg}", [128, 256], BF))
                Vtok = EE(nc.sbuf_tensor(f"a_Vtok{seg}", [128, 128], BF))
                BKt = EE(nc.sbuf_tensor(f"a_BKt{seg}", [128, 256], BF))
                Gfm = EE(nc.sbuf_tensor(f"a_Gfm{seg}", [128, 128], BF))
                Ztok = EE(nc.sbuf_tensor(f"a_Ztok{seg}", [128, 128], BF))
                cwF = EE(nc.sbuf_tensor(f"a_cwF{seg}", [128, SEGA], F32))
                cwP = EE(nc.sbuf_tensor(f"a_cwP{seg}", [128, SEGA], F32))
                subt = EE(nc.sbuf_tensor(f"a_subt{seg}", [128, SEGA], F32))
                crefb = EE(nc.sbuf_tensor(f"a_crefb{seg}", [128, SEGA], F32))
                refc = EE(nc.sbuf_tensor(f"a_refc{seg}", [128, 2 * NCH], F32))
                fPrev = EE(nc.sbuf_tensor(f"a_fPrev{seg}", [128, SEGA], BF))
                fIncl = EE(nc.sbuf_tensor(f"a_fIncl{seg}", [128, SEGA], BF))
                fHatP = EE(nc.sbuf_tensor(f"a_fHatP{seg}", [128, SEGA], BF))
                fHatI = EE(nc.sbuf_tensor(f"a_fHatI{seg}", [128, SEGA], BF))
                fTil = EE(nc.sbuf_tensor(f"a_fTil{seg}", [128, SEGA], BF))
                fBar = EE(nc.sbuf_tensor(f"a_fBar{seg}", [128, SEGA], BF))
                oA = EE(nc.sbuf_tensor(f"a_oA{seg}", [128, SEGA], BF))
                oAh = EE(nc.sbuf_tensor(f"a_oAh{seg}", [128, SEGA], BF))
                oR = EE(nc.sbuf_tensor(f"a_oR{seg}", [128, SEGA], BF))
                oRh = EE(nc.sbuf_tensor(f"a_oRh{seg}", [128, SEGA], BF))
                oBt = EE(nc.sbuf_tensor(f"a_oBt{seg}", [128, SEGA], BF))
                oKt = EE(nc.sbuf_tensor(f"a_oKt{seg}", [128, SEGA], BF))
                oBb = EE(nc.sbuf_tensor(f"a_oBb{seg}", [128, SEGA], BF))
                oKb = EE(nc.sbuf_tensor(f"a_oKb{seg}", [128, SEGA], BF))

                for ct in range(NCT):
                    csl = segsl(ct)
                    prep_tw = ("T", p.n["T"])
                    c_sc = self.scan(cwF[:], sb["cm_a"][:], wSf[:, csl], 0.0,
                                     ALU.mult, ALU.add,
                                     wait=[c_proj, c_lr, prep_tw])
                    self.tt("vector", cwP[:], cwF[:], wSf[:, csl], ALU.subtract)
                    self.cp("vector", refc[:, 0:NCH],
                            cwF[:].rearrange("q (c w) -> q c w", w=32)[:, :, 15:16]
                            .squeeze(2))
                    self.cp("vector", refc[:, NCH:2 * NCH],
                            cwF[:].rearrange("q (c w) -> q c w", w=32)[:, :, 31:32]
                            .squeeze(2))
                    self.act(PLs[:, ct * NCH:(ct + 1) * NCH], refc[:, NCH:2 * NCH],
                             AF.Exp, wait=[("V", p.n["V"])])
                    cbd = self.cp("vector",
                                  crefb[:].rearrange("q (c w) -> q c w", w=32),
                                  refc[:, 0:NCH].rearrange("q (c w) -> q c w", w=1)
                                  .broadcast_to([128, NCH, 32]))
                    c1 = self.tt("vector", subt[:], cwP[:], crefb[:], ALU.subtract)
                    self.act(fHatP[:], subt[:], AF.Exp, bias=self.col("c_m11"),
                             wait=[c1])
                    c2 = self.tt("vector", subt[:], cwF[:], crefb[:], ALU.subtract,
                                 wait=[("S", p.n["S"])])
                    self.act(fHatI[:], subt[:], AF.Exp, bias=self.col("c_m11"),
                             wait=[c2])
                    c2b = self.act(fTil[:], subt[:], AF.Exp, scale=-1.0,
                                   bias=self.col("c_m11"))
                    cbd2 = self.cp("vector",
                                   crefb[:].rearrange("q (c w) -> q c w", w=32),
                                   refc[:, NCH:2 * NCH]
                                   .rearrange("q (c w) -> q c w", w=1)
                                   .broadcast_to([128, NCH, 32]), wait=[c2b])
                    c3 = self.tt("vector", subt[:], crefb[:], cwF[:], ALU.subtract)
                    self.act(fBar[:], subt[:], AF.Exp, wait=[c3])
                    self.act(fPrev[:], cwP[:], AF.Exp)
                    c_f = self.act(fIncl[:], cwF[:], AF.Exp)
                    self.stt("vector", oA[:], kapS[:, csl], -1.0, fPrev[:],
                             ALU.mult, ALU.mult, wait=[c_f, prep_tw])
                    self.stt("vector", oAh[:], kapS[:, csl], -1.0, fHatP[:],
                             ALU.mult, ALU.mult)
                    self.tt("gpsimd", oR[:], rS[:, csl], fIncl[:], ALU.mult)
                    self.tt("gpsimd", oRh[:], rS[:, csl], fHatI[:], ALU.mult)
                    self.tt("vector", oBt[:], bS[:, csl], fTil[:], ALU.mult)
                    self.tt("vector", oKt[:], kkS[:, csl], fTil[:], ALU.mult)
                    self.tt("vector", oBb[:], bS[:, csl], fBar[:], ALU.mult)
                    c_ops = self.tt("vector", oKb[:], kkS[:, csl], fBar[:], ALU.mult)
                    c_prep = [c_ops, ("G", p.n["G"]), ("S", p.n["S"])]

                    for stl in range(NSTL):
                        scol = slice(stl * 128, (stl + 1) * 128)
                        w0 = c_prep if stl == 0 else []
                        cts = []
                        for hh, po in ((0, 0), (1, 64)):
                            cts.append(self.tr(("pb", 0) if hh == 0 else None,
                                               self.pb[0][:, hh * 64:hh * 64 + 64],
                                               vS[po:po + 64, ct * SEGA + stl * 128:
                                                  ct * SEGA + (stl + 1) * 128],
                                               sb["identb"][po:po + 64, po:po + 64],
                                               wait=w0))
                        cv = self.cp("vector", Vtok[:], self.pb[0][:, 0:128],
                                     wait=[cts[-1]], relkey=("pb", 0))
                        fm = []
                        for hh, po in ((0, 0), (1, 64)):
                            fm.append(self.mm(("pf", 1) if hh == 0 else None,
                                              self.pf[1][:, hh * 256: hh * 256 + 128],
                                              oBt[po:po + 64, scol],
                                              oAh[po:po + 64, scol], True, True,
                                              tp=(po, 0), wait=w0))
                            fm.append(self.mm(None,
                                              self.pf[1][:, hh * 256 + 128: hh * 256 + 256],
                                              oBt[po:po + 64, scol],
                                              oRh[po:po + 64, scol], True, True,
                                              tp=(po, 0)))
                        self.tt("vector", WWr[:, 0:256], self.pf[1][:, 0:256],
                                sb["mask_tu"][:], ALU.mult,
                                wait=[fm[-1], ("dlr2", 48)])
                        cw2 = self.tt("vector", WWr[:, 256:512], self.pf[1][:, 256:512],
                                      sb["mask_tu"][:], ALU.mult, relkey=("pf", 1))
                        fu = []
                        for hh, po in ((0, 0), (1, 64)):
                            fu.append(self.mm(("pf", 0) if hh == 0 else None,
                                              self.pf[0][:, hh * 256: hh * 256 + 128],
                                              oKt[po:po + 64, scol],
                                              oAh[po:po + 64, scol], True, True,
                                              tp=(po, 0)))
                            fu.append(self.mm(None,
                                              self.pf[0][:, hh * 256 + 128: hh * 256 + 256],
                                              oKt[po:po + 64, scol],
                                              oRh[po:po + 64, scol], True, True,
                                              tp=(po, 0)))
                        self.tt("vector", UUr[:, 0:256], self.pf[0][:, 0:256],
                                sb["mask_tu"][:], ALU.mult, wait=[fu[-1]])
                        cu2 = self.tt("vector", UUr[:, 256:512], self.pf[0][:, 256:512],
                                      sb["mask_tu"][:], ALU.mult, relkey=("pf", 0))
                        fw = []
                        for hh, po in ((0, 0), (1, 64)):
                            fw.append(self.mm(("pf", 1) if hh == 0 else None,
                                              self.pf[1][:, hh * 128: hh * 128 + 128],
                                              oAh[po:po + 64, scol],
                                              oBt[po:po + 64, scol], True, True,
                                              tp=(po, 0)))
                        self.tt("vector", Wb[:, 0:128], self.pf[1][:, 0:128],
                                sb["mask_sl"][:], ALU.mult, wait=[fw[-1]])
                        cwb = self.tt("vector", Wb[:, 128:256], self.pf[1][:, 128:256],
                                      sb["mask_sl"][:], ALU.mult, relkey=("pf", 1))
                        cta = []
                        ctu = []
                        for hh, po in ((0, 0), (1, 64)):
                            cta.append(self.tr(("pb", 0) if hh == 0 else None,
                                               self.pb[0][:, hh * 64: hh * 64 + 64],
                                               oA[po:po + 64, scol],
                                               sb["identb"][po:po + 64, po:po + 64],
                                               wait=[cv] if hh == 0 else []))
                            ctu.append(self.mm(("pf", 0) if hh == 0 else None,
                                               self.pf[0][:, hh * 64: hh * 64 + 64],
                                               UUr[:, hh * 256: hh * 256 + 128],
                                               Vtok[:, hh * 64: hh * 64 + 64],
                                               True, True,
                                               wait=[cu2, cv] if hh == 0 else []))
                        for hh in range(2):
                            self.cp("vector", Xsb[:, hh * 128: hh * 128 + 64],
                                    self.pb[0][:, hh * 64: hh * 64 + 64],
                                    wait=[cta[-1]] if hh == 0 else [])
                            self.cp("vector", Xsb[:, hh * 128 + 64: hh * 128 + 128],
                                    self.pf[0][:, hh * 64: hh * 64 + 64],
                                    wait=[ctu[-1]] if hh == 0 else [])
                        cx = ("V", p.n["V"])
                        self.setg(("pb", 0), cx)
                        self.setg(("pf", 0), cx)
                        ctb = []
                        for hh, po in ((0, 0), (1, 64)):
                            ctb.append(self.tr(("pb", 0) if hh == 0 else None,
                                               self.pb[0][:, hh * 64: hh * 64 + 64],
                                               oBb[po:po + 64, scol],
                                               sb["identb"][po:po + 64, po:po + 64]))
                            ctb.append(self.tr(None,
                                               self.pb[0][:, 128 + hh * 64:
                                                          128 + hh * 64 + 64],
                                               oKb[po:po + 64, scol],
                                               sb["identb"][po:po + 64, po:po + 64]))
                        cbk = self.cp("vector", BKt[:], self.pb[0][:, 0:256],
                                      wait=[ctb[-1]], relkey=("pb", 0))
                        self.setg(("pb", 0), cbk)
                        prevT = [(WWr, 0), (WWr, 256)]
                        prevU = [(Wb, 0), (Wb, 128)]
                        sqoff = 0
                        for lv in range(3):
                            mms = []
                            for hh in range(2):
                                tT, oT = prevT[hh]
                                tU, oU = prevU[hh]
                                mms.append(self.mm(("pf", 1) if hh == 0 else None,
                                                   self.pf[1][:, hh * 256: hh * 256 + 128],
                                                   tT[:, oT:oT + 128],
                                                   tU[:, oU:oU + 128], True, True,
                                                   wait=[cwb, cw2] if (lv == 0 and hh == 0) else []))
                                mms.append(self.mm(None,
                                                   self.pf[1][:, hh * 256 + 128: hh * 256 + 256],
                                                   tU[:, oU:oU + 128],
                                                   tT[:, oT:oT + 128], True, True))
                            self.cp("vector", SQ[:, sqoff:sqoff + 512],
                                    self.pf[1][:, 0:512], wait=[mms[-1]],
                                    relkey=("pf", 1))
                            prevU = [(SQ, sqoff), (SQ, sqoff + 256)]
                            prevT = [(SQ, sqoff + 128), (SQ, sqoff + 384)]
                            sqoff += 512
                        mms = []
                        for hh in range(2):
                            tT, oT = prevT[hh]
                            tU, oU = prevU[hh]
                            mms.append(self.mm(("pf", 1) if hh == 0 else None,
                                               self.pf[1][:, hh * 128: hh * 128 + 128],
                                               tU[:, oU:oU + 128],
                                               tT[:, oT:oT + 128], True, True))
                        c16 = self.cp("vector", SQ[:, 1536:1792],
                                      self.pf[1][:, 0:256], wait=[mms[-1]],
                                      relkey=("pf", 1))
                        applyT = [[(WWr, 0), (WWr, 256)],
                                  [(SQ, 128), (SQ, 384)],
                                  [(SQ, 640), (SQ, 896)],
                                  [(SQ, 1152), (SQ, 1408)],
                                  [(SQ, 1536), (SQ, 1664)]]
                        for kk_ in range(5):
                            mms = []
                            for hh in range(2):
                                tT, oT = applyT[kk_][hh]
                                mms.append(self.mm(("pf", 0) if hh == 0 else None,
                                                   self.pf[0][:, hh * 128: hh * 128 + 128],
                                                   tT[:, oT:oT + 128],
                                                   Xsb[:, hh * 128: hh * 128 + 128],
                                                   True, True,
                                                   wait=[cx, c16] if (kk_ == 0 and hh == 0) else []))
                            self.tt("vector", Xsb[:], self.pf[0][:, 0:256], Xsb[:],
                                    ALU.add, wait=[mms[-1]], relkey=("pf", 0))
                        ctg = []
                        for hh in range(2):
                            ctg.append(self.tr(("pb", 1) if hh == 0 else None,
                                               self.pb[1][hh * 64: hh * 64 + 64, 0:128],
                                               Xsb[:, hh * 128: hh * 128 + 64],
                                               sb["identb"][:]))
                        cgf = self.cp("vector", Gfm[:], self.pb[1][:, 0:128],
                                      wait=[ctg[-1]], relkey=("pb", 1))
                        copen = self.mm(("pf", 2), self.pf[2][:, 0:128],
                                        sb["zerosq"][:], WWr[:, 0:128], True, False)
                        for cch in range(4):
                            crs = slice(cch * 32, (cch + 1) * 32)
                            zw0 = [("G", czs[1])] if not self._st_init else []
                            self._st_init = True
                            zm = []
                            for hh, po in ((0, 0), (1, 64)):
                                zm.append(self.mm(("pf", 3) if hh == 0 else None,
                                                  self.pf[3][cch * 32:(cch + 1) * 32,
                                                             hh * 64: hh * 64 + 64],
                                                  Gfm[po:po + 64, cch * 32:(cch + 1) * 32],
                                                  stS[po:po + 64, ct * 64:(ct + 1) * 64],
                                                  True, True, tp=(po, cch * 32),
                                                  wait=([cgf] if (cch == 0 and hh == 0) else []) + zw0))
                            for hh in range(2):
                                self.tt("vector", Ztok[crs, hh * 64: hh * 64 + 64],
                                        self.pf[3][crs, hh * 64: hh * 64 + 64],
                                        Xsb[crs, 64 + hh * 128: 128 + hh * 128],
                                        ALU.add, wait=[zm[-1]])
                            czt = ("V", p.n["V"])
                            self.setg(("pf", 3), czt)
                            for hh, po in ((0, 0), (1, 64)):
                                self.mm(None, self.pf[2][po:po + 64, crs],
                                        stS[po:po + 64, ct * 64:(ct + 1) * 64],
                                        oR[po:po + 64, stl * 128 + cch * 32:
                                           stl * 128 + (cch + 1) * 32],
                                        False, False, tp=(po, po))
                            su1 = self.mm(("pf", 4), self.pf[4][:, 0:128],
                                          BKt[crs, 0:128], Ztok[crs, 0:128],
                                          True, False, tp=(cch * 32, 0),
                                          wait=[czt, cbk])
                            su2 = self.mm(None, self.pf[4][:, 0:128],
                                          BKt[crs, 128:256], Vtok[crs, 0:128],
                                          False, True, tp=(cch * 32, 0))
                            for hh, po in ((0, 0), (1, 64)):
                                self.stt("vector", stS[po:po + 64, ct * 64:(ct + 1) * 64],
                                         stS[po:po + 64, ct * 64:(ct + 1) * 64],
                                         PLs[po:po + 64, ct * NCH + stl * 4 + cch:
                                             ct * NCH + stl * 4 + cch + 1],
                                         self.pf[4][po:po + 64, po:po + 64],
                                         ALU.mult, ALU.add, wait=[su2])
                            self.setg(("pf", 4), ("V", p.n["V"]))
                        self.mm(None, self.pf[2][0:64, 0:128], Ztok[:, 0:64],
                                WWr[:, 128:256], False, False, tp=(0, 0))
                        self.mm(None, self.pf[2][0:64, 0:128], Vtok[:, 0:64],
                                UUr[:, 128:256], False, False, tp=(0, 0))
                        self.mm(None, self.pf[2][64:128, 0:128], Ztok[:, 64:128],
                                WWr[:, 384:512], False, False, tp=(0, 64))
                        self.mm(None, self.pf[2][64:128, 0:128], Vtok[:, 64:128],
                                UUr[:, 384:512], False, False, tp=(0, 64))
                        cyc = self.mm(None, self.pf[2][:, 0:128], sb["zerosq"][:],
                                      WWr[:, 0:128], False, True)
                        self.cp("vector", yS[:, ct * SEGA + stl * 128:
                                             ct * SEGA + (stl + 1) * 128],
                                self.pf[2][:, 0:128], wait=[cyc], relkey=("pf", 2))
                c_wkv = ("V", p.n["V"])
                if seg == 0:
                    self.tap("a_yS", yS[:], [128, SW], BF, [c_wkv])

                cs1 = cs2 = None
                for ct in range(NCT):
                    cs1 = self.mm(("pf", 2) if ct == 0 else None, self.pf[2][0:16, 0:SEGA],
                                  sb["gnpat"][:, ct * 16:(ct + 1) * 16],
                                  yS[:, segsl(ct)], ct == 0, ct == 7, wait=[c_wkv])
                for ct in range(NCT):
                    q0 = ct % 2
                    dst = t1g[:] if q0 == 0 else miu[:]
                    gq = self.guard(("gsq", q0))
                    self.act(dst, yS[:, segsl(ct)], AF.Square,
                             wait=([c_wkv] if ct == 0 else []) + ([gq] if gq else []))
                    cs2 = self.mm(("pf", 3) if ct == 0 else None, self.pf[3][0:16, 0:SEGA],
                                  sb["gnpat"][:, ct * 16:(ct + 1) * 16], dst,
                                  ct == 0, ct == 7, wait=[("S", p.n["S"])])
                    self.setg(("gsq", q0), cs2)
                gA = gnr[0:16, 0:SEGA]
                gB = gnr[0:16, SEGA:2 * SEGA]
                gC = gnr[0:16, 2 * SEGA:3 * SEGA]
                self.ts("vector", gA, self.pf[2][0:16, 0:SEGA], 1.0 / 64,
                        0.0, ALU.mult, ALU.add, wait=[cs1], relkey=("pf", 2))
                self.ts("vector", gB, self.pf[3][0:16, 0:SEGA], 1.0 / 64,
                        0.0, ALU.mult, ALU.add, wait=[cs2], relkey=("pf", 3))
                self.tt("vector", gC, gA, gA, ALU.mult)
                cv4 = self.tt("vector", gB, gB, gC, ALU.subtract)
                self.act(gB, gB, AF.Ln, bias=self.col("c_epsgn")[0:16, :], wait=[cv4])
                cr5 = self.act(gB, gB, AF.Exp, scale=-0.5)
                cm5 = self.tt("vector", gC, gA, gB, ALU.mult, wait=[cr5])
                for ct in range(NCT):
                    cb1 = self.mm(("pf", 3), self.pf[3][:, 0:SEGA],
                                  sb["gnpatTf"][0:16, ct * 128:(ct + 1) * 128],
                                  gB, True, True,
                                  wait=[cm5, ("dlr2", 16)] if ct == 0 else [cm5])
                    cb2 = self.mm(("pf", 4), self.pf[4][:, 0:SEGA],
                                  sb["gnpatTf"][0:16, ct * 128:(ct + 1) * 128],
                                  gC, True, True)
                    cb3 = self.mm(("pf", 5), self.pf[5][:, 0:SEGA],
                                  sb["gnpatTf"][0:16, ct * 128:(ct + 1) * 128],
                                  bsum[0:16, :], True, True)
                    self.tt("vector", sb1f[:], yS[:, segsl(ct)], self.pf[3][:, 0:SEGA],
                            ALU.mult, wait=[cb2], relkey=("pf", 3))
                    self.tt("vector", sb1f[:], sb1f[:], self.pf[4][:, 0:SEGA],
                            ALU.subtract, relkey=("pf", 4))
                    self.ts("vector", sb1f[:], sb1f[:], self.col("gn_gamma", ct),
                            0.0, ALU.mult, ALU.add)
                    self.ts("vector", sb1f[:], sb1f[:], self.col("gn_beta", ct),
                            0.0, ALU.add, ALU.add)
                    self.tt("vector", t1g[:], vS[:, segsl(ct)], self.pf[5][:, 0:SEGA],
                            ALU.mult, wait=[cb3], relkey=("pf", 5))
                    self.tt("vector", sb1f[:], sb1f[:], t1g[:], ALU.add)
                    self.tt("vector", xdP[:, segsl(ct)], sb1f[:], gS[:, segsl(ct)],
                            ALU.mult)
                c_zfin = ("V", p.n["V"])
                if seg == 0:
                    self.tap("a_zS", xdP[:], [128, SW], BF, [c_zfin])

                for rb in range(NCT):
                    slot = rb % 2
                    c_st = strip_dma("oWT", rb, slot)
                    bk = rb % 2
                    cm = None
                    for ctc in range(NCT):
                        cm = self.mm(("pf", bk) if ctc == 0 else None,
                                     self.pf[bk][:, 0:SEGA],
                                     wstrip[:, slot * 1024 + ctc * 128:
                                            slot * 1024 + (ctc + 1) * 128],
                                     xdP[:, segsl(ctc)], ctc == 0, ctc == NCT - 1,
                                     wait=[c_st, c_zfin] if ctc == 0 else [])
                    self.tt("vector", sb[f"x{rb}"][:, s0:s1g],
                            sb[f"x{rb}"][:, s0:s1g], self.pf[bk][:, 0:SEGA], ALU.add,
                            wait=[cm], relkey=("pf", bk))
                    self.setg(("astrip", slot), self.p.guards.get(("pf", bk)))
                stE.close()
                self.fence()
        self.fence()

    def finish_phase(self, out):
        p, sb = self.p, self.sb
        stg = self.stack.enter_context(self.nc.sbuf_tensor("ostage", [128, 2 * C], F32))
        bar = self.barrier()
        c_store = {}
        for tt in range(16):
            sl = stg[:, (tt % 2) * C:(tt % 2) * C + C]
            cvs = None
            for ct in range(NCT):
                bk = ("pf", 4 + ct % 2)
                cm = self.tr(bk, self.pf[4 + ct % 2][:, 0:128],
                             sb[f"x{ct}"][:, tt * 128:(tt + 1) * 128], sb["identf"][:],
                             wait=bar if (tt == 0 and ct < 2) else [])
                g = c_store.get(tt % 2)
                cvs = self.cp("vector", sl[:, ct * 128:(ct + 1) * 128],
                              self.pf[4 + ct % 2][:, 0:128],
                              wait=[cm] + ([g] if (g and ct == 0) else []),
                              relkey=bk)
            c_store[tt % 2] = self.dma("D1", out[tt * 128:(tt + 1) * 128, :], sl,
                                       wait=[cvs], sem=f"do{tt % 2}")
        fin = [("do0", p.n["do0"]), ("do1", p.n["do1"])]
        if "dt" in p.n:
            fin.append(("dt", p.n["dt"]))
        self.p.op("sync", lambda e: None, wait=fin)


def host_consts():
    c = {}
    c["identf"] = np.eye(128, dtype=np.float32)
    c["identb_src"] = np.eye(128, dtype=np.float32)
    c["onesrow_src"] = np.ones((1, 128), dtype=np.float32)
    c["onesrowf_src"] = np.ones((1, 128), dtype=np.float32)
    c["onescol_src"] = np.ones((128, 1), dtype=np.float32)
    c["zerosq_src"] = np.zeros((128, 128), dtype=np.float32)
    dp = np.zeros((128, 128), dtype=np.float32)
    for pp in range(128):
        dp[(pp // 32) * 32, pp] = 1.0
    c["dpat_src"] = dp
    m0 = (np.arange(SEG) % LCH == 0).astype(np.float32)
    c["cm_a_src"] = np.broadcast_to(1.0 - m0, (128, SEG)).copy()
    c["cm_b_src"] = np.broadcast_to(m0, (128, SEG)).copy()
    s_i = np.arange(128)[:, None]
    t_i = np.arange(128)[None, :]
    same = (s_i // LCH) == (t_i // LCH)
    e22 = np.float32(np.exp(np.float64(22.0)))
    c["mask_tu_src"] = np.concatenate([
        (same & (s_i < t_i)).astype(np.float32),
        (same & (s_i <= t_i)).astype(np.float32)], axis=1) * e22
    c["mask_sl_src"] = (same & (s_i > t_i)).astype(np.float32) * e22
    gp = np.zeros((128, 128), dtype=np.float32)
    gpt = np.zeros((16, 1024), dtype=np.float32)
    for ct in range(NCT):
        for r in range(128):
            h = 2 * ct + r // 64
            gp[r, ct * 16 + h] = 1.0
            gpt[h, ct * 128 + r] = 1.0
    c["gnpat_src"] = gp
    c["gnpatT_src"] = gpt
    qp = np.zeros((128, 64), dtype=np.float32)
    qpt = np.zeros((16, 512), dtype=np.float32)
    for qt in range(4):
        for r in range(128):
            h = 4 * qt + r // 32
            qp[r, qt * 16 + h] = 1.0
            qpt[h, qt * 128 + r] = 1.0
    c["qpat_src"] = qp
    c["qpatT_src"] = qpt
    es = np.zeros((8, 8 * 128), dtype=np.float32)
    for e in range(8):
        es[e, e * 128:(e + 1) * 128] = 1.0
    c["esel_src"] = es
    return c


def pack_cols(inputs):
    cols, idx = [], {}

    def add(name, vec):
        v = np.asarray(vec, np.float32).reshape(-1, 128)
        idx[name] = len(cols)
        for r in v:
            cols.append(r)

    add("gn_gamma", inputs["L0_attn_gn_gamma"])
    add("gn_beta", inputs["L0_attn_gn_beta"])
    for g in range(6):
        add(f"lam{g}", np.asarray(inputs["L0_attn_lambdas"])[g, 0, 0])
    add("x_miu", np.asarray(inputs["L0_attn_x_miu"])[0, 0])
    add("w_miu", np.asarray(inputs["L0_attn_w_miu"])[0, 0])
    add("a_time", np.asarray(inputs["L0_attn_a_time"])[0, 0])
    add("xtime", np.asarray(inputs["L0_attn_x_time"], np.float32).reshape(-1))
    add("o_b", inputs["L0_ffn_o_b"])
    for e in range(E_EXP):
        add(f"b1_{e}", inputs["L0_ffn_b1"][e])
        add(f"b2_{e}", inputs["L0_ffn_b2"][e])
    qb = np.asarray(inputs["L0_ffn_q_b"], np.float32).reshape(16, 64)[:, :32].reshape(-1)
    add("q_b", qb)
    add("k_b", inputs["L0_ffn_k_b"])
    add("c_eps1", np.full(128, 1e-5, np.float32))
    add("c_epsgn", np.full(128, 1.6e-4, np.float32))
    add("c_one", np.full(128, 1.0, np.float32))
    add("c_mhalf", np.full(128, -0.5, np.float32))
    add("c_mquart", np.full(128, -0.25, np.float32))
    add("c_m11", np.full(128, -11.0, np.float32))
    return np.stack(cols, axis=1).astype(np.float32), idx


def pack_rows(inputs):
    rows, idx = [], {}

    def add(name, vec):
        v = np.asarray(vec, np.float32).reshape(-1)
        idx[name] = (len(rows), v.size)
        rows.extend(v.tolist())

    add("v_b", inputs["L0_ffn_v_b"])
    add("router_b", inputs["L0_ffn_router_b"])
    return np.asarray(rows, np.float32).reshape(1, -1), idx


def make_base(inputs, colarr, rowarr, consts):
    qsel = np.asarray(inputs["L0_ffn_q_w"], np.float32).reshape(C, 16, 64)[:, :, :32]
    return dict(
        q_w_sel=np.ascontiguousarray(qsel.reshape(C, 512)),
        k_w=np.ascontiguousarray(inputs["L0_ffn_k_w"], np.float32),
        v_w=np.ascontiguousarray(inputs["L0_ffn_v_w"], np.float32),
        o_w=np.ascontiguousarray(np.asarray(inputs["L0_ffn_o_w"], np.float32)[:512]),
        router_w_t=np.ascontiguousarray(
            np.asarray(inputs["L0_ffn_router_w"], np.float32).reshape(8, 128, 8)
            .transpose(1, 0, 2).reshape(128, 64)),
        w1=np.ascontiguousarray(inputs["L0_ffn_w1"], np.float32),
        w2=np.ascontiguousarray(inputs["L0_ffn_w2"], np.float32),
        rW=np.ascontiguousarray(inputs["L0_attn_rW"], np.float32),
        kW=np.ascontiguousarray(inputs["L0_attn_kW"], np.float32),
        vW=np.ascontiguousarray(inputs["L0_attn_vW"], np.float32),
        oW=np.ascontiguousarray(inputs["L0_attn_oW"], np.float32),
        g_A=np.ascontiguousarray(inputs["L0_attn_g_A"], np.float32),
        g_B=np.ascontiguousarray(inputs["L0_attn_g_B"], np.float32),
        k_A=np.ascontiguousarray(inputs["L0_attn_k_A"], np.float32),
        k_B=np.ascontiguousarray(inputs["L0_attn_k_B"], np.float32),
        w_A=np.ascontiguousarray(inputs["L0_attn_w_A"], np.float32),
        w_B=np.ascontiguousarray(inputs["L0_attn_w_B"], np.float32),
        a_A=np.ascontiguousarray(inputs["L0_attn_a_A"], np.float32),
        a_B=np.ascontiguousarray(inputs["L0_attn_a_B"], np.float32),
        x_A=np.ascontiguousarray(inputs["L0_attn_x_A"], np.float32),
        x_Bp=np.ascontiguousarray(
            np.asarray(inputs["L0_attn_x_B"], np.float32).reshape(192, 1024)),
        cols=colarr, rows=rowarr, **consts)


def build_all(stages, taps, inputs, stop_after=None, racecheck=False):
    colarr, colidx = pack_cols(inputs)
    rowarr, rowidx = pack_rows(inputs)
    b = Builder(stages=stages, taps=taps, stop_after=stop_after, racecheck=racecheck)
    b.ncols = colarr.shape[1]
    b.nrows = rowarr.shape[1]
    b.colidx = colidx
    b.rowidx = rowidx
    nc = b.build()
    base = make_base(inputs, colarr, rowarr, host_consts())
    return nc, base


def kernel(**inputs):
    nc, base = build_all(("ffn1", "attn", "ffn2"), (), inputs)
    xs = np.asarray(inputs["L0_attn_x"], np.float32)
    in_maps = [dict(base, x=np.ascontiguousarray(xs[i])) for i in range(8)]
    res = run_bass_kernel_spmd(nc, in_maps, core_ids=list(range(8)))
    return np.stack([r["out"] for r in res.results]).astype(np.float32)
